# version 27
# baseline (speedup 1.0000x reference)
import numpy as np
import ml_dtypes
import concourse.bass as bass
import concourse.mybir as mybir
from concourse.bass_core_rust import IndirectOffsetOnAxis
from concourse.bass_utils import run_bass_kernel_spmd

F32 = mybir.dt.float32
BF16 = mybir.dt.bfloat16
U32 = mybir.dt.uint32
AF = mybir.ActivationFunctionType
ALU = mybir.AluOpType
AX = mybir.AxisListType

B, T, D = 8, 2048, 1024
H = 4
E, DFF, CAP = 8, 4096, 2560
SC, NSC = 512, 4
NCHS = 2
NT = 16
EPS = 1e-5
SC16 = 1.0 / 16.0
BIG = float(2 ** 24)

ENGINES = ("sync", "gpsimd", "tensor", "vector", "scalar")


class Sem:
    def __init__(self, nc, name):
        self.h = nc.alloc_semaphore(name)
        self.n = 0

    def bump(self, k):
        self.n += k
        return self.n


class Ctx:
    def __init__(self, nc):
        self.nc = nc
        self.streams = {e: [] for e in ENGINES}

    def emit(self, engine, fn):
        self.streams[engine].append(fn)

    def wait(self, engine, sem, val):
        if val <= 0:
            return
        self.emit(engine, lambda eng, s=sem.h, v=val: eng.wait_ge(s, v))

    def op(self, engine, build, sem=None, k=1):
        if sem is None:
            self.emit(engine, build)
            return None
        v = sem.bump(k)
        self.emit(engine, lambda eng, b=build, s=sem.h, kk=k: b(eng).then_inc(s, kk))
        return v


class Arena:

    def __init__(self, nc, dead_handles):
        self.nc = nc
        dsz = {F32: 4, BF16: 2, U32: 4}
        regs = []
        for h in dead_handles:
            m = nc.lookup_mloc(h)
            nb = 1
            for d in list(h.shape)[1:]:
                nb *= d
            nb *= dsz[h.dtype]
            regs.append([m.addr, m.addr + nb])
        regs.sort()
        merged = []
        for a, b in regs:
            if merged and a <= merged[-1][1]:
                merged[-1][1] = max(merged[-1][1], b)
            else:
                merged.append([a, b])
        self.free = merged

    def at(self, name, shape, dtype):
        import math
        nb = 1
        for d in shape[1:]:
            nb *= d
        nb *= {F32: 4, BF16: 2, U32: 4}[dtype]
        for r in self.free:
            a = (r[0] + 31) & ~31
            if a + nb <= r[1]:
                t = self.nc.alloc_sbuf_tensor_at(name, shape, dtype, offset=a)
                r[0] = a + nb
                return t
        raise RuntimeError(f"arena: no room for {name} ({nb}B); free={self.free}")


AG2_EARLY = False


def build_nc(debug=(), stage=4):
    nc = bass.Bass("TRN2", target_bir_lowering=False, debug=False)
    ctx = Ctx(nc)

    di = lambda n, s, d: nc.dram_tensor(n, s, d, kind="ExternalInput").ap()
    x_fm = di("x_fm", [D, T], F32)
    x_tm = di("x_tm", [T, D], F32)
    wqT = di("wqT", [D, D], F32)
    wkT = di("wkT", [D, D], F32)
    wvT = di("wvT", [D, D], F32)
    wfT = di("wfT", [D, D], F32)
    woT = di("woT", [D, D], F32)
    nw_c = di("nw_c", [128, 8], F32)
    gn_c = di("gn_c", [128, 8], F32)
    wr_in = di("wr", [D, E], F32)
    w1e_in = di("w1e", [D, DFF], F32)
    w2e_in = di("w2e", [DFF, D], F32)
    ident_bf_in = di("ident_bf", [128, 128], BF16)
    ident_f32_in = di("ident_f32", [128, 128], F32)
    utri_bf_in = di("utri_bf", [128, 128], BF16)
    utri_f_in = di("utri_f", [128, 128], F32)
    ustrict_f32_in = di("ustrict_f32", [128, 128], F32)
    iota8_in = di("iota8", [128, 8], F32)
    s160_in = di("s160", [128, 160], F32)
    str8_in = di("str8", [8, 8], F32)
    bmask_in = di("bmask", [8, 1], F32)
    ecol_in = di("ecol", [8, 1], F32)
    rowbase_in = di("rowbase", [1, 8], F32)
    ones1f_in = di("ones1f", [1, 128], F32)
    onescol_bf_in = di("onescol_bf", [128, 1], BF16)
    onescol_f_in = di("onescol_f", [128, 1], F32)
    invd_bf_in = di("invd_bf", [128, 1], BF16)
    epscol_in = di("epscol", [128, 1], F32)

    y2_out = nc.dram_tensor("y2", [T, D], F32, kind="ExternalOutput").ap()

    y1_dram = nc.dram_tensor("y1_dram", [T, D], F32).ap()
    ysort_dram = nc.dram_tensor("ysort_dram", [T, D], BF16).ap()
    cnt_in_d = nc.dram_tensor("cnt_in_d", [1, 8], F32).ap()
    cnt_ag_d = nc.dram_tensor("cnt_ag_d", [8, 8], F32, addr_space="Shared").ap()
    ag1_d = nc.dram_tensor("ag1_d", [B * T, D], BF16, addr_space="Shared").ap()
    eout_d = nc.dram_tensor("eout_d", [CAP, D], BF16).ap()
    ag2a_d = nc.dram_tensor("ag2a_d", [8 * 2048, D], BF16,
                            addr_space="Shared").ap()
    ag2b_d = nc.dram_tensor("ag2b_d", [8 * 512, D], BF16,
                            addr_space="Shared").ap()

    dbg = {}

    def dbg_out(name, shape, dt=F32):
        if name in debug:
            dbg[name] = nc.dram_tensor("dbg_" + name, shape, dt,
                                       kind="ExternalOutput").ap()
            return dbg[name]
        return None

    sb = lambda n, s, d: nc.alloc_sbuf_tensor(n, s, d)
    wr_sb = sb("wr_sb", [128, 8, E], F32)
    nw_sb = sb("nw_sb", [128, 8], F32)
    gn_sb = sb("gn_sb", [128, 8], F32)

    id_bf = sb("id_bf", [128, 128], BF16)
    id_f = sb("id_f", [128, 128], F32)
    utri = sb("utri", [128, 128], BF16)
    utri_f = sb("utri_f_sb", [128, 128], F32)
    ustr = sb("ustr", [128, 128], F32)
    iota8 = sb("iota8_sb", [128, 8], F32)
    s160 = sb("s160_sb", [128, 160], F32)
    str8 = sb("str8_sb", [8, 8], F32)
    bmask = sb("bmask_sb", [8, 1], F32)
    ecol = sb("ecol_sb", [8, 1], F32)
    rowbase = sb("rowbase_sb", [1, 8], F32)
    ones1f = sb("ones1f_sb", [1, 128], F32)
    onescol_bf = sb("onescol_bf_sb", [128, 1], BF16)
    onescol_f = sb("onescol_f_sb", [128, 1], F32)
    invd_bf = sb("invd_bf_sb", [128, 1], BF16)
    epscol = sb("epscol_sb", [128, 1], F32)
    nc.const_aps.register(F32, EPS, epscol[:, 0:1])

    x_bf = sb("x_bf", [128, 8, SC], BF16)
    xsq2 = sb("xsq2", [128, 2, SC], BF16)
    h_sb = sb("h_sb", [128, 8, SC], BF16)
    q_sb = sb("q_sb", [128, 8, SC], BF16)
    k_sb = sb("k_sb", [128, 8, SC], BF16)
    v_sb = sb("v_sb", [128, 4, D], BF16)
    g_sb = sb("g_sb", [128, 4, D], F32)
    sA = sb("sA", [128, D], F32)
    rr1 = sb("rr1", [1, SC], F32)
    rr2 = sb("rr2", [1, SC], F32)

    ghi = sb("ghi", [128, 2, D], BF16)
    glo = sb("glo", [128, 2, D], BF16)
    gc_sb = sb("gc_sb", [128, 2, D], F32)
    gd_sb = sb("gd_sb", [128, 2, D], F32)
    gtotr = sb("gtotr", [1, D], F32)
    g0r = sb("g0r", [1, D], F32)
    egtot = sb("egtot", [1, D], F32)
    _xaddr = nc.lookup_mloc(x_bf).addr
    e1t = nc.alloc_sbuf_tensor_at("e1t", [128, D], F32, offset=_xaddr)
    ept = nc.alloc_sbuf_tensor_at("ept", [128, D], F32, offset=_xaddr + 4096)
    st_sb = sb("st_sb", [128, 2, D], BF16)
    z_sb = sb("z_sb", [128, D], F32)
    p_sb = sb("p_sb", [128, D], F32)
    mx4 = sb("mx4", [128, 4], F32)
    nmx4 = sb("nmx4", [128, 4], F32)
    sum4 = sb("sum4", [128, 4], F32)
    rden = sb("rden", [128, 4], F32)
    p1t = sb("p1t", [128, D], BF16)
    p2t = sb("p2t", [128, D], BF16)
    stf = sb("stf", [128, 16, 128], BF16)
    p1f = sb("p1f", [128, 8, 128], BF16)
    p2f = sb("p2f", [128, 8, 128], BF16)
    ktm = sb("ktm", [128, 16, 128], BF16)
    am_sb = sb("am_sb", [128, 8, 128], BF16)
    bt_sb = sb("bt_sb", [128, 8, 128], BF16)
    hk_sb = sb("hk_sb", [128, 8, 256], F32)
    hv_sb = sb("hv_sb", [128, 8, 256], F32)
    hkb = sb("hkb", [128, 8, 256], BF16)
    hvb = sb("hvb", [128, 8, 256], BF16)
    lamc = sb("lamc", [128, 8], F32)
    o_sb = sb("o_sb", [128, 8, SC], BF16)

    xt_sb = nc.alloc_sbuf_tensor_at("xt_sb", [128, D], F32, offset=_xaddr)
    pre_sb = nc.alloc_sbuf_tensor_at("pre_sb", [128, D], F32, offset=_xaddr + 4096)
    mean1 = sb("mean1", [128, 1], F32)
    var1 = sb("var1", [128, 1], F32)
    mraw = sb("mraw", [128, 1], F32)
    vraw = sb("vraw", [128, 1], F32)
    mraw2 = sb("mraw2", [128, 1], F32)
    vraw2 = sb("vraw2", [128, 1], F32)
    mean1b = sb("mean1b", [128, 1], F32)
    var1b = sb("var1b", [128, 1], F32)
    rstd1b = sb("rstd1b", [128, 1], F32)
    tmp1cb = sb("tmp1cb", [128, 1], F32)
    rstd1 = sb("rstd1", [128, 1], F32)
    y1t_sb = nc.alloc_sbuf_tensor_at("y1t_sb", [128, D], F32,
                                     offset=nc.lookup_mloc(st_sb).addr)
    y1tf_sb = nc.alloc_sbuf_tensor_at("y1tf_sb", [128, 8, 128], F32,
                                      offset=nc.lookup_mloc(stf).addr)
    y1_sb = nc.alloc_sbuf_tensor_at("y1_sb", [128, D], F32,
                                    offset=nc.lookup_mloc(ghi).addr)
    y1f_sb = nc.alloc_sbuf_tensor_at("y1f_sb", [128, 8, 128], F32,
                                     offset=nc.lookup_mloc(glo).addr)
    lg_sb = sb("lg_sb", [128, 8], F32)
    mx8 = sb("mx8", [128, 8], F32)
    idxu = sb("idxu", [128, 8], U32)
    nmx = sb("nmx", [128, 1], F32)
    pex = sb("pex", [128, 8], F32)
    den8 = sb("den8", [128, 1], F32)
    oh_sb = sb("oh_sb", [128, 8], F32)
    posx = sb("posx", [128, 8], F32)
    carry = sb("carry", [1, 8], F32)
    idx_all = sb("idx_all", [128, NT], F32)
    lp_all = sb("lp_all", [128, NT], F32)
    pmax_all = sb("pmax_all", [128, NT], F32)
    tmp8 = sb("tmp8", [128, 8], F32)
    tmp1c = sb("tmp1c", [128, 1], F32)

    PA = nc.alloc_psum_tensor("PA", [128, 512], F32)
    PB = nc.alloc_psum_tensor("PB", [128, 512], F32)
    PC = nc.alloc_psum_tensor("PC", [128, 512], F32)
    PD = nc.alloc_psum_tensor("PD", [128, 512], F32)
    PE_ = nc.alloc_psum_tensor("PE_", [128, 512], F32)
    PF = nc.alloc_psum_tensor("PF", [128, 512], F32)
    PT = nc.alloc_psum_tensor("PT", [128, 1024], BF16)
    PU = nc.alloc_psum_tensor("PU", [128, 512], F32)

    L = Sem(nc, "L")
    G = Sem(nc, "G")
    SQ = Sem(nc, "SQ")
    P = Sem(nc, "P")
    V = Sem(nc, "V")
    S = Sem(nc, "S")
    CC = Sem(nc, "CC")

    def sdma(src_ap, dst_ap):
        return ctx.op("sync", lambda e, o=dst_ap, i=src_ap: e.dma_start(out=o, in_=i),
                      L, 16)

    def gdma(src_ap, dst_ap):
        return ctx.op("gpsimd", lambda e, o=dst_ap, i=src_ap: e.dma_start(out=o, in_=i),
                      G, 16)

    def qdma(src_ap, dst_ap):
        return ctx.op("scalar", lambda e, o=dst_ap, i=src_ap: e.dma_start(out=o, in_=i),
                      SQ, 16)

    def mm(out_ap, lhsT_ap, rhs_ap, start, stop, sem=None):
        return ctx.op("tensor",
                      lambda e, o=out_ap, l=lhsT_ap, r=rhs_ap, st=start, sp=stop:
                      e.matmul(o, l, r, start=st, stop=sp), sem)

    def tr(out_ap, in_ap, ident_ap, sem=None):
        return ctx.op("tensor",
                      lambda e, o=out_ap, i=in_ap, d=ident_ap: e.transpose(o, i, d),
                      sem)

    def ve(build, sem=None):
        return ctx.op("vector", build, sem)

    def vdrain():
        ctx.emit("vector", lambda e: e.drain())

    def sdrain():
        ctx.emit("scalar", lambda e: e.drain())

    def se(build, sem=None):
        return ctx.op("scalar", build, sem)

    def wt(engine, sem, val):
        ctx.wait(engine, sem, val)

    wstate = {"par": 0, "last_use": [0, 0]}

    def wload(src):
        i = wstate["par"]
        wstate["par"] ^= 1
        wt("gpsimd", P, wstate["last_use"][i])
        gv = gdma(src.rearrange("(c p) o -> p c o", p=128), wbufs[i][:, :, :])
        return i, gv

    def wdone(i):
        wstate["last_use"][i] = P.n

    for src, dst in ((ident_bf_in, id_bf), (ident_f32_in, id_f), (utri_bf_in, utri),
                     (utri_f_in, utri_f), (ustrict_f32_in, ustr), (iota8_in, iota8),
                     (s160_in, s160), (str8_in, str8), (bmask_in, bmask),
                     (ecol_in, ecol), (rowbase_in, rowbase), (ones1f_in, ones1f),
                     (onescol_bf_in, onescol_bf), (onescol_f_in, onescol_f),
                     (invd_bf_in, invd_bf), (epscol_in, epscol),
                     (nw_c, nw_sb), (gn_c, gn_sb)):
        sdma(src[:, :], dst[:, :])
    sdma(wr_in.rearrange("(c p) e -> p c e", p=128), wr_sb[:, :, :])
    consts_done = L.n

    warm_d = nc.dram_tensor("warm_d", [1, 8], F32).ap()
    warm_o = nc.dram_tensor("warm_o", [8, 8], F32, addr_space="Shared").ap()
    bcregs = {}

    def _gp_init(e):
        for nm, val in (("t", T - 1), ("bt", B * T - 1), ("ec", E * CAP - 1),
                        ("eca", 8 * 2048 - 1), ("ecb", 8 * 512 - 1)):
            r = e.alloc_register("bc_" + nm)
            e.reg_mov(r, val)
            bcregs[nm] = r
    ctx.emit("gpsimd", _gp_init)
    wt("gpsimd", L, consts_done)
    wrm = gdma(ones1f[0:1, 0:8], warm_d[:, :])
    wt("gpsimd", G, wrm)
    ctx.op("gpsimd", lambda e: e.collective_compute(
        "AllGather", ALU.bypass, replica_groups=[list(range(8))],
        ins=[warm_d[:, :].opt()], outs=[warm_o[:, :].opt()]), CC)

    ve(lambda e: e.memset(hk_sb[:, :, :], 0.0))
    ve(lambda e: e.memset(hv_sb[:, :, :], 0.0))
    ve(lambda e: e.memset(hkb[:, :, :], 0.0))
    ve(lambda e: e.memset(hvb[:, :, :], 0.0))
    init_v = ve(lambda e: e.memset(carry[:, :], 0.0), V)

    h_done_v = 0
    prev_sc_pe = 0
    prev_sc_v = 0

    whalf = [sb("wh" + str(i), [128, 8, 512], BF16) for i in range(4)]
    wst = {"last": [0, 0, 0, 0]}

    def wload_h(src, half, sl):
        wt("gpsimd", P, wst["last"][sl])
        gv = gdma(src[:, half * 512:(half + 1) * 512]
                  .rearrange("(c p) o -> p c o", p=128), whalf[sl][:, :, :])
        return sl, gv

    def wdone_h(sl):
        wst["last"][sl] = P.n

    st_front = {}
    st_scan = {}
    scan_state = {"hv_add_v": init_v, "gl_p": 0, "y1q": 0, "split": {}}

    def emit_front(s):
        t0 = s * SC
        iq = [wload_h(wqT, h2, 2 + h2) for h2 in range(2)]
        prev = st_scan.get(s - 1, {})
        wt("gpsimd", V, prev.get("scan_done_v", 0))
        xld = gdma(x_fm[:, t0:t0 + SC].rearrange("(c p) t -> p c t", p=128),
                   x_bf[:, :, :])
        wt("vector", G, xld)
        wt("vector", L, consts_done)
        sq_mm_p = [0] * 8
        for c in range(8):
            if c >= 2:
                wt("vector", P, sq_mm_p[c - 2])
            vsq = ve(lambda e, c=c: e.tensor_mul(xsq2[:, c % 2, :], x_bf[:, c, :],
                                                 x_bf[:, c, :]), V)
            wt("tensor", V, vsq)
            sq_mm_p[c] = mm(PF[0:1, 0:512], invd_bf[:, :], xsq2[:, c % 2, :],
                            c == 0, c == 7, P)
        wt("scalar", P, P.n)
        se(lambda e: e.activation(out=rr1[:, :], in_=PF[0:1, 0:512], func=AF.Ln,
                                  bias=EPS))
        sdrain()
        sr = se(lambda e: e.activation(out=rr2[:, :], in_=rr1[:, :], func=AF.Exp,
                                       scale=-0.5), S)
        wt("tensor", S, sr)
        bc = mm(PE_[:, :], ones1f[:, :], rr2[:, :], True, True, P)
        wt("vector", P, bc)
        wt("vector", V, init_v)
        hv_ = 0
        for c in range(8):
            hv_ = ve(lambda e, c=c: e.scalar_tensor_tensor(
                h_sb[:, c, :], x_bf[:, c, :], nw_sb[:, c:c + 1], PE_[:, :],
                ALU.mult, ALU.mult), V)
        h_done = hv_

        wt("tensor", V, h_done)
        evac_v = []

        def fm_proj_h(slots, dst):
            for h2 in range(2):
                sl, gw = slots[h2]
                wt("tensor", G, gw)
                for dc in range(4):
                    pb = (PC, PD, PE_)[len(evac_v) % 3]
                    if len(evac_v) >= 3:
                        wt("tensor", V, evac_v[-3])
                    for kc in range(8):
                        mm(pb[:, :], whalf[sl][:, kc, dc * 128:(dc + 1) * 128],
                           h_sb[:, kc, :], kc == 0, kc == 7,
                           P if kc == 7 else None)
                    wt("vector", P, P.n)
                    evac_v.append(ve(lambda e, d=dst, dc=dc, h2=h2, pb=pb:
                                     e.tensor_copy(d[:, h2 * 4 + dc, :], pb[:, :]),
                                     V))
                wdone_h(sl)

        def tm_proj_h(slots, is_g):
            for h2 in range(2):
                sl, gw = slots[h2]
                wt("tensor", G, gw)
                for ts in range(4):
                    pb = (PC, PD, PE_)[len(evac_v) % 3]
                    if len(evac_v) >= 3:
                        wt("tensor", V, evac_v[-3])
                    for kc in range(8):
                        mm(pb[:, :], h_sb[:, kc, ts * 128:(ts + 1) * 128],
                           whalf[sl][:, kc, :], kc == 0, kc == 7,
                           P if kc == 7 else None)
                    wt("vector", P, P.n)
                    if is_g:
                        evac_v.append(ve(lambda e, ts=ts, h2=h2, pb=pb:
                                         e.tensor_copy(
                                             g_sb[:, ts, h2 * 512:(h2 + 1) * 512],
                                             pb[:, :]), V))
                    else:
                        evac_v.append(ve(lambda e, ts=ts, h2=h2, pb=pb:
                                         e.tensor_copy(
                                             v_sb[:, ts, h2 * 512:(h2 + 1) * 512],
                                             pb[:, :]), V))
                wdone_h(sl)

        fm_proj_h(iq, q_sb)
        q_done = evac_v[-1]
        wt("scalar", V, q_done)
        se(lambda e: e.activation(out=q_sb[:, :, :], in_=q_sb[:, :, :],
                                  func=AF.Silu))
        ik = [wload_h(wkT, h2, 2 + h2) for h2 in range(2)]
        fm_proj_h(ik, k_sb)
        k_done = evac_v[-1]
        wt("scalar", V, k_done)
        silu_qk = se(lambda e: e.activation(out=k_sb[:, :, :], in_=k_sb[:, :, :],
                                            func=AF.Silu), S)
        if_ = [wload_h(wfT, h2, 2 + h2) for h2 in range(2)]
        tm_proj_h(if_, True)
        g_done = evac_v[-1]
        iv = [wload_h(wvT, h2, 2 + h2) for h2 in range(2)]
        scr = ((sA, z_sb), (p_sb, gd_sb[:, 0, :]))
        lsig_v = [0, 0, 0, 0]
        for ts in range(4):
            a_s, b_s = scr[ts % 2]
            wt("scalar", V, g_done if ts < 2 else lsig_v[ts - 2])
            se(lambda e, ts=ts, a_s=a_s: e.activation(
                out=a_s[:, :], in_=g_sb[:, ts, :], func=AF.Abs))
            sdrain()
            se(lambda e, a_s=a_s: e.activation(out=a_s[:, :], in_=a_s[:, :],
                                               func=AF.Exp, scale=-1.0))
            sdrain()
            v3 = se(lambda e, a_s=a_s: e.activation(out=a_s[:, :], in_=a_s[:, :],
                                                    func=AF.Ln, bias=1.0), S)
            wt("vector", S, v3)
            ve(lambda e, ts=ts, b_s=b_s: e.tensor_scalar_min(
                b_s[:, :], g_sb[:, ts, :], 0.0))
            lsig_v[ts] = ve(lambda e, ts=ts, a_s=a_s, b_s=b_s: e.tensor_sub(
                g_sb[:, ts, :], b_s[:, :], a_s[:, :]), V)
        gates_done = lsig_v[3]
        tm_proj_h(iv, False)
        projs_done = evac_v[-1]
        wt("scalar", V, projs_done)
        silu_done = se(lambda e: e.activation(out=v_sb[:, :, :], in_=v_sb[:, :, :],
                                              func=AF.Silu), S)
        st_front[s] = dict(silu_done=silu_done, silu_qk=silu_qk,
                           gates_done=gates_done, sr=sr, h_done=h_done)

    def emit_split(s, ci):
        gates_done = st_front[s]["gates_done"]
        ts0, ts1 = 2 * ci, 2 * ci + 1
        wt("scalar", V, gates_done)
        wt("scalar", P, scan_state["gl_p"])
        se(lambda e, ts=ts0: e.copy(out=ghi[:, 0, :], in_=g_sb[:, ts, :]))
        ch_s = se(lambda e, ts=ts1: e.copy(out=ghi[:, 1, :],
                                           in_=g_sb[:, ts, :]), S)
        wt("vector", S, ch_s)
        ve(lambda e, ts=ts0: e.tensor_sub(p_sb[:, :], g_sb[:, ts, :],
                                          ghi[:, 0, :]))
        sub_v = ve(lambda e, ts=ts1: e.tensor_sub(z_sb[:, :], g_sb[:, ts, :],
                                                  ghi[:, 1, :]), V)
        wt("scalar", V, sub_v)
        se(lambda e: e.copy(out=glo[:, 0, :], in_=p_sb[:, :]))
        cl_s = se(lambda e: e.copy(out=glo[:, 1, :], in_=z_sb[:, :]), S)
        scan_state["split"][(s, ci)] = (sub_v, cl_s)

    def emit_scan(s):
        fr = st_front[s]
        silu_done, gates_done = fr["silu_done"], fr["gates_done"]
        silu_qk = fr["silu_qk"]
        prev_chunk_pe = 0
        scan_done_v = gates_done
        for ci in range(NCHS):
            ts0, ts1 = 2 * ci, 2 * ci + 1
            wt("vector", P, prev_chunk_pe)
            wt("vector", V, gates_done)
            if (s, ci) not in scan_state["split"]:
                emit_split(s, ci)
            sub_v, cl_s = scan_state["split"][(s, ci)]
            wt("tensor", V, sub_v)
            wt("tensor", S, cl_s)
            for n in range(2):
                pb = (PA, PB)[n]
                mm(pb[:, :], utri[:, :], ghi[:, 0, n * 512:(n + 1) * 512], True,
                   False)
                mm(pb[:, :], utri[:, :], glo[:, 0, n * 512:(n + 1) * 512], False,
                   True)
            for n in range(2):
                pb = (PF, PU)[n]
                mm(pb[0:1, 0:512], onescol_bf[:, :],
                   ghi[:, 0, n * 512:(n + 1) * 512], True, False)
                mm(pb[0:1, 0:512], onescol_bf[:, :],
                   glo[:, 0, n * 512:(n + 1) * 512], False, True,
                   P if n == 1 else None)
            g0_p = P.n
            wt("vector", P, g0_p)
            ve(lambda e: e.tensor_copy(gc_sb[:, 0, 0:512], PA[:, :]))
            ve(lambda e: e.tensor_copy(gc_sb[:, 0, 512:1024], PB[:, :]))
            ve(lambda e: e.tensor_copy(g0r[0:1, 0:512], PF[0:1, 0:512]))
            g0_v = ve(lambda e: e.tensor_copy(g0r[0:1, 512:1024], PU[0:1, 0:512]),
                      V)
            wt("tensor", V, g0_v)
            for n in range(2):
                pb = (PC, PD)[n]
                mm(pb[:, :], utri[:, :], ghi[:, 1, n * 512:(n + 1) * 512], True,
                   False)
                mm(pb[:, :], utri[:, :], glo[:, 1, n * 512:(n + 1) * 512], False,
                   False)
                mm(pb[:, :], ones1f[:, :], g0r[0:1, n * 512:(n + 1) * 512], False,
                   True)
            for n in range(2):
                pb = (PF, PU)[n]
                mm(pb[0:1, 0:512], onescol_bf[:, :],
                   ghi[:, 1, n * 512:(n + 1) * 512], False, False)
                mm(pb[0:1, 0:512], onescol_bf[:, :],
                   glo[:, 1, n * 512:(n + 1) * 512], False, True,
                   P if n == 1 else None)
            gt_p = P.n
            wt("vector", P, gt_p)
            ve(lambda e: e.tensor_copy(gc_sb[:, 1, 0:512], PC[:, :]))
            ve(lambda e: e.tensor_copy(gc_sb[:, 1, 512:1024], PD[:, :]))
            ve(lambda e: e.tensor_copy(gtotr[0:1, 0:512], PF[0:1, 0:512]))
            gt_v = ve(lambda e: e.tensor_copy(gtotr[0:1, 512:1024],
                                              PU[0:1, 0:512]), V)
            wt("scalar", V, gt_v)
            eg_s = se(lambda e: e.activation(out=egtot[:, :], in_=gtotr[:, :],
                                             func=AF.Exp, scale=SC16), S)
            wt("tensor", V, gt_v)
            for mc in range(8):
                mm(PU[:, 8 + mc:9 + mc], ghi[:, 0, mc * 128:(mc + 1) * 128],
                   onescol_bf[:, :], True, False)
                mm(PU[:, 8 + mc:9 + mc], glo[:, 0, mc * 128:(mc + 1) * 128],
                   onescol_bf[:, :], False, False)
                mm(PU[:, 8 + mc:9 + mc], ghi[:, 1, mc * 128:(mc + 1) * 128],
                   onescol_bf[:, :], False, False)
                mm(PU[:, 8 + mc:9 + mc], glo[:, 1, mc * 128:(mc + 1) * 128],
                   onescol_bf[:, :], False, True, P if mc == 7 else None)
            lamc_p = P.n
            scan_state["gl_p"] = lamc_p
            wt("scalar", P, lamc_p)
            lamc_s = se(lambda e: e.activation(out=lamc[:, :], in_=PU[:, 8:16],
                                               func=AF.Exp, scale=SC16), S)
            sdrain()
            mm(PE_[:, :], ones1f[:, :], gtotr[0:1, 0:512], True, True)
            mm(PF[:, :], ones1f[:, :], gtotr[0:1, 512:1024], True, True, P)
            bcg_p = P.n

            def tr_grp(grp):
                for sl in range(8):
                    gi = grp * 8 + sl
                    if gi < 16:
                        i, msl = gi // 8, gi % 8
                        tr(PT[:, sl * 128:(sl + 1) * 128],
                           st_sb[:, i, msl * 128:(msl + 1) * 128], id_bf[:, :],
                           P if sl == 7 else None)
                    else:
                        i, dsl = (gi - 16) // 8, (gi - 16) % 8
                        ts = 2 * ci + i
                        tr(PT[:, sl * 128:(sl + 1) * 128],
                           k_sb[:, dsl, ts * 128:ts * 128 + 128], id_bf[:, :],
                           P if sl == 7 else None)
                wt("vector", P, P.n)
                if grp < 2:
                    return ve(lambda e, grp=grp: e.tensor_copy(
                        stf[:, grp * 8:(grp + 1) * 8, :]
                        .rearrange("p a b -> p (a b)"), PT[:, :]), V)
                return ve(lambda e, grp=grp: e.tensor_copy(
                    ktm[:, (grp - 2) * 8:(grp - 1) * 8, :]
                    .rearrange("p a b -> p (a b)"), PT[:, :]), V)

            wt("tensor", S, silu_qk)
            ktv2 = tr_grp(2)
            wt("tensor", V, ktv2)
            ktv = tr_grp(3)
            wt("tensor", S, eg_s)
            mm(PC[:, :], ones1f[:, :], egtot[0:1, 0:512], True, True)
            mm(PD[:, :], ones1f[:, :], egtot[0:1, 512:1024], True, True, P)
            bce_p = P.n
            wt("vector", SQ, scan_state["y1q"])
            st_vs = [0, 0]
            for i, ts in enumerate((ts0, ts1)):
                wt("vector", P, bcg_p)
                ve(lambda e, i=i: e.tensor_tensor(gd_sb[:, i, 0:512],
                                                  gc_sb[:, i, 0:512],
                                                  PE_[:, :], ALU.subtract))
                gd_v = ve(lambda e, i=i: e.tensor_tensor(gd_sb[:, i, 512:1024],
                                                         gc_sb[:, i, 512:1024],
                                                         PF[:, :], ALU.subtract),
                          V)
                wt("scalar", V, gd_v)
                se(lambda e, ts=ts: e.activation(out=p_sb[:, :],
                                                 in_=g_sb[:, ts, :],
                                                 func=AF.Exp, scale=SC16))
                es_s = se(lambda e, i=i: e.activation(out=e1t[:, :],
                                                      in_=gd_sb[:, i, :],
                                                      func=AF.Exp, scale=-SC16),
                          S)
                wt("vector", S, es_s)
                ve(lambda e: e.tensor_scalar(z_sb[:, :], p_sb[:, :], -1.0, 1.0,
                                             ALU.mult, ALU.add))
                st_vs[i] = ve(lambda e, i=i: e.tensor_tensor(st_sb[:, i, :],
                                                             z_sb[:, :],
                                                             e1t[:, :],
                                                             ALU.mult), V)
            st_v = st_vs[1]
            se(lambda e: e.activation(out=gd_sb[:, 0, :], in_=gd_sb[:, 0, :],
                                      func=AF.Exp, scale=SC16))
            ept_s = se(lambda e: e.activation(out=gd_sb[:, 1, :],
                                              in_=gd_sb[:, 1, :],
                                              func=AF.Exp, scale=SC16), S)

            twins = (slice(ts0 * 128, ts0 * 128 + 128),
                     slice(ts1 * 128, ts1 * 128 + 128))
            epts = (gd_sb[:, 0, :], gd_sb[:, 1, :])
            a_slots = ((PU, PU), (PE_, PF))
            z_slots = ((PA, PB), (PC, PD))

            def a_mms(i):
                twin = twins[i]
                for jj in range(i + 1):
                    pslot = a_slots[i][jj]
                    for hh in range(H):
                        mm(pslot[:, hh * 128:hh * 128 + 128],
                           k_sb[:, 2 * hh, 2 * ci * 128 + jj * 128:
                                2 * ci * 128 + jj * 128 + 128],
                           q_sb[:, 2 * hh, twin], True, False)
                        mm(pslot[:, hh * 128:hh * 128 + 128],
                           k_sb[:, 2 * hh + 1, 2 * ci * 128 + jj * 128:
                                2 * ci * 128 + jj * 128 + 128],
                           q_sb[:, 2 * hh + 1, twin], False, True,
                           P if hh == H - 1 else None)
                return P.n

            def mask_evac(i, dst_sb, slots):
                e_s = 0
                for jj in range(i + 1):
                    pslot = slots[i][jj]
                    dst3 = dst_sb[:, jj * 4:jj * 4 + 4, :]
                    src3 = pslot[:, :].rearrange("p (h t) -> p h t", h=H)
                    if jj == i:
                        ve(lambda e, dst3=dst3, src3=src3: e.tensor_tensor(
                            dst3, src3,
                            utri_f[:, :].unsqueeze(1)
                            .broadcast_to([128, H, 128]), ALU.mult), V)
                    else:
                        e_s = se(lambda e, dst3=dst3, src3=src3:
                                 e.copy(out=dst3, in_=src3), S)
                return V.n, e_s

            def z_mms(i):
                twin = twins[i]
                for hh in range(H):
                    pz = z_slots[i][hh // 2]
                    col = (hh % 2) * 256
                    for jj in range(i + 1):
                        slot = hh if i == 0 else jj * 4 + hh
                        mm(pz[:, col:col + 256], am_sb[:, slot, :],
                           st_sb[:, jj, hh * 256:(hh + 1) * 256], jj == 0,
                           False)
                for hh in range(H):
                    pz = z_slots[i][hh // 2]
                    col = (hh % 2) * 256
                    mm(pz[:, col:col + 256], q_sb[:, 2 * hh, twin],
                       hkb[:, 2 * hh, :], False, False)
                    mm(pz[:, col:col + 256], q_sb[:, 2 * hh + 1, twin],
                       hkb[:, 2 * hh + 1, :], False, True,
                       P if hh == H - 1 else None)
                return P.n

            def sm_max(i, z_p):
                pz0, pz1 = z_slots[i]
                wt("vector", P, z_p)
                ve(lambda e, i=i, pz0=pz0: e.tensor_tensor(
                    z_sb[:, 0:512], epts[i][:, 0:512], pz0[:, :], ALU.mult))
                ve(lambda e, i=i, pz1=pz1: e.tensor_tensor(
                    z_sb[:, 512:1024], epts[i][:, 512:1024], pz1[:, :],
                    ALU.mult), V)
                ve(lambda e: e.reduce_max(
                    mx4[:, :], z_sb[:, :].rearrange("p (h m) -> p h m", h=H),
                    axis=AX.X))
                vdrain()
                return ve(lambda e: e.tensor_scalar_mul(nmx4[:, :], mx4[:, :],
                                                        -1.0), V)

            def sm_exp(nm_v):
                wt("scalar", V, nm_v)
                for hh2 in range(H):
                    se(lambda e, hh2=hh2: e.activation(
                        out=p_sb[:, hh2 * 256:(hh2 + 1) * 256],
                        in_=z_sb[:, hh2 * 256:(hh2 + 1) * 256], func=AF.Exp,
                        bias=nmx4[:, hh2:hh2 + 1],
                        accum_out=sum4[:, hh2:hh2 + 1]),
                       S if hh2 == H - 1 else None)
                return S.n

            def p2_chain(i, px_s):
                wt("vector", S, px_s)
                ve(lambda e: e.reciprocal(rden[:, :], sum4[:, :]))
                vdrain()
                ve(lambda e, i=i: e.tensor_tensor(z_sb[:, :], p_sb[:, :],
                                                  epts[i][:, :], ALU.mult))
                return ve(lambda e: e.tensor_tensor(
                    p2t[:, :].rearrange("p (h m) -> p h m", h=H),
                    z_sb[:, :].rearrange("p (h m) -> p h m", h=H),
                    rden[:, :].unsqueeze(2).broadcast_to([128, H, 256]),
                    ALU.mult), V)

            def p2_tr(pt_v):
                wt("tensor", V, pt_v)
                for sl in range(8):
                    tr(PT[:, sl * 128:(sl + 1) * 128],
                       p2t[:, sl * 128:(sl + 1) * 128], id_bf[:, :],
                       P if sl == 7 else None)
                wt("vector", P, P.n)
                return ve(lambda e: e.tensor_copy(
                    p2f[:, :, :].rearrange("p a b -> p (a b)"), PT[:, :]), V)

            def b_mms(i, tv2):
                wt("tensor", V, tv2)
                for jj in range(i + 1):
                    pslot = a_slots[i][jj]
                    for hh in range(H):
                        mm(pslot[:, hh * 128:hh * 128 + 128],
                           stf[:, jj * 8 + 2 * hh, :], p2f[:, 2 * hh, :],
                           True, False)
                        mm(pslot[:, hh * 128:hh * 128 + 128],
                           stf[:, jj * 8 + 2 * hh + 1, :],
                           p2f[:, 2 * hh + 1, :], False, True,
                           P if hh == H - 1 else None)
                return P.n

            def o_mms(i, bt_v, bt_s):
                wt("tensor", V, bt_v)
                if bt_s:
                    wt("tensor", S, bt_s)
                if i == 0:
                    wt("tensor", S, hvb_s)
                    wt("tensor", S, silu_done)
                for hh in range(H):
                    po = z_slots[i][hh // 2]
                    for vc in range(2):
                        col = (hh % 2) * 256 + vc * 128
                        mm(po[:, col:col + 128],
                           hvb[:, 2 * hh, vc * 128:vc * 128 + 128],
                           p2f[:, 2 * hh, :], True, False)
                        mm(po[:, col:col + 128],
                           hvb[:, 2 * hh + 1, vc * 128:vc * 128 + 128],
                           p2f[:, 2 * hh + 1, :], False, False)
                        for jj in range(i + 1):
                            slot = hh if i == 0 else jj * 4 + hh
                            mm(po[:, col:col + 128],
                               v_sb[:, 2 * ci + jj,
                                    hh * 256 + vc * 128:hh * 256 + vc * 128
                                    + 128],
                               bt_sb[:, slot, :], False, jj == i,
                               P if (hh == H - 1 and vc == 1 and jj == i)
                               else None)
                return P.n

            def o_evac(i, o_p):
                ts = (ts0, ts1)[i]
                po0, po1 = z_slots[i]
                wt("vector", P, o_p)
                wt("scalar", P, o_p)
                ov = ve(lambda e, ts=ts, po0=po0: e.tensor_copy(
                    o_sb[:, 0:4, ts * 128:ts * 128 + 128],
                    po0[:, :].rearrange("p (a t) -> p a t", a=4)), V)
                os_ = se(lambda e, ts=ts, po1=po1: e.copy(
                    out=o_sb[:, 4:8, ts * 128:ts * 128 + 128],
                    in_=po1[:, :].rearrange("p (a t) -> p a t", a=4)), S)
                return ov, os_

            wt("tensor", S, lamc_s)
            a0_p = a_mms(0)
            wt("tensor", V, st_vs[0])
            wt("tensor", V, ktv)
            stv0 = tr_grp(0)
            wt("vector", P, a0_p)
            am0_v, _ = mask_evac(0, am_sb, a_slots)
            wt("vector", P, bce_p)
            for hh in range(H):
                pb = (PC, PD)[hh // 2]
                col = (hh % 2) * 256
                ve(lambda e, hh=hh, pb=pb, col=col: e.tensor_tensor(
                    hk_sb[:, 2 * hh:2 * hh + 2, :],
                    hk_sb[:, 2 * hh:2 * hh + 2, :],
                    pb[:, col:col + 256].unsqueeze(1).broadcast_to([128, 2, 256]),
                    ALU.mult), V if hh == H - 1 else None)
            hkscale_v = V.n
            wt("scalar", V, hkscale_v)
            hkb_s = se(lambda e: e.copy(out=hkb[:, :, :], in_=hk_sb[:, :, :]), S)
            wt("scalar", V, scan_state["hv_add_v"])
            for sl in range(8):
                se(lambda e, sl=sl: e.activation(
                    out=hv_sb[:, sl, :], in_=hv_sb[:, sl, :], func=AF.Identity,
                    scale=lamc[:, sl:sl + 1]))
            hvb_s = se(lambda e: e.copy(out=hvb[:, :, :], in_=hv_sb[:, :, :]), S)
            wt("tensor", V, st_vs[1])
            wt("tensor", V, stv0)
            tr_grp(1)
            wt("tensor", V, am0_v)
            wt("tensor", S, hkb_s)
            z0_p = z_mms(0)
            a1_p = a_mms(1)
            wt("vector", S, ept_s)
            nm0 = sm_max(0, z0_p)
            wt("vector", P, a1_p)
            wt("scalar", P, a1_p)
            am1_v, am1_s = mask_evac(1, am_sb, a_slots)
            px0 = sm_exp(nm0)
            wt("tensor", V, am1_v)
            wt("tensor", S, am1_s)
            z1_p = z_mms(1)
            pt0 = p2_chain(0, px0)
            tv20 = p2_tr(pt0)
            nm1 = sm_max(1, z1_p)
            px1 = sm_exp(nm1)
            b0_p = b_mms(0, tv20)
            wt("vector", P, b0_p)
            bt0_v, _ = mask_evac(0, bt_sb, a_slots)
            o0_p = o_mms(0, bt0_v, 0)
            o_v, o_s = o_evac(0, o0_p)
            pt1 = p2_chain(1, px1)
            tv21 = p2_tr(pt1)
            b1_p = b_mms(1, tv21)
            wt("vector", P, b1_p)
            wt("scalar", P, b1_p)
            bt1_v, bt1_s = mask_evac(1, bt_sb, a_slots)
            o1_p = o_mms(1, bt1_v, bt1_s)
            o_v, o_s = o_evac(1, o1_p)

            wt("tensor", V, o_v)
            wt("tensor", S, o_s)
            last_chunk = (s == NSC - 1 and ci == NCHS - 1)
            if not last_chunk:
                for hh in range(H):
                    phk = (PA, PB, PC, PD)[hh]
                    for dc in range(2):
                        for jj in range(2):
                            mm(phk[:, dc * 256:dc * 256 + 256],
                               ktm[:, jj * 8 + 2 * hh + dc, :],
                               st_sb[:, jj, hh * 256:(hh + 1) * 256], jj == 0,
                               jj == 1,
                               P if (hh == H - 1 and dc == 1 and jj == 1)
                               else None)
                hk_p = P.n
                wt("vector", P, hk_p)
                for hh in range(H):
                    phk = (PA, PB, PC, PD)[hh]
                    ve(lambda e, hh=hh, phk=phk: e.tensor_tensor(
                        hk_sb[:, 2 * hh:2 * hh + 2, :]
                        .rearrange("p a b -> p (a b)"),
                        hk_sb[:, 2 * hh:2 * hh + 2, :]
                        .rearrange("p a b -> p (a b)"),
                        phk[:, :], ALU.add), V if hh == H - 1 else None)
                hk_v = V.n
                wt("tensor", V, hk_v)
                for hh in range(H):
                    phv = (PA, PB, PC, PD)[hh]
                    for mc in range(2):
                        for jj in range(2):
                            mm(phv[:, mc * 256:mc * 256 + 256],
                               st_sb[:, jj,
                                     hh * 256 + mc * 128:hh * 256 + mc * 128
                                     + 128],
                               v_sb[:, 2 * ci + jj, hh * 256:(hh + 1) * 256],
                               jj == 0, jj == 1,
                               P if (hh == H - 1 and mc == 1 and jj == 1)
                               else None)
                hv_p = P.n
                wt("vector", P, hv_p)
                wt("vector", S, hvb_s)
                for hh in range(H):
                    phv = (PA, PB, PC, PD)[hh]
                    ve(lambda e, hh=hh, phv=phv: e.tensor_tensor(
                        hv_sb[:, 2 * hh:2 * hh + 2, :]
                        .rearrange("p a b -> p (a b)"),
                        hv_sb[:, 2 * hh:2 * hh + 2, :]
                        .rearrange("p a b -> p (a b)"),
                        phv[:, :], ALU.add), V if hh == H - 1 else None)
                hv_v = V.n
                scan_state["hv_add_v"] = hv_v
                prev_chunk_pe = P.n
                scan_done_v = hv_v
            else:
                prev_chunk_pe = P.n
                scan_done_v = o_v
        wt("scalar", V, scan_done_v)
        sil_c = [0] * 8
        for c in range(8):
            sil_c[c] = se(lambda e, c=c: e.activation(
                out=o_sb[:, c, :], in_=o_sb[:, c, :], func=AF.Silu), S)
        osilu_s = sil_c[7]
        sq_mm_p = [0] * 8
        for c in range(8):
            wt("vector", S, sil_c[c])
            if c >= 2:
                wt("vector", P, sq_mm_p[c - 2])
            vsq2 = ve(lambda e, c=c: e.tensor_mul(xsq2[:, c % 2, :], o_sb[:, c, :],
                                                  o_sb[:, c, :]), V)
            wt("tensor", V, vsq2)
            sq_mm_p[c] = mm(PF[0:1, 0:512], invd_bf[:, :], xsq2[:, c % 2, :],
                            c == 0, c == 7, P)
        wt("scalar", P, P.n)
        se(lambda e: e.activation(out=rr1[:, :], in_=PF[0:1, 0:512], func=AF.Ln,
                                  bias=EPS))
        sdrain()
        sr2 = se(lambda e: e.activation(out=rr2[:, :], in_=rr1[:, :], func=AF.Exp,
                                        scale=-0.5), S)
        wt("tensor", S, sr2)
        bc2_ = mm(PE_[:, :], ones1f[:, :], rr2[:, :], True, True, P)
        wt("vector", P, bc2_)
        onv = 0
        for c in range(8):
            onv = ve(lambda e, c=c: e.scalar_tensor_tensor(
                o_sb[:, c, :], o_sb[:, c, :], gn_sb[:, c:c + 1], PE_[:, :],
                ALU.mult, ALU.mult), V)
        st_scan[s] = dict(scan_done_v=scan_done_v, on_v=onv)

    def emit_tail(s, wo_g):
        fr, sc_ = st_front[s], st_scan[s]
        wt("tensor", V, sc_["on_v"])
        wt("tensor", G, wo_g)
        if s + 1 in st_front:
            wt("tensor", S, st_front[s + 1]["sr"])
            wt("sync", V, st_front[s + 1]["h_done"])
        wt("sync", V, sc_["scan_done_v"])
        pre_done_v = 0
        pos_s_prev = 0
        for ts in range(4):
            tile = s * 4 + ts
            for nt in range(2):
                pb = (PA, PB)[nt]
                for kc in range(8):
                    mm(pb[:, :], o_sb[:, kc, ts * 128:(ts + 1) * 128],
                       whalf[nt][:, kc, :], kc == 0, kc == 7,
                       P if (kc == 7 and nt == 1) else None)
            at_p = P.n
            if ts > 0:
                wt("sync", V, pre_done_v)
            xl = sdma(x_tm[tile * 128:(tile + 1) * 128, :], xt_sb[:, :])
            wt("vector", P, at_p)
            wt("vector", L, xl)
            ve(lambda e: e.tensor_tensor(pre_sb[:, 0:512], xt_sb[:, 0:512],
                                         PA[:, :], ALU.add))
            pre_done_v = ve(lambda e: e.tensor_tensor(pre_sb[:, 512:1024],
                                                      xt_sb[:, 512:1024],
                                                      PB[:, :], ALU.add), V)
            wt("scalar", V, pre_done_v)
            se(lambda e: e.activation(out=sA[:, :], in_=pre_sb[:, :],
                                      func=AF.Identity, accum_out=mraw[:, :]))
            se(lambda e: e.activation(out=sA[:, :], in_=pre_sb[:, :],
                                      func=AF.Square, accum_out=vraw[:, :]))
            sdrain()
            se(lambda e: e.activation(out=tmp1c[:, :], in_=mraw[:, :],
                                      func=AF.Square, scale=1.0 / D))
            mn_s = se(lambda e: e.activation(out=mean1[:, :], in_=mraw[:, :],
                                             func=AF.Identity, scale=1.0 / D), S)
            wt("vector", S, mn_s)
            v2 = ve(lambda e: e.tensor_scalar(var1[:, :], vraw[:, :], 1.0 / D,
                                              tmp1c[:, :], ALU.mult,
                                              ALU.subtract), V)
            wt("scalar", V, v2)
            se(lambda e: e.activation(out=rstd1[:, :], in_=var1[:, :], func=AF.Ln,
                                      bias=EPS))
            sdrain()
            rs_s = se(lambda e: e.activation(out=rstd1[:, :], in_=rstd1[:, :],
                                             func=AF.Exp, scale=-0.5), S)
            wt("vector", S, rs_s)
            if ts > 0 or s > 0:
                wt("vector", SQ, SQ.n)
            y1v = ve(lambda e: e.tensor_scalar(
                y1t_sb[:, :], pre_sb[:, :], mean1[:, :], rstd1[:, :],
                ALU.subtract, ALU.mult), V)
            wt("scalar", V, y1v)
            qdma(y1t_sb[:, :], y1_dram[tile * 128:(tile + 1) * 128, :])
            wt("tensor", V, y1v)
            for half in range(2):
                for sl in range(4):
                    tr(PU[:, sl * 128:(sl + 1) * 128],
                       y1t_sb[:, half * 512 + sl * 128:half * 512 + (sl + 1)
                             * 128], id_f[:, :], P if sl == 3 else None)
                if half == 0:
                    wt("scalar", P, P.n)
                    tvh_s = se(lambda e: e.copy(
                        out=y1tf_sb[:, 0:4, :].rearrange("p a b -> p (a b)"),
                        in_=PU[:, :]), S)
                    wt("tensor", S, tvh_s)
                else:
                    wt("vector", P, P.n)
                    tvh = ve(lambda e: e.tensor_copy(
                        y1tf_sb[:, 4:8, :].rearrange("p a b -> p (a b)"),
                        PU[:, :]), V)
                    wt("tensor", V, tvh)
            if pos_s_prev:
                wt("tensor", S, pos_s_prev)
            for kc in range(8):
                mm(PF[:, 0:8], y1tf_sb[:, kc, :], wr_sb[:, kc, :],
                   kc == 0, kc == 7, P if kc == 7 else None)
            lg_p = P.n
            wt("scalar", P, lg_p)
            lg_s = se(lambda e: e.copy(out=lg_sb[:, :], in_=PF[:, 0:8]), S)
            wt("vector", S, lg_s)
            ve(lambda e: e.max(mx8[:, :], lg_sb[:, :]))
            vdrain()
            ve(lambda e: e.max_index(idxu[:, :], mx8[:, :], lg_sb[:, :]))
            vdrain()
            ve(lambda e, tile=tile: e.tensor_copy(idx_all[:, tile:tile + 1],
                                                  idxu[:, 0:1]))
            nm_v = ve(lambda e: e.tensor_scalar_mul(nmx[:, :], mx8[:, 0:1], -1.0),
                      V)
            wt("scalar", V, nm_v)
            px_s = se(lambda e: e.activation(out=pex[:, :], in_=lg_sb[:, :],
                                             func=AF.Exp, bias=nmx[:, :],
                                             accum_out=den8[:, :]), S)
            wt("vector", S, px_s)
            ve(lambda e, tile=tile: e.reciprocal(pmax_all[:, tile:tile + 1],
                                                 den8[:, :]))
            vdrain()
            oh_v = ve(lambda e, tile=tile: e.tensor_scalar(
                oh_sb[:, :], iota8[:, :], idx_all[:, tile:tile + 1], None,
                ALU.is_equal), V)
            wt("tensor", V, oh_v)
            mm(PF[:, 8:16], ustr[:, :], oh_sb[:, :], True, False)
            mm(PF[:, 8:16], ones1f[:, :], carry[0:1, :], False, True)
            cs_p = mm(PF[0:1, 16:24], onescol_f[:, :], oh_sb[:, :], True, True,
                      P)
            wt("scalar", P, cs_p)
            pos_s_prev = se(lambda e: e.copy(out=posx[:, :], in_=PF[:, 8:16]), S)
            wt("vector", S, pos_s_prev)
            wt("vector", P, cs_p)
            ve(lambda e: e.tensor_mul(tmp8[:, :], oh_sb[:, :], posx[:, :]))
            vdrain()
            ve(lambda e, tile=tile: e.reduce_sum(lp_all[:, tile:tile + 1],
                                                 tmp8[:, :], axis=AX.X))
            vdrain()
            cu_v = ve(lambda e: e.tensor_tensor(carry[0:1, :], carry[0:1, :],
                                                PF[0:1, 16:24], ALU.add), V)
            wt("tensor", V, cu_v)
        scan_state["y1q"] = SQ.n
        wdone_h(0)
        wdone_h(1)
        if s + 1 < NSC:
            return max(wload_h(woT, h2, h2)[1] for h2 in range(2))
        return 0

    wo_g = max(wload_h(woT, h2, h2)[1] for h2 in range(2))
    emit_front(0)
    for s in range(NSC):
        emit_scan(s)
        if s + 1 < NSC:
            emit_front(s + 1)
            emit_split(s + 1, 0)
        wo_g = emit_tail(s, wo_g)
    prev_sc_v = V.n
    prev_sc_pe = P.n

    def dump(ap, name, shape, dt=F32):
        d = dbg_out(name, shape, dt)
        if d is not None:
            wt("gpsimd", V, V.n)
            wt("gpsimd", S, S.n)
            wt("gpsimd", P, P.n)
            gdma(ap, d)

    dump(h_sb[:, :, :], "on", [128, 8, SC], BF16)
    dump(q_sb[:, :, :], "q", [128, 8, SC], BF16)
    dump(k_sb[:, :, :], "k", [128, 8, SC], BF16)
    dump(v_sb[:, :, :], "v", [128, 4, D], BF16)
    dump(g_sb[:, :, :], "g", [128, 4, D])
    dump(o_sb[:, :, :], "o", [128, 8, SC], BF16)
    dump(carry[:, :], "carry", [1, 8])
    dump(idx_all[:, :], "idx", [128, NT])
    dump(lp_all[:, :], "lp", [128, NT])
    dump(pmax_all[:, :], "pmax", [128, NT])
    dump(hk_sb[:, :, :], "hk", [128, 8, 256])
    dump(hv_sb[:, :, :], "hv", [128, 8, 256])
    dump(z_sb[:, :], "z", [128, D])
    dump(p_sb[:, :], "p", [128, D])
    dump(p1t[:, :], "p1t", [128, D], BF16)
    dump(p2t[:, :], "p2t", [128, D], BF16)
    dump(am_sb[:, :, :], "am", [128, 8, 128], BF16)
    dump(bt_sb[:, :, :], "bt", [128, 8, 128], BF16)
    dump(stf[:, :, :], "stf", [128, 16, 128], BF16)
    dump(ktm[:, :, :], "ktm", [128, 16, 128], BF16)
    dump(st_sb[:, :, :], "st", [128, 2, D], BF16)
    dump(gc_sb[:, :, :], "gc", [128, 2, D])
    dump(pre_sb[:, :], "pre", [128, D])
    dump(y1t_sb[:, :], "y1t", [128, D])
    dump(xt_sb[:, :], "xt", [128, D])
    dump(lg_sb[:, :], "lg", [128, 8])
    dump(mx8[:, :], "mx8", [128, 8])
    dump(y1tf_sb[:, :, :], "y1f", [128, 8, 128])
    dump(mean1[:, :], "mean1", [128, 1])
    dump(var1[:, :], "var1", [128, 1])

    if stage >= 4:
        arena = Arena(nc, [h_sb, q_sb, k_sb, v_sb, g_sb, gc_sb, gd_sb, st_sb,
                           z_sb, p_sb, p1t, p2t, stf, p1f, p2f, ktm, am_sb,
                           bt_sb, hk_sb, hv_sb, hkb, hvb, o_sb, xsq2,
                           whalf[0], whalf[1], whalf[2], whalf[3],
                           sA, rr1, rr2])
        w1p = [arena.at(f"w1_{i}", [128, 2, DFF], BF16) for i in range(4)]
        w2p = [arena.at(f"w2_{i}", [128, 8, D], BF16) for i in range(4)]
        h1p = [arena.at(f"h1_{i}", [128, 16, 256], BF16) for i in range(2)]
        dgat = arena.at("dgat", [128, 2, D], BF16)
        dfm = arena.at("dfm", [128, 8, 256], BF16)
        eo_sb = arena.at("eo_sb", [128, 2, D], BF16)
        eo32 = arena.at("eo32", [128, D], F32)
        comb = arena.at("comb", [128, D], BF16)
        cnt_sb = arena.at("cnt_sb", [8, 8], F32)
        cntT = arena.at("cntT", [8, 8], F32)
        offsT = arena.at("offsT", [8, 8], F32)
        cstT = arena.at("cstT", [8, 8], F32)
        ccol = arena.at("ccol", [8, 1], F32)
        cst_row = arena.at("cst_row", [1, 8], F32)
        exoffs = arena.at("exoffs", [1, 8], F32)
        excnt = arena.at("excnt", [1, 8], F32)
        hi_row = arena.at("hi_row", [1, 8], F32)
        base_row = arena.at("base_row", [1, 8], F32)
        lo_bc = arena.at("lo_bc", [128, 8], F32)
        hi_bc = arena.at("hi_bc", [128, 8], F32)
        base_bc = arena.at("base_bc", [128, 8], F32)
        obc = arena.at("obc", [128, 8], F32)
        m160 = arena.at("m160", [128, 160], F32)
        v160 = arena.at("v160", [128, 160], F32)
        srcg = arena.at("srcg", [128, 20], F32)
        srcgu = arena.at("srcgu", [128, 20], U32)
        sru = arena.at("sru", [128, NT], U32)
        dval = arena.at("dval", [128, 1], F32)
        offs_row = arena.at("offs_row", [1, 8], F32)
        kcol = arena.at("kcol", [128, 1], F32)
        didx = arena.at("didx", [128, 1], U32)
        y1bf = arena.at("y1bf", [128, D], BF16)
        cstb = arena.at("cstb", [128, 8], F32)
        sr_all = arena.at("sr_all", [128, NT], F32)
        f2_all = arena.at("f2_all", [128, NT], F32)
        srb_all = arena.at("srb_all", [128, NT], F32)
        srub = arena.at("srub", [128, NT], U32)
        kca = arena.at("kca", [128, 1], F32)
        kcb = arena.at("kcb", [128, 1], F32)

        wt("scalar", V, V.n)
        cq = qdma(carry[0:1, :], cnt_in_d[0:1, :])
        wt("gpsimd", SQ, cq)
        cc1 = ctx.op("gpsimd", lambda e: e.collective_compute(
            "AllGather", ALU.bypass, replica_groups=[list(range(8))],
            ins=[cnt_in_d[:, :].opt()], outs=[cnt_ag_d[:, :].opt()]), CC)

        wt("tensor", V, V.n)
        tr(PU[0:8, 0:1], carry[0:1, :], id_f[0:1, 0:1], P)
        wt("vector", P, P.n)
        ve(lambda e: e.tensor_copy(ccol[:, :], PU[0:8, 0:1]), V)
        vdrain()
        wt("tensor", V, V.n)
        mm(PU[0:1, 8:16], ccol[:, :], str8[:, :], True, True, P)
        wt("vector", P, P.n)
        ve(lambda e: e.tensor_copy(cst_row[:, :], PU[0:1, 8:16]), V)
        vdrain()
        wt("tensor", V, V.n)
        mm(PU[:, 16:24], ones1f[:, :], cst_row[0:1, :], True, True, P)
        wt("vector", P, P.n)
        ve(lambda e: e.tensor_copy(cstb[:, :], PU[:, 16:24]), V)
        vdrain()

        dall = arena.at("dall", [128, NT], F32)
        didx_all = arena.at("didx_all", [128, NT], U32)
        y1s2 = nc.alloc_sbuf_tensor_at("y1s2", [128, D], F32,
                                       offset=nc.lookup_mloc(w1p[0]).addr)
        y1bf2 = nc.alloc_sbuf_tensor_at("y1bf2", [128, D], BF16,
                                        offset=nc.lookup_mloc(w1p[0]).addr + 4096)
        y1s3 = nc.alloc_sbuf_tensor_at("y1s3", [128, D], F32,
                                       offset=nc.lookup_mloc(w1p[0]).addr + 6144)
        y1bf3 = nc.alloc_sbuf_tensor_at(
            "y1bf3", [128, D], BF16, offset=nc.lookup_mloc(w1p[0]).addr + 10240)
        y1s4 = nc.alloc_sbuf_tensor_at("y1s4", [128, D], F32,
                                       offset=nc.lookup_mloc(w1p[1]).addr)
        y1bf4 = nc.alloc_sbuf_tensor_at("y1bf4", [128, D], BF16,
                                        offset=nc.lookup_mloc(w1p[1]).addr + 4096)
        ybufs = (y1_sb, y1s2, y1s3, y1s4)
        bbufs = (y1bf, y1bf2, y1bf3, y1bf4)
        wt("sync", V, V.n)
        wt("sync", P, P.n)
        yls = []
        for tile in range(4):
            yls.append((sdma(y1_dram[tile * 128:(tile + 1) * 128, :],
                             ybufs[tile % 4][:, :]), "sync", L))
        for tile in range(NT):
            ve(lambda e, tile=tile: e.tensor_scalar(
                oh_sb[:, :], iota8[:, :], idx_all[:, tile:tile + 1], None,
                ALU.is_equal))
            vdrain()
            ve(lambda e: e.tensor_mul(tmp8[:, :], oh_sb[:, :], cstb[:, :]))
            vdrain()
            ve(lambda e, tile=tile: e.reduce_sum(dall[:, tile:tile + 1],
                                                 tmp8[:, :], axis=AX.X))
            vdrain()
            ve(lambda e, tile=tile: e.tensor_tensor(
                dall[:, tile:tile + 1], dall[:, tile:tile + 1],
                lp_all[:, tile:tile + 1], ALU.add))
            vdrain()
        ve(lambda e: e.tensor_copy(didx_all[:, :], dall[:, :]), V)
        vdrain()
        d_v = V.n
        scat_g = []
        for tile in range(NT):
            lsem, _, lq = yls[tile]
            wt("vector", lq, lsem)
            if len(scat_g) >= 4:
                wt("vector", G, scat_g[-4])
            yb = ve(lambda e, tile=tile: e.tensor_copy(bbufs[tile % 4][:, :],
                                                       ybufs[tile % 4][:, :]), V)
            wt("gpsimd", V, max(yb, d_v))
            sg = ctx.op("gpsimd", lambda e, tile=tile: e.indirect_dma_start(
                out=ysort_dram[:, :],
                out_offset=IndirectOffsetOnAxis(didx_all[:, tile:tile + 1], 0),
                in_=bbufs[tile % 4][:, :], in_offset=None,
                bounds_check=bcregs["t"], oob_is_err=False), G, 16)
            scat_g.append(sg)
            if tile + 4 < NT:
                eng = ("sync", "scalar")[tile % 2]
                wt(eng, V, yb)
                yls.append(((sdma, qdma)[tile % 2](
                    y1_dram[(tile + 4) * 128:(tile + 5) * 128, :],
                    ybufs[tile % 4][:, :]), eng, (L, SQ)[tile % 2]))
        wt("gpsimd", G, scat_g[-1])
        cc2 = ctx.op("gpsimd", lambda e: e.collective_compute(
            "AllGather", ALU.bypass, replica_groups=[list(range(8))],
            ins=[ysort_dram[:, :].opt()], outs=[ag1_d[:, :].opt()]), CC)


        wt("sync", CC, cc1)
        cl = sdma(cnt_ag_d[:, :], cnt_sb[:, :])

        for i in range(4):
            gw1 = gdma(w1e_in[i * 256:(i + 1) * 256, :]
                       .rearrange("(c p) f -> p c f", p=128), w1p[i][:, :, :])
        for i in range(4):
            gw2b = gdma(w2e_in[i * 1024:(i + 1) * 1024, :]
                        .rearrange("(c p) d -> p c d", p=128), w2p[i][:, :, :])

        wt("vector", L, cl)
        wt("tensor", L, cl)
        wt("tensor", V, V.n)
        mm(PU[0:1, 24:32], bmask[:, :], cnt_sb[:, :], True, True, P)
        wt("vector", P, P.n)
        ve(lambda e: e.tensor_copy(offs_row[:, :], PU[0:1, 24:32]), V)
        vdrain()
        wt("tensor", V, V.n)
        mm(PU[:, 32:40], ones1f[:, :], offs_row[0:1, :], True, True, P)
        wt("vector", P, P.n)
        ve(lambda e: e.tensor_copy(obc[:, :], PU[:, 32:40]), V)
        vdrain()
        for tile in range(NT):
            ve(lambda e, tile=tile: e.tensor_scalar(
                oh_sb[:, :], iota8[:, :], idx_all[:, tile:tile + 1], None,
                ALU.is_equal))
            vdrain()
            ve(lambda e: e.tensor_mul(tmp8[:, :], oh_sb[:, :], obc[:, :]))
            vdrain()
            ve(lambda e: e.reduce_sum(dval[:, :], tmp8[:, :], axis=AX.X))
            vdrain()
            ve(lambda e, tile=tile: e.tensor_tensor(
                dval[:, :], dval[:, :], lp_all[:, tile:tile + 1], ALU.add))
            vdrain()
            ve(lambda e: e.tensor_single_scalar(kcol[:, :], dval[:, :], float(CAP),
                                                ALU.is_lt))
            ve(lambda e: e.tensor_single_scalar(kca[:, :], dval[:, :], 2048.0,
                                                ALU.is_lt))
            vdrain()
            ve(lambda e, tile=tile: e.tensor_scalar(
                f2_all[:, tile:tile + 1], kcol[:, :], -1.0, 2.0, ALU.mult, ALU.add))
            ve(lambda e: e.tensor_sub(kcb[:, :], kcol[:, :], kca[:, :]))
            ve(lambda e, tile=tile: e.scalar_tensor_tensor(
                sr_all[:, tile:tile + 1], idx_all[:, tile:tile + 1], 2048.0,
                dval[:, :], ALU.mult, ALU.add))
            ve(lambda e, tile=tile: e.scalar_tensor_tensor(
                srb_all[:, tile:tile + 1], idx_all[:, tile:tile + 1], 512.0,
                dval[:, :], ALU.mult, ALU.add))
            vdrain()
            ve(lambda e: e.tensor_scalar(kca[:, :], kca[:, :], -BIG, BIG,
                                         ALU.mult, ALU.add))
            ve(lambda e: e.tensor_scalar(kcb[:, :], kcb[:, :], -BIG, BIG - 2048.0,
                                         ALU.mult, ALU.add))
            vdrain()
            ve(lambda e, tile=tile: e.tensor_tensor(
                sr_all[:, tile:tile + 1], sr_all[:, tile:tile + 1], kca[:, :],
                ALU.add))
            ve(lambda e, tile=tile: e.tensor_tensor(
                srb_all[:, tile:tile + 1], srb_all[:, tile:tile + 1], kcb[:, :],
                ALU.add), V)
            vdrain()
        ve(lambda e: e.tensor_copy(sru[:, :], sr_all[:, :]))
        ve(lambda e: e.tensor_copy(srub[:, :], srb_all[:, :]), V)
        vdrain()

        wt("tensor", V, V.n)
        tr(PU[0:8, 40:48], cnt_sb[:, :], id_f[0:8, 0:8], P)
        wt("vector", P, P.n)
        ve(lambda e: e.tensor_copy(cntT[:, :], PU[0:8, 40:48]), V)
        vdrain()
        wt("tensor", V, V.n)
        mm(PU[0:8, 48:56], cnt_sb[:, :], str8[:, :], True, True)
        mm(PU[0:8, 56:64], str8[:, :], cntT[:, :], True, True, P)
        wt("vector", P, P.n)
        ve(lambda e: e.tensor_copy(offsT[:, :], PU[0:8, 48:56]))
        ve(lambda e: e.tensor_copy(cstT[:, :], PU[0:8, 56:64]), V)
        vdrain()
        wt("tensor", V, V.n)
        mm(PU[0:1, 64:72], ecol[:, :], offsT[:, :], True, True)
        mm(PU[0:1, 72:80], ecol[:, :], cstT[:, :], True, True)
        mm(PU[0:1, 80:88], ecol[:, :], cntT[:, :], True, True, P)
        wt("vector", P, P.n)
        ve(lambda e: e.tensor_copy(exoffs[:, :], PU[0:1, 64:72]))
        ve(lambda e: e.tensor_copy(cst_row[:, :], PU[0:1, 72:80]))
        ve(lambda e: e.tensor_copy(excnt[:, :], PU[0:1, 80:88]), V)
        vdrain()
        ve(lambda e: e.tensor_tensor(hi_row[:, :], exoffs[:, :], excnt[:, :],
                                     ALU.add))
        ve(lambda e: e.tensor_tensor(base_row[:, :], rowbase[:, :], cst_row[:, :],
                                     ALU.add))
        vdrain()
        ve(lambda e: e.tensor_tensor(base_row[:, :], base_row[:, :], exoffs[:, :],
                                     ALU.subtract))
        vdrain()
        br = ve(lambda e: e.tensor_scalar_add(base_row[:, :], base_row[:, :],
                                              -BIG), V)
        wt("tensor", V, br)
        mm(PU[:, 88:96], ones1f[:, :], exoffs[0:1, :], True, True)
        mm(PU[:, 96:104], ones1f[:, :], hi_row[0:1, :], True, True)
        mm(PU[:, 104:112], ones1f[:, :], base_row[0:1, :], True, True, P)
        wt("vector", P, P.n)
        ve(lambda e: e.tensor_copy(lo_bc[:, :], PU[:, 88:96]))
        ve(lambda e: e.tensor_copy(hi_bc[:, :], PU[:, 96:104]))
        ve(lambda e: e.tensor_copy(base_bc[:, :], PU[:, 104:112]), V)
        vdrain()
        s3 = s160[:, :].rearrange("p (g e) -> p g e", e=8)
        ve(lambda e: e.tensor_tensor(
            m160[:, :].rearrange("p (g e) -> p g e", e=8), s3,
            lo_bc[:, :].unsqueeze(1).broadcast_to([128, 20, 8]), ALU.is_ge))
        ve(lambda e: e.tensor_tensor(
            v160[:, :].rearrange("p (g e) -> p g e", e=8), s3,
            hi_bc[:, :].unsqueeze(1).broadcast_to([128, 20, 8]), ALU.is_lt))
        vdrain()
        ve(lambda e: e.tensor_mul(m160[:, :], m160[:, :], v160[:, :]))
        ve(lambda e: e.tensor_tensor(
            v160[:, :].rearrange("p (g e) -> p g e", e=8), s3,
            base_bc[:, :].unsqueeze(1).broadcast_to([128, 20, 8]), ALU.add))
        vdrain()
        ve(lambda e: e.tensor_mul(v160[:, :], v160[:, :], m160[:, :]))
        vdrain()
        ve(lambda e: e.reduce_sum(srcg[:, :],
                                  v160[:, :].rearrange("p (g e) -> p g e", e=8),
                                  axis=AX.X))
        vdrain()
        ve(lambda e: e.tensor_scalar_add(srcg[:, :], srcg[:, :], BIG))
        vdrain()
        sgu = ve(lambda e: e.tensor_copy(srcgu[:, :], srcg[:, :]), V)
        vdrain()

        wt("gpsimd", CC, cc2)
        wt("gpsimd", V, sgu)
        wt("tensor", G, gw1)
        wt("tensor", G, gw2b)
        eo_q = [0, 0]
        dg_g = [0] * 22
        tr_p = [0] * 22
        ev_v = [0, 0]

        def dgat_fetch(tt):
            buf = tt % 2
            if tt >= 2:
                wt("vector", P, tr_p[tt - 2])
            ms = ve(lambda e, buf=buf: e.memset(dgat[:, buf, :], 0.0), V)
            wt("gpsimd", V, ms)
            dg_g[tt] = ctx.op("gpsimd", lambda e, buf=buf, tt=tt:
                              e.indirect_dma_start(
                                  out=dgat[:, buf, :], out_offset=None,
                                  in_=ag1_d[:, :],
                                  in_offset=IndirectOffsetOnAxis(
                                      srcgu[:, tt:tt + 1], 0),
                                  bounds_check=bcregs["bt"],
                                  oob_is_err=False), G, 16)

        dgat_fetch(0)
        dgat_fetch(1)
        for grp in range(10):
            for half in range(2):
                tt = grp * 2 + half
                buf = tt % 2
                wt("tensor", G, dg_g[tt])
                for sl in range(8):
                    tr(PT[:, sl * 128:(sl + 1) * 128],
                       dgat[:, buf, sl * 128:(sl + 1) * 128], id_bf[:, :],
                       P if sl == 7 else None)
                tr_p[tt] = P.n
                wt("vector", P, tr_p[tt])
                tv = ve(lambda e, half=half: e.tensor_copy(
                    dfm[:, :, :].rearrange("p a b -> p a b")[:, :, half * 128:
                                                            half * 128 + 128],
                    PT[:, :].rearrange("p (a b) -> p a b", a=8)), V)
                wt("tensor", V, tv)
                if tt + 2 < 20:
                    dgat_fetch(tt + 2)
            hv_evac = []
            for fp in range(16):
                pb = (PE_, PF)[fp % 2]
                if fp >= 2:
                    wt("tensor", S, hv_evac[fp - 2])
                for half2 in range(2):
                    fc = 2 * fp + half2
                    for dc in range(8):
                        mm(pb[:, half2 * 256:half2 * 256 + 256],
                           w1p[dc // 2][:, dc % 2, fc * 128:(fc + 1) * 128],
                           dfm[:, dc, :], dc == 0, dc == 7,
                           P if (dc == 7 and half2 == 1) else None)
                wt("scalar", P, P.n)
                hv_evac.append(se(lambda e, fp=fp, pb=pb: e.activation(
                    out=h1p[fp // 8][:, (2 * fp) % 16:(2 * fp) % 16 + 2, :]
                    .rearrange("p a b -> p (a b)"),
                    in_=pb[:, :], func=AF.Relu), S))
            h1_s = hv_evac[-1]
            wt("tensor", S, h1_s)
            for half in range(2):
                wt("tensor", V, ev_v[half])
                bks = ((PC, PD), (PA, PB))[half]
                for nt in range(2):
                    pb = bks[nt]
                    for fc in range(32):
                        mm(pb[:, :],
                           h1p[fc // 16][:, fc % 16, half * 128:half * 128 + 128],
                           w2p[fc // 8][:, fc % 8, nt * 512:(nt + 1) * 512],
                           fc == 0, fc == 31,
                           P if (fc == 31 and nt == 1) else None)
                eo_p = P.n
                wt("vector", P, eo_p)
                wt("vector", L, eo_q[half])
                ve(lambda e, half=half, bks=bks: e.tensor_copy(
                    eo_sb[:, half, 0:512], bks[0][:, :]))
                ev_v[half] = ve(lambda e, half=half, bks=bks: e.tensor_copy(
                    eo_sb[:, half, 512:1024], bks[1][:, :]), V)
                wt("sync", V, ev_v[half])
                row0 = grp * 256 + half * 128
                eo_q[half] = sdma(eo_sb[:, half, :], eout_d[row0:row0 + 128, :])
            if grp == 7 and AG2_EARLY:
                wt("gpsimd", L, max(eo_q))
                cc3a = ctx.op("gpsimd", lambda e: e.collective_compute(
                    "AllGather", ALU.bypass, replica_groups=[list(range(8))],
                    ins=[eout_d[0:2048, :].opt()], outs=[ag2a_d[:, :].opt()]), CC)

        if not AG2_EARLY:
            wt("gpsimd", L, max(eo_q))
            cc3a = ctx.op("gpsimd", lambda e: e.collective_compute(
                "AllGather", ALU.bypass, replica_groups=[list(range(8))],
                ins=[eout_d[0:2048, :].opt()], outs=[ag2a_d[:, :].opt()]), CC)
        wt("gpsimd", L, max(eo_q))
        if AG2_EARLY:
            wt("gpsimd", CC, cc3a)
        cc3 = ctx.op("gpsimd", lambda e: e.collective_compute(
            "AllGather", ALU.bypass, replica_groups=[list(range(8))],
            ins=[eout_d[2048:2560, :].opt()], outs=[ag2b_d[:, :].opt()]), CC)

        comb2 = nc.alloc_sbuf_tensor_at("comb2", [128, D], BF16,
                                        offset=nc.lookup_mloc(w1p[1]).addr)
        y2st2 = nc.alloc_sbuf_tensor_at("y2st2", [128, D], F32,
                                        offset=nc.lookup_mloc(dfm).addr)
        y1c2 = nc.alloc_sbuf_tensor_at("y1c2", [128, D], F32,
                                       offset=nc.lookup_mloc(w1p[1]).addr + 4096)
        preb = nc.alloc_sbuf_tensor_at("preb", [128, D], F32, offset=_xaddr)
        lnsc = nc.alloc_sbuf_tensor_at("lnsc", [128, D], F32,
                                       offset=nc.lookup_mloc(h1p[0]).addr)
        cbufs = (comb, comb2)
        ycbufs = (y1_sb, y1c2)
        prebufs = (pre_sb, preb)
        statn = ((mraw, vraw, mean1, var1, rstd1, tmp1c),
                 (mraw2, vraw2, mean1b, var1b, rstd1b, tmp1cb))
        wt("gpsimd", CC, cc3)
        out_q = {}
        cgs = {}
        ylc = {}
        pd_t = {}
        mn_t = {}
        v2_t = {}
        rs_t = {}

        def c_prefetch(tile):
            p = tile % 2
            msv = ve(lambda e, p=p: e.memset(cbufs[p][:, :], 0.0), V)
            wt("gpsimd", V, msv)
            if tile >= 2:
                wt("gpsimd", G, cgs[tile - 2])
            ctx.op("gpsimd", lambda e, p=p, tile=tile: e.indirect_dma_start(
                out=cbufs[p][:, :], out_offset=None, in_=ag2a_d[:, :],
                in_offset=IndirectOffsetOnAxis(sru[:, tile:tile + 1], 0),
                bounds_check=bcregs["eca"], oob_is_err=False), G, 16)
            cgs[tile] = ctx.op(
                "gpsimd", lambda e, p=p, tile=tile: e.indirect_dma_start(
                    out=cbufs[p][:, :], out_offset=None, in_=ag2b_d[:, :],
                    in_offset=IndirectOffsetOnAxis(srub[:, tile:tile + 1], 0),
                    bounds_check=bcregs["ecb"], oob_is_err=False), G, 16)
            wt(("sync", "scalar")[p], V, pd_t.get(tile - 2, 0))
            ylc[tile] = (sdma, qdma)[p](y1_dram[tile * 128:(tile + 1) * 128, :],
                                        ycbufs[p][:, :])

        def c_stage1(tile):
            p = tile % 2
            wt("vector", G, cgs[tile])
            wt("vector", (L, SQ)[p], ylc[tile])
            ve(lambda e, p=p, tile=tile: e.tensor_scalar(
                prebufs[p][:, :], ycbufs[p][:, :], f2_all[:, tile:tile + 1],
                None, ALU.mult))
            vdrain()
            pd_t[tile] = ve(lambda e, p=p, tile=tile: e.scalar_tensor_tensor(
                prebufs[p][:, :], cbufs[p][:, :], pmax_all[:, tile:tile + 1],
                prebufs[p][:, :], ALU.mult, ALU.add), V)
            if tile + 2 < NT:
                c_prefetch(tile + 2)

        def c_stage2(tile):
            p = tile % 2
            mr, vr, mn, vv, rs, tc = statn[p]
            wt("scalar", V, pd_t[tile])
            se(lambda e, p=p, mr=mr: e.activation(
                out=lnsc[:, :], in_=prebufs[p][:, :], func=AF.Identity,
                accum_out=mr[:, :]))
            se(lambda e, p=p, vr=vr: e.activation(
                out=lnsc[:, :], in_=prebufs[p][:, :], func=AF.Square,
                accum_out=vr[:, :]))
            sdrain()
            se(lambda e, mr=mr, tc=tc: e.activation(
                out=tc[:, :], in_=mr[:, :], func=AF.Square, scale=1.0 / D))
            mn_t[tile] = se(lambda e, mr=mr, mn=mn: e.activation(
                out=mn[:, :], in_=mr[:, :], func=AF.Identity, scale=1.0 / D), S)

        def c_finish(tile):
            p = tile % 2
            mr, vr, mn, vv, rs, tc = statn[p]
            wt("vector", S, mn_t[tile])
            v2_t[tile] = ve(lambda e, vr=vr, vv=vv, tc=tc: e.tensor_scalar(
                vv[:, :], vr[:, :], 1.0 / D, tc[:, :], ALU.mult,
                ALU.subtract), V)
            wt("scalar", V, v2_t[tile])
            se(lambda e, vv=vv, rs=rs: e.activation(out=rs[:, :], in_=vv[:, :],
                                                    func=AF.Ln, bias=EPS))
            sdrain()
            rs_t[tile] = se(lambda e, rs=rs: e.activation(
                out=rs[:, :], in_=rs[:, :], func=AF.Exp, scale=-0.5), S)
            wt("vector", S, rs_t[tile])
            if tile >= 2:
                wt("vector", SQ, out_q[tile - 2])
            y2buf = (eo32[:, :], y2st2[:, :])[p]
            y2v = ve(lambda e, y2buf=y2buf, p=p, mn=mn, rs=rs: e.tensor_scalar(
                y2buf, prebufs[p][:, :], mn[:, :], rs[:, :],
                ALU.subtract, ALU.mult), V)
            wt("scalar", V, y2v)
            out_q[tile] = qdma(y2buf, y2_out[tile * 128:(tile + 1) * 128, :])

        for tile in range(min(2, NT)):
            c_prefetch(tile)
        for tile in range(NT):
            c_stage1(tile)
            c_stage2(tile)
            if tile > 0:
                c_finish(tile - 1)
        c_finish(NT - 1)
        wt("gpsimd", SQ, out_q[NT - 1])
    else:
        wt("gpsimd", SQ, SQ.n)
        wt("gpsimd", V, V.n)
        for tile in range(NT):
            gdma(y1_dram[tile * 128:(tile + 1) * 128, :],
                 y2_out[tile * 128:(tile + 1) * 128, :])
        wt("gpsimd", G, G.n)

    with nc.Block() as block:
        @block.sync
        def _(eng):
            for fn in ctx.streams["sync"]:
                fn(eng)

        @block.gpsimd
        def _(eng):
            for fn in ctx.streams["gpsimd"]:
                fn(eng)

        @block.tensor
        def _(eng):
            for fn in ctx.streams["tensor"]:
                fn(eng)

        @block.vector
        def _(eng):
            for fn in ctx.streams["vector"]:
                fn(eng)

        @block.scalar
        def _(eng):
            for fn in ctx.streams["scalar"]:
                fn(eng)

    return nc


def _consts():
    ident = np.eye(128)
    jj, tt = np.mgrid[0:128, 0:128]
    utri = (jj <= tt).astype(np.float32)
    ustrict = (jj < tt).astype(np.float32)
    iota8 = np.tile(np.arange(8, dtype=np.float32), (128, 1))
    s160 = np.zeros((128, 160), np.float32)
    for g in range(20):
        for e in range(8):
            s160[:, g * 8 + e] = g * 128 + np.arange(128)
    ee, e2 = np.mgrid[0:8, 0:8]
    str8 = (ee < e2).astype(np.float32)
    return ident, utri, ustrict, iota8, s160, str8


_BUILD_KW = {}


def prep_in_maps(inputs):
    x = np.asarray(inputs["L0_attn_hidden_states"], np.float32)
    nw = np.asarray(inputs["L0_attn_norm_weight"], np.float32)
    gn = np.asarray(inputs["L0_attn_gnorm_weight"], np.float32)
    wq = np.asarray(inputs["L0_attn_w_q"], np.float32)
    wk = np.asarray(inputs["L0_attn_w_k"], np.float32)
    wv = np.asarray(inputs["L0_attn_w_v"], np.float32)
    wf = np.asarray(inputs["L0_attn_w_f"], np.float32)
    wo = np.asarray(inputs["L0_attn_w_o"], np.float32)
    Wr = np.asarray(inputs["L0_ffn_Wr"], np.float32)
    W1 = np.asarray(inputs["L0_ffn_W1"], np.float32)
    W2 = np.asarray(inputs["L0_ffn_W2"], np.float32)

    ident, utri, ustrict, iota8, s160, str8 = _consts()
    shared = dict(
        wqT=np.ascontiguousarray(wq.T), wkT=np.ascontiguousarray(wk.T),
        wvT=np.ascontiguousarray(wv.T), wfT=np.ascontiguousarray(wf.T),
        woT=np.ascontiguousarray(wo.T),
        nw_c=np.ascontiguousarray(nw.reshape(8, 128).T),
        gn_c=np.ascontiguousarray(gn.reshape(8, 128).T),
        wr=Wr,
        ident_bf=ident.astype(ml_dtypes.bfloat16),
        ident_f32=ident.astype(np.float32),
        utri_bf=utri.astype(ml_dtypes.bfloat16), utri_f=utri,
        ustrict_f32=ustrict,
        iota8=iota8, s160=s160, str8=str8,
        rowbase=(np.arange(8, dtype=np.float32) * T)[None, :],
        ones1f=np.ones((1, 128), np.float32),
        onescol_bf=np.ones((128, 1), ml_dtypes.bfloat16),
        onescol_f=np.ones((128, 1), np.float32),
        invd_bf=np.full((128, 1), 1.0 / 1024.0, ml_dtypes.bfloat16),
        epscol=np.full((128, 1), 1e-5, np.float32),
    )
    in_maps = []
    for b in range(8):
        m = dict(shared)
        m["x_fm"] = np.ascontiguousarray(x[b].T)
        m["x_tm"] = np.ascontiguousarray(x[b])
        m["w1e"] = np.ascontiguousarray(W1[b])
        m["w2e"] = np.ascontiguousarray(W2[b])
        m["bmask"] = (np.arange(8) < b).astype(np.float32)[:, None]
        m["ecol"] = (np.arange(8) == b).astype(np.float32)[:, None]
        in_maps.append(m)
    return in_maps


def kernel(**inputs):
    in_maps = prep_in_maps(inputs)
    nc = build_nc(**_BUILD_KW)
    res = run_bass_kernel_spmd(nc, in_maps, core_ids=list(range(8)))
    out = np.stack([r["y2"] for r in res.results], axis=0)
    return out


# version 28
# speedup vs baseline: 1.2854x; 1.2854x over previous
import numpy as np
import ml_dtypes
import concourse.bass as bass
import concourse.mybir as mybir
from concourse.bass_core_rust import IndirectOffsetOnAxis
from concourse.bass_utils import run_bass_kernel_spmd

F32 = mybir.dt.float32
BF16 = mybir.dt.bfloat16
U32 = mybir.dt.uint32
AF = mybir.ActivationFunctionType
ALU = mybir.AluOpType
AX = mybir.AxisListType

B, T, D = 8, 2048, 1024
H = 4
E, DFF, CAP = 8, 4096, 2560
SC, NSC = 512, 4
NCHS = 2
NT = 16
EPS = 1e-5
SC16 = 1.0 / 16.0
BIG = float(2 ** 24)

ENGINES = ("sync", "gpsimd", "tensor", "vector", "scalar")


class Sem:
    def __init__(self, nc, name):
        self.h = nc.alloc_semaphore(name)
        self.n = 0

    def bump(self, k):
        self.n += k
        return self.n


class Ctx:
    def __init__(self, nc):
        self.nc = nc
        self.streams = {e: [] for e in ENGINES}

    def emit(self, engine, fn):
        self.streams[engine].append(fn)

    def wait(self, engine, sem, val):
        if val <= 0:
            return
        self.emit(engine, lambda eng, s=sem.h, v=val: eng.wait_ge(s, v))

    def op(self, engine, build, sem=None, k=1):
        if sem is None:
            self.emit(engine, build)
            return None
        v = sem.bump(k)
        self.emit(engine, lambda eng, b=build, s=sem.h, kk=k: b(eng).then_inc(s, kk))
        return v


class Arena:

    def __init__(self, nc, dead_handles):
        self.nc = nc
        dsz = {F32: 4, BF16: 2, U32: 4}
        regs = []
        for h in dead_handles:
            m = nc.lookup_mloc(h)
            nb = 1
            for d in list(h.shape)[1:]:
                nb *= d
            nb *= dsz[h.dtype]
            regs.append([m.addr, m.addr + nb])
        regs.sort()
        merged = []
        for a, b in regs:
            if merged and a <= merged[-1][1]:
                merged[-1][1] = max(merged[-1][1], b)
            else:
                merged.append([a, b])
        self.free = merged

    def at(self, name, shape, dtype):
        import math
        nb = 1
        for d in shape[1:]:
            nb *= d
        nb *= {F32: 4, BF16: 2, U32: 4}[dtype]
        for r in self.free:
            a = (r[0] + 31) & ~31
            if a + nb <= r[1]:
                t = self.nc.alloc_sbuf_tensor_at(name, shape, dtype, offset=a)
                r[0] = a + nb
                return t
        raise RuntimeError(f"arena: no room for {name} ({nb}B); free={self.free}")


AG2_EARLY = False


def build_nc(debug=(), stage=4):
    nc = bass.Bass("TRN2", target_bir_lowering=False, debug=False)
    ctx = Ctx(nc)

    di = lambda n, s, d: nc.dram_tensor(n, s, d, kind="ExternalInput").ap()
    x_fm = di("x_fm", [D, T], F32)
    x_tm = di("x_tm", [T, D], F32)
    wqT = di("wqT", [D, D], F32)
    wkT = di("wkT", [D, D], F32)
    wvT = di("wvT", [D, D], F32)
    wfT = di("wfT", [D, D], F32)
    woT = di("woT", [D, D], F32)
    nw_c = di("nw_c", [128, 8], F32)
    gn_c = di("gn_c", [128, 8], F32)
    wr_in = di("wr", [D, E], F32)
    w1e_in = di("w1e", [D, DFF], F32)
    w2e_in = di("w2e", [DFF, D], F32)
    ident_bf_in = di("ident_bf", [128, 128], BF16)
    ident_f32_in = di("ident_f32", [128, 128], F32)
    utri_bf_in = di("utri_bf", [128, 128], BF16)
    utri_f_in = di("utri_f", [128, 128], F32)
    ustrict_f32_in = di("ustrict_f32", [128, 128], F32)
    iota8_in = di("iota8", [128, 8], F32)
    s160_in = di("s160", [128, 160], F32)
    str8_in = di("str8", [8, 8], F32)
    bmask_in = di("bmask", [8, 1], F32)
    ecol_in = di("ecol", [8, 1], F32)
    rowbase_in = di("rowbase", [1, 8], F32)
    ones1f_in = di("ones1f", [1, 128], F32)
    onescol_bf_in = di("onescol_bf", [128, 1], BF16)
    onescol_f_in = di("onescol_f", [128, 1], F32)
    invd_bf_in = di("invd_bf", [128, 1], BF16)
    epscol_in = di("epscol", [128, 1], F32)

    y2_out = nc.dram_tensor("y2", [T, D], F32, kind="ExternalOutput").ap()

    y1_dram = nc.dram_tensor("y1_dram", [T, D], F32).ap()
    ysort_dram = nc.dram_tensor("ysort_dram", [T, D], BF16).ap()
    cnt_in_d = nc.dram_tensor("cnt_in_d", [1, 8], F32).ap()
    cnt_ag_d = nc.dram_tensor("cnt_ag_d", [8, 8], F32, addr_space="Shared").ap()
    ag1_d = nc.dram_tensor("ag1_d", [B * T, D], BF16, addr_space="Shared").ap()
    eout_d = nc.dram_tensor("eout_d", [CAP, D], BF16).ap()
    ag2a_d = nc.dram_tensor("ag2a_d", [8 * 2048, D], BF16,
                            addr_space="Shared").ap()
    ag2b_d = nc.dram_tensor("ag2b_d", [8 * 512, D], BF16,
                            addr_space="Shared").ap()

    dbg = {}

    def dbg_out(name, shape, dt=F32):
        if name in debug:
            dbg[name] = nc.dram_tensor("dbg_" + name, shape, dt,
                                       kind="ExternalOutput").ap()
            return dbg[name]
        return None

    sb = lambda n, s, d: nc.alloc_sbuf_tensor(n, s, d)
    wr_sb = sb("wr_sb", [128, 8, E], F32)
    nw_sb = sb("nw_sb", [128, 8], F32)
    gn_sb = sb("gn_sb", [128, 8], F32)

    id_bf = sb("id_bf", [128, 128], BF16)
    id_f = sb("id_f", [128, 128], F32)
    utri = sb("utri", [128, 128], BF16)
    utri_f = sb("utri_f_sb", [128, 128], F32)
    ustr = sb("ustr", [128, 128], F32)
    iota8 = sb("iota8_sb", [128, 8], F32)
    s160 = sb("s160_sb", [128, 160], F32)
    str8 = sb("str8_sb", [8, 8], F32)
    bmask = sb("bmask_sb", [8, 1], F32)
    ecol = sb("ecol_sb", [8, 1], F32)
    rowbase = sb("rowbase_sb", [1, 8], F32)
    ones1f = sb("ones1f_sb", [1, 128], F32)
    onescol_bf = sb("onescol_bf_sb", [128, 1], BF16)
    onescol_f = sb("onescol_f_sb", [128, 1], F32)
    invd_bf = sb("invd_bf_sb", [128, 1], BF16)
    epscol = sb("epscol_sb", [128, 1], F32)
    nc.const_aps.register(F32, EPS, epscol[:, 0:1])

    x_bf = sb("x_bf", [128, 8, SC], BF16)
    xsq2 = sb("xsq2", [128, 2, SC], BF16)
    h_sb = sb("h_sb", [128, 8, SC], BF16)
    q_sb = sb("q_sb", [128, 8, SC], BF16)
    k_sb = sb("k_sb", [128, 8, SC], BF16)
    v_sb = sb("v_sb", [128, 4, D], BF16)
    g_sb = sb("g_sb", [128, 4, D], F32)
    sA = sb("sA", [128, D], F32)
    rr1 = sb("rr1", [1, SC], F32)
    rr2 = sb("rr2", [1, SC], F32)

    ghi = sb("ghi", [128, 2, D], BF16)
    glo = sb("glo", [128, 2, D], BF16)
    gc_sb = sb("gc_sb", [128, 2, D], F32)
    gd_sb = sb("gd_sb", [128, 2, D], F32)
    gtotr = sb("gtotr", [1, D], F32)
    g0r = sb("g0r", [1, D], F32)
    egtot = sb("egtot", [1, D], F32)
    _xaddr = nc.lookup_mloc(x_bf).addr
    e1t = nc.alloc_sbuf_tensor_at("e1t", [128, D], F32, offset=_xaddr)
    ept = nc.alloc_sbuf_tensor_at("ept", [128, D], F32, offset=_xaddr + 4096)
    st_sb = sb("st_sb", [128, 2, D], BF16)
    z_sb = sb("z_sb", [128, D], F32)
    p_sb = sb("p_sb", [128, D], F32)
    mx4 = sb("mx4", [128, 4], F32)
    nmx4 = sb("nmx4", [128, 4], F32)
    sum4 = sb("sum4", [128, 4], F32)
    rden = sb("rden", [128, 4], F32)
    p1t = sb("p1t", [128, D], BF16)
    p2t = sb("p2t", [128, D], BF16)
    stf = sb("stf", [128, 16, 128], BF16)
    p1f = sb("p1f", [128, 8, 128], BF16)
    p2f = sb("p2f", [128, 8, 128], BF16)
    ktm = sb("ktm", [128, 16, 128], BF16)
    am_sb = sb("am_sb", [128, 8, 128], BF16)
    bt_sb = sb("bt_sb", [128, 8, 128], BF16)
    hk_sb = sb("hk_sb", [128, 8, 256], F32)
    hv_sb = sb("hv_sb", [128, 8, 256], F32)
    hkb = sb("hkb", [128, 8, 256], BF16)
    hvb = sb("hvb", [128, 8, 256], BF16)
    lamc = sb("lamc", [128, 8], F32)
    o_sb = sb("o_sb", [128, 8, SC], BF16)

    xt_sb = nc.alloc_sbuf_tensor_at("xt_sb", [128, D], F32, offset=_xaddr)
    pre_sb = nc.alloc_sbuf_tensor_at("pre_sb", [128, D], F32, offset=_xaddr + 4096)
    mean1 = sb("mean1", [128, 1], F32)
    var1 = sb("var1", [128, 1], F32)
    mraw = sb("mraw", [128, 1], F32)
    vraw = sb("vraw", [128, 1], F32)
    mraw2 = sb("mraw2", [128, 1], F32)
    vraw2 = sb("vraw2", [128, 1], F32)
    mean1b = sb("mean1b", [128, 1], F32)
    var1b = sb("var1b", [128, 1], F32)
    rstd1b = sb("rstd1b", [128, 1], F32)
    tmp1cb = sb("tmp1cb", [128, 1], F32)
    rstd1 = sb("rstd1", [128, 1], F32)
    y1t_sb = nc.alloc_sbuf_tensor_at("y1t_sb", [128, D], F32,
                                     offset=nc.lookup_mloc(st_sb).addr)
    y1tf_sb = nc.alloc_sbuf_tensor_at("y1tf_sb", [128, 8, 128], F32,
                                      offset=nc.lookup_mloc(stf).addr)
    y1_sb = nc.alloc_sbuf_tensor_at("y1_sb", [128, D], F32,
                                    offset=nc.lookup_mloc(ghi).addr)
    y1f_sb = nc.alloc_sbuf_tensor_at("y1f_sb", [128, 8, 128], F32,
                                     offset=nc.lookup_mloc(glo).addr)
    lg_sb = sb("lg_sb", [128, 8], F32)
    mx8 = sb("mx8", [128, 8], F32)
    idxu = sb("idxu", [128, 8], U32)
    nmx = sb("nmx", [128, 1], F32)
    pex = sb("pex", [128, 8], F32)
    den8 = sb("den8", [128, 1], F32)
    oh_sb = sb("oh_sb", [128, 8], F32)
    posx = sb("posx", [128, 8], F32)
    carry = sb("carry", [1, 8], F32)
    idx_all = sb("idx_all", [128, NT], F32)
    lp_all = sb("lp_all", [128, NT], F32)
    pmax_all = sb("pmax_all", [128, NT], F32)
    tmp8 = sb("tmp8", [128, 8], F32)
    tmp1c = sb("tmp1c", [128, 1], F32)

    PA = nc.alloc_psum_tensor("PA", [128, 512], F32)
    PB = nc.alloc_psum_tensor("PB", [128, 512], F32)
    PC = nc.alloc_psum_tensor("PC", [128, 512], F32)
    PD = nc.alloc_psum_tensor("PD", [128, 512], F32)
    PE_ = nc.alloc_psum_tensor("PE_", [128, 512], F32)
    PF = nc.alloc_psum_tensor("PF", [128, 512], F32)
    PT = nc.alloc_psum_tensor("PT", [128, 1024], BF16)
    PU = nc.alloc_psum_tensor("PU", [128, 512], F32)

    L = Sem(nc, "L")
    G = Sem(nc, "G")
    SQ = Sem(nc, "SQ")
    P = Sem(nc, "P")
    V = Sem(nc, "V")
    S = Sem(nc, "S")
    CC = Sem(nc, "CC")

    def sdma(src_ap, dst_ap):
        return ctx.op("sync", lambda e, o=dst_ap, i=src_ap: e.dma_start(out=o, in_=i),
                      L, 16)

    def gdma(src_ap, dst_ap):
        return ctx.op("gpsimd", lambda e, o=dst_ap, i=src_ap: e.dma_start(out=o, in_=i),
                      G, 16)

    def qdma(src_ap, dst_ap):
        return ctx.op("scalar", lambda e, o=dst_ap, i=src_ap: e.dma_start(out=o, in_=i),
                      SQ, 16)

    def mm(out_ap, lhsT_ap, rhs_ap, start, stop, sem=None):
        return ctx.op("tensor",
                      lambda e, o=out_ap, l=lhsT_ap, r=rhs_ap, st=start, sp=stop:
                      e.matmul(o, l, r, start=st, stop=sp), sem)

    def tr(out_ap, in_ap, ident_ap, sem=None):
        return ctx.op("tensor",
                      lambda e, o=out_ap, i=in_ap, d=ident_ap: e.transpose(o, i, d),
                      sem)

    def ve(build, sem=None):
        return ctx.op("vector", build, sem)

    def vdrain():
        ctx.emit("vector", lambda e: e.drain())

    def sdrain():
        ctx.emit("scalar", lambda e: e.drain())

    def se(build, sem=None):
        return ctx.op("scalar", build, sem)

    def wt(engine, sem, val):
        ctx.wait(engine, sem, val)

    wstate = {"par": 0, "last_use": [0, 0]}

    def wload(src):
        i = wstate["par"]
        wstate["par"] ^= 1
        wt("gpsimd", P, wstate["last_use"][i])
        gv = gdma(src.rearrange("(c p) o -> p c o", p=128), wbufs[i][:, :, :])
        return i, gv

    def wdone(i):
        wstate["last_use"][i] = P.n

    for src, dst in ((ident_bf_in, id_bf), (ident_f32_in, id_f), (utri_bf_in, utri),
                     (utri_f_in, utri_f), (ustrict_f32_in, ustr), (iota8_in, iota8),
                     (s160_in, s160), (str8_in, str8), (bmask_in, bmask),
                     (ecol_in, ecol), (rowbase_in, rowbase), (ones1f_in, ones1f),
                     (onescol_bf_in, onescol_bf), (onescol_f_in, onescol_f),
                     (invd_bf_in, invd_bf), (epscol_in, epscol),
                     (nw_c, nw_sb), (gn_c, gn_sb)):
        sdma(src[:, :], dst[:, :])
    sdma(wr_in.rearrange("(c p) e -> p c e", p=128), wr_sb[:, :, :])
    consts_done = L.n

    warm_d = nc.dram_tensor("warm_d", [1, 8], F32).ap()
    warm_o = nc.dram_tensor("warm_o", [8, 8], F32, addr_space="Shared").ap()
    bcregs = {}

    def _gp_init(e):
        for nm, val in (("t", T - 1), ("bt", B * T - 1), ("ec", E * CAP - 1),
                        ("eca", 8 * 2048 - 1), ("ecb", 8 * 512 - 1)):
            r = e.alloc_register("bc_" + nm)
            e.reg_mov(r, val)
            bcregs[nm] = r
    ctx.emit("gpsimd", _gp_init)
    wt("gpsimd", L, consts_done)
    wrm = gdma(ones1f[0:1, 0:8], warm_d[:, :])
    wt("gpsimd", G, wrm)
    ctx.op("gpsimd", lambda e: e.collective_compute(
        "AllGather", ALU.bypass, replica_groups=[list(range(8))],
        ins=[warm_d[:, :].opt()], outs=[warm_o[:, :].opt()]), CC)

    ve(lambda e: e.memset(hk_sb[:, :, :], 0.0))
    ve(lambda e: e.memset(hv_sb[:, :, :], 0.0))
    ve(lambda e: e.memset(hkb[:, :, :], 0.0))
    ve(lambda e: e.memset(hvb[:, :, :], 0.0))
    init_v = ve(lambda e: e.memset(carry[:, :], 0.0), V)

    h_done_v = 0
    prev_sc_pe = 0
    prev_sc_v = 0

    whalf = [sb("wh" + str(i), [128, 8, 512], BF16) for i in range(4)]
    wst = {"last": [0, 0, 0, 0]}

    def wload_h(src, half, sl):
        wt("gpsimd", P, wst["last"][sl])
        gv = gdma(src[:, half * 512:(half + 1) * 512]
                  .rearrange("(c p) o -> p c o", p=128), whalf[sl][:, :, :])
        return sl, gv

    def wdone_h(sl):
        wst["last"][sl] = P.n

    st_front = {}
    st_scan = {}
    scan_state = {"hv_add_v": init_v, "gl_p": 0, "y1q": 0, "split": {}}

    def emit_front(s):
        t0 = s * SC
        iq = [wload_h(wqT, h2, 2 + h2) for h2 in range(2)]
        prev = st_scan.get(s - 1, {})
        wt("gpsimd", V, prev.get("scan_done_v", 0))
        xld = gdma(x_fm[:, t0:t0 + SC].rearrange("(c p) t -> p c t", p=128),
                   x_bf[:, :, :])
        wt("vector", G, xld)
        wt("vector", L, consts_done)
        sq_mm_p = [0] * 8
        for c in range(8):
            if c >= 2:
                wt("vector", P, sq_mm_p[c - 2])
            vsq = ve(lambda e, c=c: e.tensor_mul(xsq2[:, c % 2, :], x_bf[:, c, :],
                                                 x_bf[:, c, :]), V)
            wt("tensor", V, vsq)
            sq_mm_p[c] = mm(PF[0:1, 0:512], invd_bf[:, :], xsq2[:, c % 2, :],
                            c == 0, c == 7, P)
        wt("scalar", P, P.n)
        se(lambda e: e.activation(out=rr1[:, :], in_=PF[0:1, 0:512], func=AF.Ln,
                                  bias=EPS))
        sdrain()
        sr = se(lambda e: e.activation(out=rr2[:, :], in_=rr1[:, :], func=AF.Exp,
                                       scale=-0.5), S)
        wt("tensor", S, sr)
        bc = mm(PE_[:, :], ones1f[:, :], rr2[:, :], True, True, P)
        wt("vector", P, bc)
        wt("vector", V, init_v)
        hv_ = 0
        for c in range(8):
            hv_ = ve(lambda e, c=c: e.scalar_tensor_tensor(
                h_sb[:, c, :], x_bf[:, c, :], nw_sb[:, c:c + 1], PE_[:, :],
                ALU.mult, ALU.mult), V)
        h_done = hv_

        wt("tensor", V, h_done)
        evac_v = []

        def fm_proj_h(slots, dst):
            for h2 in range(2):
                sl, gw = slots[h2]
                wt("tensor", G, gw)
                for dc in range(4):
                    pb = (PC, PD, PE_)[len(evac_v) % 3]
                    if len(evac_v) >= 3:
                        wt("tensor", V, evac_v[-3])
                    for kc in range(8):
                        mm(pb[:, :], whalf[sl][:, kc, dc * 128:(dc + 1) * 128],
                           h_sb[:, kc, :], kc == 0, kc == 7,
                           P if kc == 7 else None)
                    wt("vector", P, P.n)
                    evac_v.append(ve(lambda e, d=dst, dc=dc, h2=h2, pb=pb:
                                     e.tensor_copy(d[:, h2 * 4 + dc, :], pb[:, :]),
                                     V))
                wdone_h(sl)

        def tm_proj_h(slots, is_g):
            for h2 in range(2):
                sl, gw = slots[h2]
                wt("tensor", G, gw)
                for ts in range(4):
                    pb = (PC, PD, PE_)[len(evac_v) % 3]
                    if len(evac_v) >= 3:
                        wt("tensor", V, evac_v[-3])
                    for kc in range(8):
                        mm(pb[:, :], h_sb[:, kc, ts * 128:(ts + 1) * 128],
                           whalf[sl][:, kc, :], kc == 0, kc == 7,
                           P if kc == 7 else None)
                    wt("vector", P, P.n)
                    if is_g:
                        evac_v.append(ve(lambda e, ts=ts, h2=h2, pb=pb:
                                         e.tensor_copy(
                                             g_sb[:, ts, h2 * 512:(h2 + 1) * 512],
                                             pb[:, :]), V))
                    else:
                        evac_v.append(ve(lambda e, ts=ts, h2=h2, pb=pb:
                                         e.tensor_copy(
                                             v_sb[:, ts, h2 * 512:(h2 + 1) * 512],
                                             pb[:, :]), V))
                wdone_h(sl)

        fm_proj_h(iq, q_sb)
        q_done = evac_v[-1]
        wt("scalar", V, q_done)
        se(lambda e: e.activation(out=q_sb[:, :, :], in_=q_sb[:, :, :],
                                  func=AF.Silu))
        ik = [wload_h(wkT, h2, 2 + h2) for h2 in range(2)]
        fm_proj_h(ik, k_sb)
        k_done = evac_v[-1]
        wt("scalar", V, k_done)
        silu_qk = se(lambda e: e.activation(out=k_sb[:, :, :], in_=k_sb[:, :, :],
                                            func=AF.Silu), S)
        if_ = [wload_h(wfT, h2, 2 + h2) for h2 in range(2)]
        tm_proj_h(if_, True)
        g_done = evac_v[-1]
        iv = [wload_h(wvT, h2, 2 + h2) for h2 in range(2)]
        scr = ((sA, z_sb), (p_sb, gd_sb[:, 0, :]))
        lsig_v = [0, 0, 0, 0]
        for ts in range(4):
            a_s, b_s = scr[ts % 2]
            wt("scalar", V, g_done if ts < 2 else lsig_v[ts - 2])
            se(lambda e, ts=ts, a_s=a_s: e.activation(
                out=a_s[:, :], in_=g_sb[:, ts, :], func=AF.Abs))
            sdrain()
            se(lambda e, a_s=a_s: e.activation(out=a_s[:, :], in_=a_s[:, :],
                                               func=AF.Exp, scale=-1.0))
            sdrain()
            v3 = se(lambda e, a_s=a_s: e.activation(out=a_s[:, :], in_=a_s[:, :],
                                                    func=AF.Ln, bias=1.0), S)
            wt("vector", S, v3)
            ve(lambda e, ts=ts, b_s=b_s: e.tensor_scalar_min(
                b_s[:, :], g_sb[:, ts, :], 0.0))
            lsig_v[ts] = ve(lambda e, ts=ts, a_s=a_s, b_s=b_s: e.tensor_sub(
                g_sb[:, ts, :], b_s[:, :], a_s[:, :]), V)
        gates_done = lsig_v[3]
        st_front[s] = dict(gates_done=gates_done)
        emit_split(s, 0)
        tm_proj_h(iv, False)
        projs_done = evac_v[-1]
        wt("scalar", V, projs_done)
        silu_done = se(lambda e: e.activation(out=v_sb[:, :, :], in_=v_sb[:, :, :],
                                              func=AF.Silu), S)
        st_front[s] = dict(silu_done=silu_done, silu_qk=silu_qk,
                           gates_done=gates_done, sr=sr, h_done=h_done)

    def emit_split(s, ci):
        gates_done = st_front[s]["gates_done"]
        ts0, ts1 = 2 * ci, 2 * ci + 1
        wt("scalar", V, gates_done)
        wt("scalar", P, scan_state["gl_p"])
        se(lambda e, ts=ts0: e.copy(out=ghi[:, 0, :], in_=g_sb[:, ts, :]))
        ch_s = se(lambda e, ts=ts1: e.copy(out=ghi[:, 1, :],
                                           in_=g_sb[:, ts, :]), S)
        wt("vector", S, ch_s)
        ve(lambda e, ts=ts0: e.tensor_sub(p_sb[:, :], g_sb[:, ts, :],
                                          ghi[:, 0, :]))
        sub_v = ve(lambda e, ts=ts1: e.tensor_sub(z_sb[:, :], g_sb[:, ts, :],
                                                  ghi[:, 1, :]), V)
        wt("scalar", V, sub_v)
        se(lambda e: e.copy(out=glo[:, 0, :], in_=p_sb[:, :]))
        cl_s = se(lambda e: e.copy(out=glo[:, 1, :], in_=z_sb[:, :]), S)
        scan_state["split"][(s, ci)] = (sub_v, cl_s)

    def emit_scan(s):
        fr = st_front[s]
        silu_done, gates_done = fr["silu_done"], fr["gates_done"]
        silu_qk = fr["silu_qk"]
        prev_chunk_pe = 0
        scan_done_v = gates_done
        for ci in range(NCHS):
            ts0, ts1 = 2 * ci, 2 * ci + 1
            wt("vector", P, prev_chunk_pe)
            wt("vector", V, gates_done)
            if (s, ci) not in scan_state["split"]:
                emit_split(s, ci)
            sub_v, cl_s = scan_state["split"][(s, ci)]
            wt("tensor", V, sub_v)
            wt("tensor", S, cl_s)
            for n in range(2):
                pb = (PA, PB)[n]
                mm(pb[:, :], utri[:, :], ghi[:, 0, n * 512:(n + 1) * 512], True,
                   False)
                mm(pb[:, :], utri[:, :], glo[:, 0, n * 512:(n + 1) * 512], False,
                   True)
            for n in range(2):
                pb = (PF, PU)[n]
                mm(pb[0:1, 0:512], onescol_bf[:, :],
                   ghi[:, 0, n * 512:(n + 1) * 512], True, False)
                mm(pb[0:1, 0:512], onescol_bf[:, :],
                   glo[:, 0, n * 512:(n + 1) * 512], False, True,
                   P if n == 1 else None)
            g0_p = P.n
            wt("vector", P, g0_p)
            ve(lambda e: e.tensor_copy(gc_sb[:, 0, 0:512], PA[:, :]))
            ve(lambda e: e.tensor_copy(gc_sb[:, 0, 512:1024], PB[:, :]))
            ve(lambda e: e.tensor_copy(g0r[0:1, 0:512], PF[0:1, 0:512]))
            g0_v = ve(lambda e: e.tensor_copy(g0r[0:1, 512:1024], PU[0:1, 0:512]),
                      V)
            wt("tensor", V, g0_v)
            for n in range(2):
                pb = (PC, PD)[n]
                mm(pb[:, :], utri[:, :], ghi[:, 1, n * 512:(n + 1) * 512], True,
                   False)
                mm(pb[:, :], utri[:, :], glo[:, 1, n * 512:(n + 1) * 512], False,
                   False)
                mm(pb[:, :], ones1f[:, :], g0r[0:1, n * 512:(n + 1) * 512], False,
                   True)
            for n in range(2):
                pb = (PF, PU)[n]
                mm(pb[0:1, 0:512], onescol_bf[:, :],
                   ghi[:, 1, n * 512:(n + 1) * 512], False, False)
                mm(pb[0:1, 0:512], onescol_bf[:, :],
                   glo[:, 1, n * 512:(n + 1) * 512], False, True,
                   P if n == 1 else None)
            gt_p = P.n
            wt("vector", P, gt_p)
            ve(lambda e: e.tensor_copy(gc_sb[:, 1, 0:512], PC[:, :]))
            ve(lambda e: e.tensor_copy(gc_sb[:, 1, 512:1024], PD[:, :]))
            ve(lambda e: e.tensor_copy(gtotr[0:1, 0:512], PF[0:1, 0:512]))
            gt_v = ve(lambda e: e.tensor_copy(gtotr[0:1, 512:1024],
                                              PU[0:1, 0:512]), V)
            wt("scalar", V, gt_v)
            eg_s = se(lambda e: e.activation(out=egtot[:, :], in_=gtotr[:, :],
                                             func=AF.Exp, scale=SC16), S)
            wt("tensor", V, gt_v)
            for mc in range(8):
                mm(PU[:, 8 + mc:9 + mc], ghi[:, 0, mc * 128:(mc + 1) * 128],
                   onescol_bf[:, :], True, False)
                mm(PU[:, 8 + mc:9 + mc], glo[:, 0, mc * 128:(mc + 1) * 128],
                   onescol_bf[:, :], False, False)
                mm(PU[:, 8 + mc:9 + mc], ghi[:, 1, mc * 128:(mc + 1) * 128],
                   onescol_bf[:, :], False, False)
                mm(PU[:, 8 + mc:9 + mc], glo[:, 1, mc * 128:(mc + 1) * 128],
                   onescol_bf[:, :], False, True, P if mc == 7 else None)
            lamc_p = P.n
            scan_state["gl_p"] = lamc_p
            wt("scalar", P, lamc_p)
            lamc_s = se(lambda e: e.activation(out=lamc[:, :], in_=PU[:, 8:16],
                                               func=AF.Exp, scale=SC16), S)
            sdrain()
            mm(PE_[:, :], ones1f[:, :], gtotr[0:1, 0:512], True, True)
            mm(PF[:, :], ones1f[:, :], gtotr[0:1, 512:1024], True, True, P)
            bcg_p = P.n

            def tr_grp(grp):
                for sl in range(8):
                    gi = grp * 8 + sl
                    if gi < 16:
                        i, msl = gi // 8, gi % 8
                        tr(PT[:, sl * 128:(sl + 1) * 128],
                           st_sb[:, i, msl * 128:(msl + 1) * 128], id_bf[:, :],
                           P if sl == 7 else None)
                    else:
                        i, dsl = (gi - 16) // 8, (gi - 16) % 8
                        ts = 2 * ci + i
                        tr(PT[:, sl * 128:(sl + 1) * 128],
                           k_sb[:, dsl, ts * 128:ts * 128 + 128], id_bf[:, :],
                           P if sl == 7 else None)
                wt("vector", P, P.n)
                if grp < 2:
                    return ve(lambda e, grp=grp: e.tensor_copy(
                        stf[:, grp * 8:(grp + 1) * 8, :]
                        .rearrange("p a b -> p (a b)"), PT[:, :]), V)
                return ve(lambda e, grp=grp: e.tensor_copy(
                    ktm[:, (grp - 2) * 8:(grp - 1) * 8, :]
                    .rearrange("p a b -> p (a b)"), PT[:, :]), V)

            wt("tensor", S, silu_qk)
            ktv2 = tr_grp(2)
            wt("tensor", V, ktv2)
            ktv = tr_grp(3)
            wt("tensor", S, eg_s)
            mm(PC[:, :], ones1f[:, :], egtot[0:1, 0:512], True, True)
            mm(PD[:, :], ones1f[:, :], egtot[0:1, 512:1024], True, True, P)
            bce_p = P.n
            wt("vector", SQ, scan_state["y1q"])
            st_vs = [0, 0]
            for i, ts in enumerate((ts0, ts1)):
                wt("vector", P, bcg_p)
                ve(lambda e, i=i: e.tensor_tensor(gd_sb[:, i, 0:512],
                                                  gc_sb[:, i, 0:512],
                                                  PE_[:, :], ALU.subtract))
                gd_v = ve(lambda e, i=i: e.tensor_tensor(gd_sb[:, i, 512:1024],
                                                         gc_sb[:, i, 512:1024],
                                                         PF[:, :], ALU.subtract),
                          V)
                wt("scalar", V, gd_v)
                se(lambda e, ts=ts: e.activation(out=p_sb[:, :],
                                                 in_=g_sb[:, ts, :],
                                                 func=AF.Exp, scale=SC16))
                es_s = se(lambda e, i=i: e.activation(out=e1t[:, :],
                                                      in_=gd_sb[:, i, :],
                                                      func=AF.Exp, scale=-SC16),
                          S)
                wt("vector", S, es_s)
                ve(lambda e: e.tensor_scalar(z_sb[:, :], p_sb[:, :], -1.0, 1.0,
                                             ALU.mult, ALU.add))
                st_vs[i] = ve(lambda e, i=i: e.tensor_tensor(st_sb[:, i, :],
                                                             z_sb[:, :],
                                                             e1t[:, :],
                                                             ALU.mult), V)
            st_v = st_vs[1]
            se(lambda e: e.activation(out=gd_sb[:, 0, :], in_=gd_sb[:, 0, :],
                                      func=AF.Exp, scale=SC16))
            ept_s = se(lambda e: e.activation(out=gd_sb[:, 1, :],
                                              in_=gd_sb[:, 1, :],
                                              func=AF.Exp, scale=SC16), S)

            twins = (slice(ts0 * 128, ts0 * 128 + 128),
                     slice(ts1 * 128, ts1 * 128 + 128))
            epts = (gd_sb[:, 0, :], gd_sb[:, 1, :])
            a_slots = ((PU, PU), (PE_, PF))
            z_slots = ((PA, PB), (PC, PD))

            def a_mms(i):
                twin = twins[i]
                for jj in range(i + 1):
                    pslot = a_slots[i][jj]
                    for hh in range(H):
                        mm(pslot[:, hh * 128:hh * 128 + 128],
                           k_sb[:, 2 * hh, 2 * ci * 128 + jj * 128:
                                2 * ci * 128 + jj * 128 + 128],
                           q_sb[:, 2 * hh, twin], True, False)
                        mm(pslot[:, hh * 128:hh * 128 + 128],
                           k_sb[:, 2 * hh + 1, 2 * ci * 128 + jj * 128:
                                2 * ci * 128 + jj * 128 + 128],
                           q_sb[:, 2 * hh + 1, twin], False, True,
                           P if hh == H - 1 else None)
                return P.n

            def mask_evac(i, dst_sb, slots):
                e_s = 0
                for jj in range(i + 1):
                    pslot = slots[i][jj]
                    dst3 = dst_sb[:, jj * 4:jj * 4 + 4, :]
                    src3 = pslot[:, :].rearrange("p (h t) -> p h t", h=H)
                    if jj == i:
                        ve(lambda e, dst3=dst3, src3=src3: e.tensor_tensor(
                            dst3, src3,
                            utri_f[:, :].unsqueeze(1)
                            .broadcast_to([128, H, 128]), ALU.mult), V)
                    else:
                        e_s = se(lambda e, dst3=dst3, src3=src3:
                                 e.copy(out=dst3, in_=src3), S)
                return V.n, e_s

            def z_mms(i):
                twin = twins[i]
                for hh in range(H):
                    pz = z_slots[i][hh // 2]
                    col = (hh % 2) * 256
                    for jj in range(i + 1):
                        slot = hh if i == 0 else jj * 4 + hh
                        mm(pz[:, col:col + 256], am_sb[:, slot, :],
                           st_sb[:, jj, hh * 256:(hh + 1) * 256], jj == 0,
                           False)
                for hh in range(H):
                    pz = z_slots[i][hh // 2]
                    col = (hh % 2) * 256
                    mm(pz[:, col:col + 256], q_sb[:, 2 * hh, twin],
                       hkb[:, 2 * hh, :], False, False)
                    mm(pz[:, col:col + 256], q_sb[:, 2 * hh + 1, twin],
                       hkb[:, 2 * hh + 1, :], False, True,
                       P if hh == H - 1 else None)
                return P.n

            def sm_max(i, z_p):
                pz0, pz1 = z_slots[i]
                wt("vector", P, z_p)
                ve(lambda e, i=i, pz0=pz0: e.tensor_tensor(
                    z_sb[:, 0:512], epts[i][:, 0:512], pz0[:, :], ALU.mult))
                ve(lambda e, i=i, pz1=pz1: e.tensor_tensor(
                    z_sb[:, 512:1024], epts[i][:, 512:1024], pz1[:, :],
                    ALU.mult), V)
                ve(lambda e: e.reduce_max(
                    mx4[:, :], z_sb[:, :].rearrange("p (h m) -> p h m", h=H),
                    axis=AX.X))
                vdrain()
                return ve(lambda e: e.tensor_scalar_mul(nmx4[:, :], mx4[:, :],
                                                        -1.0), V)

            def sm_exp(nm_v):
                wt("scalar", V, nm_v)
                for hh2 in range(H):
                    se(lambda e, hh2=hh2: e.activation(
                        out=p_sb[:, hh2 * 256:(hh2 + 1) * 256],
                        in_=z_sb[:, hh2 * 256:(hh2 + 1) * 256], func=AF.Exp,
                        bias=nmx4[:, hh2:hh2 + 1],
                        accum_out=sum4[:, hh2:hh2 + 1]),
                       S if hh2 == H - 1 else None)
                return S.n

            def p2_chain(i, px_s):
                wt("vector", S, px_s)
                ve(lambda e: e.reciprocal(rden[:, :], sum4[:, :]))
                vdrain()
                ve(lambda e, i=i: e.tensor_tensor(z_sb[:, :], p_sb[:, :],
                                                  epts[i][:, :], ALU.mult))
                return ve(lambda e: e.tensor_tensor(
                    p2t[:, :].rearrange("p (h m) -> p h m", h=H),
                    z_sb[:, :].rearrange("p (h m) -> p h m", h=H),
                    rden[:, :].unsqueeze(2).broadcast_to([128, H, 256]),
                    ALU.mult), V)

            def p2_tr(pt_v):
                wt("tensor", V, pt_v)
                for sl in range(8):
                    tr(PT[:, sl * 128:(sl + 1) * 128],
                       p2t[:, sl * 128:(sl + 1) * 128], id_bf[:, :],
                       P if sl == 7 else None)
                wt("vector", P, P.n)
                return ve(lambda e: e.tensor_copy(
                    p2f[:, :, :].rearrange("p a b -> p (a b)"), PT[:, :]), V)

            def b_mms(i, tv2):
                wt("tensor", V, tv2)
                for jj in range(i + 1):
                    pslot = a_slots[i][jj]
                    for hh in range(H):
                        mm(pslot[:, hh * 128:hh * 128 + 128],
                           stf[:, jj * 8 + 2 * hh, :], p2f[:, 2 * hh, :],
                           True, False)
                        mm(pslot[:, hh * 128:hh * 128 + 128],
                           stf[:, jj * 8 + 2 * hh + 1, :],
                           p2f[:, 2 * hh + 1, :], False, True,
                           P if hh == H - 1 else None)
                return P.n

            def o_mms(i, bt_v, bt_s):
                wt("tensor", V, bt_v)
                if bt_s:
                    wt("tensor", S, bt_s)
                if i == 0:
                    wt("tensor", S, hvb_s)
                    wt("tensor", S, silu_done)
                for hh in range(H):
                    po = z_slots[i][hh // 2]
                    for vc in range(2):
                        col = (hh % 2) * 256 + vc * 128
                        mm(po[:, col:col + 128],
                           hvb[:, 2 * hh, vc * 128:vc * 128 + 128],
                           p2f[:, 2 * hh, :], True, False)
                        mm(po[:, col:col + 128],
                           hvb[:, 2 * hh + 1, vc * 128:vc * 128 + 128],
                           p2f[:, 2 * hh + 1, :], False, False)
                        for jj in range(i + 1):
                            slot = hh if i == 0 else jj * 4 + hh
                            mm(po[:, col:col + 128],
                               v_sb[:, 2 * ci + jj,
                                    hh * 256 + vc * 128:hh * 256 + vc * 128
                                    + 128],
                               bt_sb[:, slot, :], False, jj == i,
                               P if (hh == H - 1 and vc == 1 and jj == i)
                               else None)
                return P.n

            def o_evac(i, o_p):
                ts = (ts0, ts1)[i]
                po0, po1 = z_slots[i]
                wt("vector", P, o_p)
                wt("scalar", P, o_p)
                ov = ve(lambda e, ts=ts, po0=po0: e.tensor_copy(
                    o_sb[:, 0:4, ts * 128:ts * 128 + 128],
                    po0[:, :].rearrange("p (a t) -> p a t", a=4)), V)
                os_ = se(lambda e, ts=ts, po1=po1: e.copy(
                    out=o_sb[:, 4:8, ts * 128:ts * 128 + 128],
                    in_=po1[:, :].rearrange("p (a t) -> p a t", a=4)), S)
                return ov, os_

            wt("tensor", S, lamc_s)
            a0_p = a_mms(0)
            wt("tensor", V, st_vs[0])
            wt("tensor", V, ktv)
            stv0 = tr_grp(0)
            wt("vector", P, a0_p)
            am0_v, _ = mask_evac(0, am_sb, a_slots)
            wt("vector", P, bce_p)
            for hh in range(H):
                pb = (PC, PD)[hh // 2]
                col = (hh % 2) * 256
                ve(lambda e, hh=hh, pb=pb, col=col: e.tensor_tensor(
                    hk_sb[:, 2 * hh:2 * hh + 2, :],
                    hk_sb[:, 2 * hh:2 * hh + 2, :],
                    pb[:, col:col + 256].unsqueeze(1).broadcast_to([128, 2, 256]),
                    ALU.mult), V if hh == H - 1 else None)
            hkscale_v = V.n
            wt("scalar", V, hkscale_v)
            hkb_s = se(lambda e: e.copy(out=hkb[:, :, :], in_=hk_sb[:, :, :]), S)
            wt("scalar", V, scan_state["hv_add_v"])
            for sl in range(8):
                se(lambda e, sl=sl: e.activation(
                    out=hv_sb[:, sl, :], in_=hv_sb[:, sl, :], func=AF.Identity,
                    scale=lamc[:, sl:sl + 1]))
            hvb_s = se(lambda e: e.copy(out=hvb[:, :, :], in_=hv_sb[:, :, :]), S)
            wt("tensor", V, st_vs[1])
            wt("tensor", V, stv0)
            tr_grp(1)
            wt("tensor", V, am0_v)
            wt("tensor", S, hkb_s)
            z0_p = z_mms(0)
            a1_p = a_mms(1)
            wt("vector", S, ept_s)
            nm0 = sm_max(0, z0_p)
            wt("vector", P, a1_p)
            wt("scalar", P, a1_p)
            am1_v, am1_s = mask_evac(1, am_sb, a_slots)
            px0 = sm_exp(nm0)
            wt("tensor", V, am1_v)
            wt("tensor", S, am1_s)
            z1_p = z_mms(1)
            pt0 = p2_chain(0, px0)
            tv20 = p2_tr(pt0)
            nm1 = sm_max(1, z1_p)
            px1 = sm_exp(nm1)
            b0_p = b_mms(0, tv20)
            wt("vector", P, b0_p)
            bt0_v, _ = mask_evac(0, bt_sb, a_slots)
            o0_p = o_mms(0, bt0_v, 0)
            o_v, o_s = o_evac(0, o0_p)
            pt1 = p2_chain(1, px1)
            tv21 = p2_tr(pt1)
            b1_p = b_mms(1, tv21)
            wt("vector", P, b1_p)
            wt("scalar", P, b1_p)
            bt1_v, bt1_s = mask_evac(1, bt_sb, a_slots)
            o1_p = o_mms(1, bt1_v, bt1_s)
            o_v, o_s = o_evac(1, o1_p)

            wt("tensor", V, o_v)
            wt("tensor", S, o_s)
            last_chunk = (s == NSC - 1 and ci == NCHS - 1)
            if not last_chunk:
                for hh in range(H):
                    phk = (PA, PB, PC, PD)[hh]
                    for dc in range(2):
                        for jj in range(2):
                            mm(phk[:, dc * 256:dc * 256 + 256],
                               ktm[:, jj * 8 + 2 * hh + dc, :],
                               st_sb[:, jj, hh * 256:(hh + 1) * 256], jj == 0,
                               jj == 1,
                               P if (hh == H - 1 and dc == 1 and jj == 1)
                               else None)
                hk_p = P.n
                wt("vector", P, hk_p)
                for hh in range(H):
                    phk = (PA, PB, PC, PD)[hh]
                    ve(lambda e, hh=hh, phk=phk: e.tensor_tensor(
                        hk_sb[:, 2 * hh:2 * hh + 2, :]
                        .rearrange("p a b -> p (a b)"),
                        hk_sb[:, 2 * hh:2 * hh + 2, :]
                        .rearrange("p a b -> p (a b)"),
                        phk[:, :], ALU.add), V if hh == H - 1 else None)
                hk_v = V.n
                wt("tensor", V, hk_v)
                for hh in range(H):
                    phv = (PA, PB, PC, PD)[hh]
                    for mc in range(2):
                        for jj in range(2):
                            mm(phv[:, mc * 256:mc * 256 + 256],
                               st_sb[:, jj,
                                     hh * 256 + mc * 128:hh * 256 + mc * 128
                                     + 128],
                               v_sb[:, 2 * ci + jj, hh * 256:(hh + 1) * 256],
                               jj == 0, jj == 1,
                               P if (hh == H - 1 and mc == 1 and jj == 1)
                               else None)
                hv_p = P.n
                wt("vector", P, hv_p)
                wt("vector", S, hvb_s)
                for hh in range(H):
                    phv = (PA, PB, PC, PD)[hh]
                    ve(lambda e, hh=hh, phv=phv: e.tensor_tensor(
                        hv_sb[:, 2 * hh:2 * hh + 2, :]
                        .rearrange("p a b -> p (a b)"),
                        hv_sb[:, 2 * hh:2 * hh + 2, :]
                        .rearrange("p a b -> p (a b)"),
                        phv[:, :], ALU.add), V if hh == H - 1 else None)
                hv_v = V.n
                scan_state["hv_add_v"] = hv_v
                prev_chunk_pe = P.n
                scan_done_v = hv_v
            else:
                prev_chunk_pe = P.n
                scan_done_v = o_v
        wt("scalar", V, scan_done_v)
        sil_c = [0] * 8
        for c in range(8):
            sil_c[c] = se(lambda e, c=c: e.activation(
                out=o_sb[:, c, :], in_=o_sb[:, c, :], func=AF.Silu), S)
        osilu_s = sil_c[7]
        sq_mm_p = [0] * 8
        for c in range(8):
            wt("vector", S, sil_c[c])
            if c >= 2:
                wt("vector", P, sq_mm_p[c - 2])
            vsq2 = ve(lambda e, c=c: e.tensor_mul(xsq2[:, c % 2, :], o_sb[:, c, :],
                                                  o_sb[:, c, :]), V)
            wt("tensor", V, vsq2)
            sq_mm_p[c] = mm(PF[0:1, 0:512], invd_bf[:, :], xsq2[:, c % 2, :],
                            c == 0, c == 7, P)
        wt("scalar", P, P.n)
        se(lambda e: e.activation(out=rr1[:, :], in_=PF[0:1, 0:512], func=AF.Ln,
                                  bias=EPS))
        sdrain()
        sr2 = se(lambda e: e.activation(out=rr2[:, :], in_=rr1[:, :], func=AF.Exp,
                                        scale=-0.5), S)
        wt("tensor", S, sr2)
        bc2_ = mm(PE_[:, :], ones1f[:, :], rr2[:, :], True, True, P)
        wt("vector", P, bc2_)
        onv = 0
        for c in range(8):
            onv = ve(lambda e, c=c: e.scalar_tensor_tensor(
                o_sb[:, c, :], o_sb[:, c, :], gn_sb[:, c:c + 1], PE_[:, :],
                ALU.mult, ALU.mult), V)
        st_scan[s] = dict(scan_done_v=scan_done_v, on_v=onv)

    def emit_tail(s, wo_g):
        fr, sc_ = st_front[s], st_scan[s]
        wt("tensor", V, sc_["on_v"])
        wt("tensor", G, wo_g)
        if s + 1 in st_front:
            wt("tensor", S, st_front[s + 1]["sr"])
            wt("sync", V, st_front[s + 1]["h_done"])
        wt("sync", V, sc_["scan_done_v"])
        pre_done_v = 0
        pos_s_prev = 0
        for ts in range(4):
            tile = s * 4 + ts
            for nt in range(2):
                pb = (PA, PB)[nt]
                for kc in range(8):
                    mm(pb[:, :], o_sb[:, kc, ts * 128:(ts + 1) * 128],
                       whalf[nt][:, kc, :], kc == 0, kc == 7,
                       P if (kc == 7 and nt == 1) else None)
            at_p = P.n
            if ts > 0:
                wt("sync", V, pre_done_v)
            xl = sdma(x_tm[tile * 128:(tile + 1) * 128, :], xt_sb[:, :])
            wt("vector", P, at_p)
            wt("vector", L, xl)
            ve(lambda e: e.tensor_tensor(pre_sb[:, 0:512], xt_sb[:, 0:512],
                                         PA[:, :], ALU.add))
            pre_done_v = ve(lambda e: e.tensor_tensor(pre_sb[:, 512:1024],
                                                      xt_sb[:, 512:1024],
                                                      PB[:, :], ALU.add), V)
            wt("scalar", V, pre_done_v)
            se(lambda e: e.activation(out=sA[:, :], in_=pre_sb[:, :],
                                      func=AF.Identity, accum_out=mraw[:, :]))
            se(lambda e: e.activation(out=sA[:, :], in_=pre_sb[:, :],
                                      func=AF.Square, accum_out=vraw[:, :]))
            sdrain()
            se(lambda e: e.activation(out=tmp1c[:, :], in_=mraw[:, :],
                                      func=AF.Square, scale=1.0 / D))
            mn_s = se(lambda e: e.activation(out=mean1[:, :], in_=mraw[:, :],
                                             func=AF.Identity, scale=1.0 / D), S)
            wt("vector", S, mn_s)
            v2 = ve(lambda e: e.tensor_scalar(var1[:, :], vraw[:, :], 1.0 / D,
                                              tmp1c[:, :], ALU.mult,
                                              ALU.subtract), V)
            wt("scalar", V, v2)
            se(lambda e: e.activation(out=rstd1[:, :], in_=var1[:, :], func=AF.Ln,
                                      bias=EPS))
            sdrain()
            rs_s = se(lambda e: e.activation(out=rstd1[:, :], in_=rstd1[:, :],
                                             func=AF.Exp, scale=-0.5), S)
            wt("vector", S, rs_s)
            if ts > 0 or s > 0:
                wt("vector", SQ, SQ.n)
            y1v = ve(lambda e: e.tensor_scalar(
                y1t_sb[:, :], pre_sb[:, :], mean1[:, :], rstd1[:, :],
                ALU.subtract, ALU.mult), V)
            wt("scalar", V, y1v)
            qdma(y1t_sb[:, :], y1_dram[tile * 128:(tile + 1) * 128, :])
            wt("tensor", V, y1v)
            for half in range(2):
                for sl in range(4):
                    tr(PU[:, sl * 128:(sl + 1) * 128],
                       y1t_sb[:, half * 512 + sl * 128:half * 512 + (sl + 1)
                             * 128], id_f[:, :], P if sl == 3 else None)
                if half == 0:
                    wt("scalar", P, P.n)
                    tvh_s = se(lambda e: e.copy(
                        out=y1tf_sb[:, 0:4, :].rearrange("p a b -> p (a b)"),
                        in_=PU[:, :]), S)
                    wt("tensor", S, tvh_s)
                else:
                    wt("vector", P, P.n)
                    tvh = ve(lambda e: e.tensor_copy(
                        y1tf_sb[:, 4:8, :].rearrange("p a b -> p (a b)"),
                        PU[:, :]), V)
                    wt("tensor", V, tvh)
            if pos_s_prev:
                wt("tensor", S, pos_s_prev)
            for kc in range(8):
                mm(PF[:, 0:8], y1tf_sb[:, kc, :], wr_sb[:, kc, :],
                   kc == 0, kc == 7, P if kc == 7 else None)
            lg_p = P.n
            wt("scalar", P, lg_p)
            lg_s = se(lambda e: e.copy(out=lg_sb[:, :], in_=PF[:, 0:8]), S)
            wt("vector", S, lg_s)
            ve(lambda e: e.max(mx8[:, :], lg_sb[:, :]))
            vdrain()
            ve(lambda e: e.max_index(idxu[:, :], mx8[:, :], lg_sb[:, :]))
            vdrain()
            ve(lambda e, tile=tile: e.tensor_copy(idx_all[:, tile:tile + 1],
                                                  idxu[:, 0:1]))
            nm_v = ve(lambda e: e.tensor_scalar_mul(nmx[:, :], mx8[:, 0:1], -1.0),
                      V)
            wt("scalar", V, nm_v)
            px_s = se(lambda e: e.activation(out=pex[:, :], in_=lg_sb[:, :],
                                             func=AF.Exp, bias=nmx[:, :],
                                             accum_out=den8[:, :]), S)
            wt("vector", S, px_s)
            ve(lambda e, tile=tile: e.reciprocal(pmax_all[:, tile:tile + 1],
                                                 den8[:, :]))
            vdrain()
            oh_v = ve(lambda e, tile=tile: e.tensor_scalar(
                oh_sb[:, :], iota8[:, :], idx_all[:, tile:tile + 1], None,
                ALU.is_equal), V)
            wt("tensor", V, oh_v)
            mm(PF[:, 8:16], ustr[:, :], oh_sb[:, :], True, False)
            mm(PF[:, 8:16], ones1f[:, :], carry[0:1, :], False, True)
            cs_p = mm(PF[0:1, 16:24], onescol_f[:, :], oh_sb[:, :], True, True,
                      P)
            wt("scalar", P, cs_p)
            pos_s_prev = se(lambda e: e.copy(out=posx[:, :], in_=PF[:, 8:16]), S)
            wt("vector", S, pos_s_prev)
            wt("vector", P, cs_p)
            ve(lambda e: e.tensor_mul(tmp8[:, :], oh_sb[:, :], posx[:, :]))
            vdrain()
            ve(lambda e, tile=tile: e.reduce_sum(lp_all[:, tile:tile + 1],
                                                 tmp8[:, :], axis=AX.X))
            vdrain()
            cu_v = ve(lambda e: e.tensor_tensor(carry[0:1, :], carry[0:1, :],
                                                PF[0:1, 16:24], ALU.add), V)
            wt("tensor", V, cu_v)
        scan_state["y1q"] = SQ.n
        wdone_h(0)
        wdone_h(1)
        if s + 1 < NSC:
            return max(wload_h(woT, h2, h2)[1] for h2 in range(2))
        return 0

    wo_g = max(wload_h(woT, h2, h2)[1] for h2 in range(2))
    emit_front(0)
    for s in range(NSC):
        emit_scan(s)
        if s + 1 < NSC:
            emit_front(s + 1)
        wo_g = emit_tail(s, wo_g)
    prev_sc_v = V.n
    prev_sc_pe = P.n

    def dump(ap, name, shape, dt=F32):
        d = dbg_out(name, shape, dt)
        if d is not None:
            wt("gpsimd", V, V.n)
            wt("gpsimd", S, S.n)
            wt("gpsimd", P, P.n)
            gdma(ap, d)

    dump(h_sb[:, :, :], "on", [128, 8, SC], BF16)
    dump(q_sb[:, :, :], "q", [128, 8, SC], BF16)
    dump(k_sb[:, :, :], "k", [128, 8, SC], BF16)
    dump(v_sb[:, :, :], "v", [128, 4, D], BF16)
    dump(g_sb[:, :, :], "g", [128, 4, D])
    dump(o_sb[:, :, :], "o", [128, 8, SC], BF16)
    dump(carry[:, :], "carry", [1, 8])
    dump(idx_all[:, :], "idx", [128, NT])
    dump(lp_all[:, :], "lp", [128, NT])
    dump(pmax_all[:, :], "pmax", [128, NT])
    dump(hk_sb[:, :, :], "hk", [128, 8, 256])
    dump(hv_sb[:, :, :], "hv", [128, 8, 256])
    dump(z_sb[:, :], "z", [128, D])
    dump(p_sb[:, :], "p", [128, D])
    dump(p1t[:, :], "p1t", [128, D], BF16)
    dump(p2t[:, :], "p2t", [128, D], BF16)
    dump(am_sb[:, :, :], "am", [128, 8, 128], BF16)
    dump(bt_sb[:, :, :], "bt", [128, 8, 128], BF16)
    dump(stf[:, :, :], "stf", [128, 16, 128], BF16)
    dump(ktm[:, :, :], "ktm", [128, 16, 128], BF16)
    dump(st_sb[:, :, :], "st", [128, 2, D], BF16)
    dump(gc_sb[:, :, :], "gc", [128, 2, D])
    dump(pre_sb[:, :], "pre", [128, D])
    dump(y1t_sb[:, :], "y1t", [128, D])
    dump(xt_sb[:, :], "xt", [128, D])
    dump(lg_sb[:, :], "lg", [128, 8])
    dump(mx8[:, :], "mx8", [128, 8])
    dump(y1tf_sb[:, :, :], "y1f", [128, 8, 128])
    dump(mean1[:, :], "mean1", [128, 1])
    dump(var1[:, :], "var1", [128, 1])

    if stage >= 4:
        arena = Arena(nc, [h_sb, q_sb, k_sb, v_sb, g_sb, gc_sb, gd_sb, st_sb,
                           z_sb, p_sb, p1t, p2t, stf, p1f, p2f, ktm, am_sb,
                           bt_sb, hk_sb, hv_sb, hkb, hvb, o_sb, xsq2,
                           whalf[0], whalf[1], whalf[2], whalf[3],
                           sA, rr1, rr2])
        w1p = [arena.at(f"w1_{i}", [128, 2, DFF], BF16) for i in range(4)]
        w2p = [arena.at(f"w2_{i}", [128, 8, D], BF16) for i in range(4)]
        h1p = [arena.at(f"h1_{i}", [128, 16, 256], BF16) for i in range(2)]
        dgat = arena.at("dgat", [128, 2, D], BF16)
        dfm = arena.at("dfm", [128, 8, 256], BF16)
        eo_sb = arena.at("eo_sb", [128, 2, D], BF16)
        eo32 = arena.at("eo32", [128, D], F32)
        comb = arena.at("comb", [128, D], BF16)
        cnt_sb = arena.at("cnt_sb", [8, 8], F32)
        cntT = arena.at("cntT", [8, 8], F32)
        offsT = arena.at("offsT", [8, 8], F32)
        cstT = arena.at("cstT", [8, 8], F32)
        ccol = arena.at("ccol", [8, 1], F32)
        cst_row = arena.at("cst_row", [1, 8], F32)
        exoffs = arena.at("exoffs", [1, 8], F32)
        excnt = arena.at("excnt", [1, 8], F32)
        hi_row = arena.at("hi_row", [1, 8], F32)
        base_row = arena.at("base_row", [1, 8], F32)
        lo_bc = arena.at("lo_bc", [128, 8], F32)
        hi_bc = arena.at("hi_bc", [128, 8], F32)
        base_bc = arena.at("base_bc", [128, 8], F32)
        obc = arena.at("obc", [128, 8], F32)
        m160 = arena.at("m160", [128, 160], F32)
        v160 = arena.at("v160", [128, 160], F32)
        srcg = arena.at("srcg", [128, 20], F32)
        srcgu = arena.at("srcgu", [128, 20], U32)
        sru = arena.at("sru", [128, NT], U32)
        dval = arena.at("dval", [128, 1], F32)
        offs_row = arena.at("offs_row", [1, 8], F32)
        kcol = arena.at("kcol", [128, 1], F32)
        didx = arena.at("didx", [128, 1], U32)
        y1bf = arena.at("y1bf", [128, D], BF16)
        cstb = arena.at("cstb", [128, 8], F32)
        sr_all = arena.at("sr_all", [128, NT], F32)
        f2_all = arena.at("f2_all", [128, NT], F32)
        srb_all = arena.at("srb_all", [128, NT], F32)
        srub = arena.at("srub", [128, NT], U32)
        kca = arena.at("kca", [128, 1], F32)
        kcb = arena.at("kcb", [128, 1], F32)

        wt("scalar", V, V.n)
        cq = qdma(carry[0:1, :], cnt_in_d[0:1, :])
        wt("gpsimd", SQ, cq)
        cc1 = ctx.op("gpsimd", lambda e: e.collective_compute(
            "AllGather", ALU.bypass, replica_groups=[list(range(8))],
            ins=[cnt_in_d[:, :].opt()], outs=[cnt_ag_d[:, :].opt()]), CC)

        wt("tensor", V, V.n)
        tr(PU[0:8, 0:1], carry[0:1, :], id_f[0:1, 0:1], P)
        wt("vector", P, P.n)
        ve(lambda e: e.tensor_copy(ccol[:, :], PU[0:8, 0:1]), V)
        vdrain()
        wt("tensor", V, V.n)
        mm(PU[0:1, 8:16], ccol[:, :], str8[:, :], True, True, P)
        wt("vector", P, P.n)
        ve(lambda e: e.tensor_copy(cst_row[:, :], PU[0:1, 8:16]), V)
        vdrain()
        wt("tensor", V, V.n)
        mm(PU[:, 16:24], ones1f[:, :], cst_row[0:1, :], True, True, P)
        wt("vector", P, P.n)
        ve(lambda e: e.tensor_copy(cstb[:, :], PU[:, 16:24]), V)
        vdrain()

        dall = arena.at("dall", [128, NT], F32)
        didx_all = arena.at("didx_all", [128, NT], U32)
        y1s2 = nc.alloc_sbuf_tensor_at("y1s2", [128, D], F32,
                                       offset=nc.lookup_mloc(w1p[0]).addr)
        y1bf2 = nc.alloc_sbuf_tensor_at("y1bf2", [128, D], BF16,
                                        offset=nc.lookup_mloc(w1p[0]).addr + 4096)
        y1s3 = nc.alloc_sbuf_tensor_at("y1s3", [128, D], F32,
                                       offset=nc.lookup_mloc(w1p[0]).addr + 6144)
        y1bf3 = nc.alloc_sbuf_tensor_at(
            "y1bf3", [128, D], BF16, offset=nc.lookup_mloc(w1p[0]).addr + 10240)
        y1s4 = nc.alloc_sbuf_tensor_at("y1s4", [128, D], F32,
                                       offset=nc.lookup_mloc(w1p[1]).addr)
        y1bf4 = nc.alloc_sbuf_tensor_at("y1bf4", [128, D], BF16,
                                        offset=nc.lookup_mloc(w1p[1]).addr + 4096)
        ybufs = (y1_sb, y1s2, y1s3, y1s4)
        bbufs = (y1bf, y1bf2, y1bf3, y1bf4)
        wt("sync", V, V.n)
        wt("sync", P, P.n)
        yls = []
        for tile in range(4):
            yls.append((sdma(y1_dram[tile * 128:(tile + 1) * 128, :],
                             ybufs[tile % 4][:, :]), "sync", L))
        for tile in range(NT):
            ve(lambda e, tile=tile: e.tensor_scalar(
                oh_sb[:, :], iota8[:, :], idx_all[:, tile:tile + 1], None,
                ALU.is_equal))
            vdrain()
            ve(lambda e: e.tensor_mul(tmp8[:, :], oh_sb[:, :], cstb[:, :]))
            vdrain()
            ve(lambda e, tile=tile: e.reduce_sum(dall[:, tile:tile + 1],
                                                 tmp8[:, :], axis=AX.X))
            vdrain()
            ve(lambda e, tile=tile: e.tensor_tensor(
                dall[:, tile:tile + 1], dall[:, tile:tile + 1],
                lp_all[:, tile:tile + 1], ALU.add))
            vdrain()
        ve(lambda e: e.tensor_copy(didx_all[:, :], dall[:, :]), V)
        vdrain()
        d_v = V.n
        scat_g = []
        for tile in range(NT):
            lsem, _, lq = yls[tile]
            wt("vector", lq, lsem)
            if len(scat_g) >= 4:
                wt("vector", G, scat_g[-4])
            yb = ve(lambda e, tile=tile: e.tensor_copy(bbufs[tile % 4][:, :],
                                                       ybufs[tile % 4][:, :]), V)
            wt("gpsimd", V, max(yb, d_v))
            sg = ctx.op("gpsimd", lambda e, tile=tile: e.indirect_dma_start(
                out=ysort_dram[:, :],
                out_offset=IndirectOffsetOnAxis(didx_all[:, tile:tile + 1], 0),
                in_=bbufs[tile % 4][:, :], in_offset=None,
                bounds_check=bcregs["t"], oob_is_err=False), G, 16)
            scat_g.append(sg)
            if tile + 4 < NT:
                eng = ("sync", "scalar")[tile % 2]
                wt(eng, V, yb)
                yls.append(((sdma, qdma)[tile % 2](
                    y1_dram[(tile + 4) * 128:(tile + 5) * 128, :],
                    ybufs[tile % 4][:, :]), eng, (L, SQ)[tile % 2]))
        wt("gpsimd", G, scat_g[-1])
        cc2 = ctx.op("gpsimd", lambda e: e.collective_compute(
            "AllGather", ALU.bypass, replica_groups=[list(range(8))],
            ins=[ysort_dram[:, :].opt()], outs=[ag1_d[:, :].opt()]), CC)


        wt("sync", CC, cc1)
        cl = sdma(cnt_ag_d[:, :], cnt_sb[:, :])

        for i in range(4):
            gw1 = gdma(w1e_in[i * 256:(i + 1) * 256, :]
                       .rearrange("(c p) f -> p c f", p=128), w1p[i][:, :, :])
        for i in range(4):
            gw2b = gdma(w2e_in[i * 1024:(i + 1) * 1024, :]
                        .rearrange("(c p) d -> p c d", p=128), w2p[i][:, :, :])

        wt("vector", L, cl)
        wt("tensor", L, cl)
        wt("tensor", V, V.n)
        mm(PU[0:1, 24:32], bmask[:, :], cnt_sb[:, :], True, True, P)
        wt("vector", P, P.n)
        ve(lambda e: e.tensor_copy(offs_row[:, :], PU[0:1, 24:32]), V)
        vdrain()
        wt("tensor", V, V.n)
        mm(PU[:, 32:40], ones1f[:, :], offs_row[0:1, :], True, True, P)
        wt("vector", P, P.n)
        ve(lambda e: e.tensor_copy(obc[:, :], PU[:, 32:40]), V)
        vdrain()
        for tile in range(NT):
            ve(lambda e, tile=tile: e.tensor_scalar(
                oh_sb[:, :], iota8[:, :], idx_all[:, tile:tile + 1], None,
                ALU.is_equal))
            vdrain()
            ve(lambda e: e.tensor_mul(tmp8[:, :], oh_sb[:, :], obc[:, :]))
            vdrain()
            ve(lambda e: e.reduce_sum(dval[:, :], tmp8[:, :], axis=AX.X))
            vdrain()
            ve(lambda e, tile=tile: e.tensor_tensor(
                dval[:, :], dval[:, :], lp_all[:, tile:tile + 1], ALU.add))
            vdrain()
            ve(lambda e: e.tensor_single_scalar(kcol[:, :], dval[:, :], float(CAP),
                                                ALU.is_lt))
            ve(lambda e: e.tensor_single_scalar(kca[:, :], dval[:, :], 2048.0,
                                                ALU.is_lt))
            vdrain()
            ve(lambda e, tile=tile: e.tensor_scalar(
                f2_all[:, tile:tile + 1], kcol[:, :], -1.0, 2.0, ALU.mult, ALU.add))
            ve(lambda e: e.tensor_sub(kcb[:, :], kcol[:, :], kca[:, :]))
            ve(lambda e, tile=tile: e.scalar_tensor_tensor(
                sr_all[:, tile:tile + 1], idx_all[:, tile:tile + 1], 2048.0,
                dval[:, :], ALU.mult, ALU.add))
            ve(lambda e, tile=tile: e.scalar_tensor_tensor(
                srb_all[:, tile:tile + 1], idx_all[:, tile:tile + 1], 512.0,
                dval[:, :], ALU.mult, ALU.add))
            vdrain()
            ve(lambda e: e.tensor_scalar(kca[:, :], kca[:, :], -BIG, BIG,
                                         ALU.mult, ALU.add))
            ve(lambda e: e.tensor_scalar(kcb[:, :], kcb[:, :], -BIG, BIG - 2048.0,
                                         ALU.mult, ALU.add))
            vdrain()
            ve(lambda e, tile=tile: e.tensor_tensor(
                sr_all[:, tile:tile + 1], sr_all[:, tile:tile + 1], kca[:, :],
                ALU.add))
            ve(lambda e, tile=tile: e.tensor_tensor(
                srb_all[:, tile:tile + 1], srb_all[:, tile:tile + 1], kcb[:, :],
                ALU.add), V)
            vdrain()
        ve(lambda e: e.tensor_copy(sru[:, :], sr_all[:, :]))
        ve(lambda e: e.tensor_copy(srub[:, :], srb_all[:, :]), V)
        vdrain()

        wt("tensor", V, V.n)
        tr(PU[0:8, 40:48], cnt_sb[:, :], id_f[0:8, 0:8], P)
        wt("vector", P, P.n)
        ve(lambda e: e.tensor_copy(cntT[:, :], PU[0:8, 40:48]), V)
        vdrain()
        wt("tensor", V, V.n)
        mm(PU[0:8, 48:56], cnt_sb[:, :], str8[:, :], True, True)
        mm(PU[0:8, 56:64], str8[:, :], cntT[:, :], True, True, P)
        wt("vector", P, P.n)
        ve(lambda e: e.tensor_copy(offsT[:, :], PU[0:8, 48:56]))
        ve(lambda e: e.tensor_copy(cstT[:, :], PU[0:8, 56:64]), V)
        vdrain()
        wt("tensor", V, V.n)
        mm(PU[0:1, 64:72], ecol[:, :], offsT[:, :], True, True)
        mm(PU[0:1, 72:80], ecol[:, :], cstT[:, :], True, True)
        mm(PU[0:1, 80:88], ecol[:, :], cntT[:, :], True, True, P)
        wt("vector", P, P.n)
        ve(lambda e: e.tensor_copy(exoffs[:, :], PU[0:1, 64:72]))
        ve(lambda e: e.tensor_copy(cst_row[:, :], PU[0:1, 72:80]))
        ve(lambda e: e.tensor_copy(excnt[:, :], PU[0:1, 80:88]), V)
        vdrain()
        ve(lambda e: e.tensor_tensor(hi_row[:, :], exoffs[:, :], excnt[:, :],
                                     ALU.add))
        ve(lambda e: e.tensor_tensor(base_row[:, :], rowbase[:, :], cst_row[:, :],
                                     ALU.add))
        vdrain()
        ve(lambda e: e.tensor_tensor(base_row[:, :], base_row[:, :], exoffs[:, :],
                                     ALU.subtract))
        vdrain()
        br = ve(lambda e: e.tensor_scalar_add(base_row[:, :], base_row[:, :],
                                              -BIG), V)
        wt("tensor", V, br)
        mm(PU[:, 88:96], ones1f[:, :], exoffs[0:1, :], True, True)
        mm(PU[:, 96:104], ones1f[:, :], hi_row[0:1, :], True, True)
        mm(PU[:, 104:112], ones1f[:, :], base_row[0:1, :], True, True, P)
        wt("vector", P, P.n)
        ve(lambda e: e.tensor_copy(lo_bc[:, :], PU[:, 88:96]))
        ve(lambda e: e.tensor_copy(hi_bc[:, :], PU[:, 96:104]))
        ve(lambda e: e.tensor_copy(base_bc[:, :], PU[:, 104:112]), V)
        vdrain()
        s3 = s160[:, :].rearrange("p (g e) -> p g e", e=8)
        ve(lambda e: e.tensor_tensor(
            m160[:, :].rearrange("p (g e) -> p g e", e=8), s3,
            lo_bc[:, :].unsqueeze(1).broadcast_to([128, 20, 8]), ALU.is_ge))
        ve(lambda e: e.tensor_tensor(
            v160[:, :].rearrange("p (g e) -> p g e", e=8), s3,
            hi_bc[:, :].unsqueeze(1).broadcast_to([128, 20, 8]), ALU.is_lt))
        vdrain()
        ve(lambda e: e.tensor_mul(m160[:, :], m160[:, :], v160[:, :]))
        ve(lambda e: e.tensor_tensor(
            v160[:, :].rearrange("p (g e) -> p g e", e=8), s3,
            base_bc[:, :].unsqueeze(1).broadcast_to([128, 20, 8]), ALU.add))
        vdrain()
        ve(lambda e: e.tensor_mul(v160[:, :], v160[:, :], m160[:, :]))
        vdrain()
        ve(lambda e: e.reduce_sum(srcg[:, :],
                                  v160[:, :].rearrange("p (g e) -> p g e", e=8),
                                  axis=AX.X))
        vdrain()
        ve(lambda e: e.tensor_scalar_add(srcg[:, :], srcg[:, :], BIG))
        vdrain()
        sgu = ve(lambda e: e.tensor_copy(srcgu[:, :], srcg[:, :]), V)
        vdrain()

        wt("gpsimd", CC, cc2)
        wt("gpsimd", V, sgu)
        wt("tensor", G, gw1)
        wt("tensor", G, gw2b)
        eo_q = [0, 0]
        dg_g = [0] * 22
        tr_p = [0] * 22
        ev_v = [0, 0]

        def dgat_fetch(tt):
            buf = tt % 2
            if tt >= 2:
                wt("vector", P, tr_p[tt - 2])
            ms = ve(lambda e, buf=buf: e.memset(dgat[:, buf, :], 0.0), V)
            wt("gpsimd", V, ms)
            dg_g[tt] = ctx.op("gpsimd", lambda e, buf=buf, tt=tt:
                              e.indirect_dma_start(
                                  out=dgat[:, buf, :], out_offset=None,
                                  in_=ag1_d[:, :],
                                  in_offset=IndirectOffsetOnAxis(
                                      srcgu[:, tt:tt + 1], 0),
                                  bounds_check=bcregs["bt"],
                                  oob_is_err=False), G, 16)

        dgat_fetch(0)
        dgat_fetch(1)
        for grp in range(10):
            for half in range(2):
                tt = grp * 2 + half
                buf = tt % 2
                wt("tensor", G, dg_g[tt])
                for sl in range(8):
                    tr(PT[:, sl * 128:(sl + 1) * 128],
                       dgat[:, buf, sl * 128:(sl + 1) * 128], id_bf[:, :],
                       P if sl == 7 else None)
                tr_p[tt] = P.n
                wt("vector", P, tr_p[tt])
                tv = ve(lambda e, half=half: e.tensor_copy(
                    dfm[:, :, :].rearrange("p a b -> p a b")[:, :, half * 128:
                                                            half * 128 + 128],
                    PT[:, :].rearrange("p (a b) -> p a b", a=8)), V)
                wt("tensor", V, tv)
                if tt + 2 < 20:
                    dgat_fetch(tt + 2)
            hv_evac = []
            for fp in range(16):
                pb = (PE_, PF)[fp % 2]
                if fp >= 2:
                    wt("tensor", S, hv_evac[fp - 2])
                for half2 in range(2):
                    fc = 2 * fp + half2
                    for dc in range(8):
                        mm(pb[:, half2 * 256:half2 * 256 + 256],
                           w1p[dc // 2][:, dc % 2, fc * 128:(fc + 1) * 128],
                           dfm[:, dc, :], dc == 0, dc == 7,
                           P if (dc == 7 and half2 == 1) else None)
                wt("scalar", P, P.n)
                hv_evac.append(se(lambda e, fp=fp, pb=pb: e.activation(
                    out=h1p[fp // 8][:, (2 * fp) % 16:(2 * fp) % 16 + 2, :]
                    .rearrange("p a b -> p (a b)"),
                    in_=pb[:, :], func=AF.Relu), S))
            h1_s = hv_evac[-1]
            wt("tensor", S, h1_s)
            for half in range(2):
                wt("tensor", V, ev_v[half])
                bks = ((PC, PD), (PA, PB))[half]
                for nt in range(2):
                    pb = bks[nt]
                    for fc in range(32):
                        mm(pb[:, :],
                           h1p[fc // 16][:, fc % 16, half * 128:half * 128 + 128],
                           w2p[fc // 8][:, fc % 8, nt * 512:(nt + 1) * 512],
                           fc == 0, fc == 31,
                           P if (fc == 31 and nt == 1) else None)
                eo_p = P.n
                wt("vector", P, eo_p)
                wt("vector", L, eo_q[half])
                ve(lambda e, half=half, bks=bks: e.tensor_copy(
                    eo_sb[:, half, 0:512], bks[0][:, :]))
                ev_v[half] = ve(lambda e, half=half, bks=bks: e.tensor_copy(
                    eo_sb[:, half, 512:1024], bks[1][:, :]), V)
                wt("sync", V, ev_v[half])
                row0 = grp * 256 + half * 128
                eo_q[half] = sdma(eo_sb[:, half, :], eout_d[row0:row0 + 128, :])
            if grp == 7 and AG2_EARLY:
                wt("gpsimd", L, max(eo_q))
                cc3a = ctx.op("gpsimd", lambda e: e.collective_compute(
                    "AllGather", ALU.bypass, replica_groups=[list(range(8))],
                    ins=[eout_d[0:2048, :].opt()], outs=[ag2a_d[:, :].opt()]), CC)

        if not AG2_EARLY:
            wt("gpsimd", L, max(eo_q))
            cc3a = ctx.op("gpsimd", lambda e: e.collective_compute(
                "AllGather", ALU.bypass, replica_groups=[list(range(8))],
                ins=[eout_d[0:2048, :].opt()], outs=[ag2a_d[:, :].opt()]), CC)
        wt("gpsimd", L, max(eo_q))
        if AG2_EARLY:
            wt("gpsimd", CC, cc3a)
        cc3 = ctx.op("gpsimd", lambda e: e.collective_compute(
            "AllGather", ALU.bypass, replica_groups=[list(range(8))],
            ins=[eout_d[2048:2560, :].opt()], outs=[ag2b_d[:, :].opt()]), CC)

        comb2 = nc.alloc_sbuf_tensor_at("comb2", [128, D], BF16,
                                        offset=nc.lookup_mloc(w1p[1]).addr)
        y2st2 = nc.alloc_sbuf_tensor_at("y2st2", [128, D], F32,
                                        offset=nc.lookup_mloc(dfm).addr)
        y1c2 = nc.alloc_sbuf_tensor_at("y1c2", [128, D], F32,
                                       offset=nc.lookup_mloc(w1p[1]).addr + 4096)
        preb = nc.alloc_sbuf_tensor_at("preb", [128, D], F32, offset=_xaddr)
        lnsc = nc.alloc_sbuf_tensor_at("lnsc", [128, D], F32,
                                       offset=nc.lookup_mloc(h1p[0]).addr)
        cbufs = (comb, comb2)
        ycbufs = (y1_sb, y1c2)
        prebufs = (pre_sb, preb)
        statn = ((mraw, vraw, mean1, var1, rstd1, tmp1c),
                 (mraw2, vraw2, mean1b, var1b, rstd1b, tmp1cb))
        wt("gpsimd", CC, cc3)
        out_q = {}
        cgs = {}
        ylc = {}
        pd_t = {}
        mn_t = {}
        v2_t = {}
        rs_t = {}

        def c_prefetch(tile):
            p = tile % 2
            msv = ve(lambda e, p=p: e.memset(cbufs[p][:, :], 0.0), V)
            wt("gpsimd", V, msv)
            if tile >= 2:
                wt("gpsimd", G, cgs[tile - 2])
            ctx.op("gpsimd", lambda e, p=p, tile=tile: e.indirect_dma_start(
                out=cbufs[p][:, :], out_offset=None, in_=ag2a_d[:, :],
                in_offset=IndirectOffsetOnAxis(sru[:, tile:tile + 1], 0),
                bounds_check=bcregs["eca"], oob_is_err=False), G, 16)
            cgs[tile] = ctx.op(
                "gpsimd", lambda e, p=p, tile=tile: e.indirect_dma_start(
                    out=cbufs[p][:, :], out_offset=None, in_=ag2b_d[:, :],
                    in_offset=IndirectOffsetOnAxis(srub[:, tile:tile + 1], 0),
                    bounds_check=bcregs["ecb"], oob_is_err=False), G, 16)
            wt(("sync", "scalar")[p], V, pd_t.get(tile - 2, 0))
            ylc[tile] = (sdma, qdma)[p](y1_dram[tile * 128:(tile + 1) * 128, :],
                                        ycbufs[p][:, :])

        def c_stage1(tile):
            p = tile % 2
            wt("vector", G, cgs[tile])
            wt("vector", (L, SQ)[p], ylc[tile])
            ve(lambda e, p=p, tile=tile: e.tensor_scalar(
                prebufs[p][:, :], ycbufs[p][:, :], f2_all[:, tile:tile + 1],
                None, ALU.mult))
            vdrain()
            pd_t[tile] = ve(lambda e, p=p, tile=tile: e.scalar_tensor_tensor(
                prebufs[p][:, :], cbufs[p][:, :], pmax_all[:, tile:tile + 1],
                prebufs[p][:, :], ALU.mult, ALU.add), V)
            if tile + 2 < NT:
                c_prefetch(tile + 2)

        def c_stage2(tile):
            p = tile % 2
            mr, vr, mn, vv, rs, tc = statn[p]
            wt("scalar", V, pd_t[tile])
            se(lambda e, p=p, mr=mr: e.activation(
                out=lnsc[:, :], in_=prebufs[p][:, :], func=AF.Identity,
                accum_out=mr[:, :]))
            se(lambda e, p=p, vr=vr: e.activation(
                out=lnsc[:, :], in_=prebufs[p][:, :], func=AF.Square,
                accum_out=vr[:, :]))
            sdrain()
            se(lambda e, mr=mr, tc=tc: e.activation(
                out=tc[:, :], in_=mr[:, :], func=AF.Square, scale=1.0 / D))
            mn_t[tile] = se(lambda e, mr=mr, mn=mn: e.activation(
                out=mn[:, :], in_=mr[:, :], func=AF.Identity, scale=1.0 / D), S)

        def c_finish(tile):
            p = tile % 2
            mr, vr, mn, vv, rs, tc = statn[p]
            wt("vector", S, mn_t[tile])
            v2_t[tile] = ve(lambda e, vr=vr, vv=vv, tc=tc: e.tensor_scalar(
                vv[:, :], vr[:, :], 1.0 / D, tc[:, :], ALU.mult,
                ALU.subtract), V)
            wt("scalar", V, v2_t[tile])
            se(lambda e, vv=vv, rs=rs: e.activation(out=rs[:, :], in_=vv[:, :],
                                                    func=AF.Ln, bias=EPS))
            sdrain()
            rs_t[tile] = se(lambda e, rs=rs: e.activation(
                out=rs[:, :], in_=rs[:, :], func=AF.Exp, scale=-0.5), S)
            wt("vector", S, rs_t[tile])
            if tile >= 2:
                wt("vector", SQ, out_q[tile - 2])
            y2buf = (eo32[:, :], y2st2[:, :])[p]
            y2v = ve(lambda e, y2buf=y2buf, p=p, mn=mn, rs=rs: e.tensor_scalar(
                y2buf, prebufs[p][:, :], mn[:, :], rs[:, :],
                ALU.subtract, ALU.mult), V)
            wt("scalar", V, y2v)
            out_q[tile] = qdma(y2buf, y2_out[tile * 128:(tile + 1) * 128, :])

        for tile in range(min(2, NT)):
            c_prefetch(tile)
        for tile in range(NT):
            c_stage1(tile)
            c_stage2(tile)
            if tile > 0:
                c_finish(tile - 1)
        c_finish(NT - 1)
        wt("gpsimd", SQ, out_q[NT - 1])
    else:
        wt("gpsimd", SQ, SQ.n)
        wt("gpsimd", V, V.n)
        for tile in range(NT):
            gdma(y1_dram[tile * 128:(tile + 1) * 128, :],
                 y2_out[tile * 128:(tile + 1) * 128, :])
        wt("gpsimd", G, G.n)

    with nc.Block() as block:
        @block.sync
        def _(eng):
            for fn in ctx.streams["sync"]:
                fn(eng)

        @block.gpsimd
        def _(eng):
            for fn in ctx.streams["gpsimd"]:
                fn(eng)

        @block.tensor
        def _(eng):
            for fn in ctx.streams["tensor"]:
                fn(eng)

        @block.vector
        def _(eng):
            for fn in ctx.streams["vector"]:
                fn(eng)

        @block.scalar
        def _(eng):
            for fn in ctx.streams["scalar"]:
                fn(eng)

    return nc


def _consts():
    ident = np.eye(128)
    jj, tt = np.mgrid[0:128, 0:128]
    utri = (jj <= tt).astype(np.float32)
    ustrict = (jj < tt).astype(np.float32)
    iota8 = np.tile(np.arange(8, dtype=np.float32), (128, 1))
    s160 = np.zeros((128, 160), np.float32)
    for g in range(20):
        for e in range(8):
            s160[:, g * 8 + e] = g * 128 + np.arange(128)
    ee, e2 = np.mgrid[0:8, 0:8]
    str8 = (ee < e2).astype(np.float32)
    return ident, utri, ustrict, iota8, s160, str8


_BUILD_KW = {}


def prep_in_maps(inputs):
    x = np.asarray(inputs["L0_attn_hidden_states"], np.float32)
    nw = np.asarray(inputs["L0_attn_norm_weight"], np.float32)
    gn = np.asarray(inputs["L0_attn_gnorm_weight"], np.float32)
    wq = np.asarray(inputs["L0_attn_w_q"], np.float32)
    wk = np.asarray(inputs["L0_attn_w_k"], np.float32)
    wv = np.asarray(inputs["L0_attn_w_v"], np.float32)
    wf = np.asarray(inputs["L0_attn_w_f"], np.float32)
    wo = np.asarray(inputs["L0_attn_w_o"], np.float32)
    Wr = np.asarray(inputs["L0_ffn_Wr"], np.float32)
    W1 = np.asarray(inputs["L0_ffn_W1"], np.float32)
    W2 = np.asarray(inputs["L0_ffn_W2"], np.float32)

    ident, utri, ustrict, iota8, s160, str8 = _consts()
    shared = dict(
        wqT=np.ascontiguousarray(wq.T), wkT=np.ascontiguousarray(wk.T),
        wvT=np.ascontiguousarray(wv.T), wfT=np.ascontiguousarray(wf.T),
        woT=np.ascontiguousarray(wo.T),
        nw_c=np.ascontiguousarray(nw.reshape(8, 128).T),
        gn_c=np.ascontiguousarray(gn.reshape(8, 128).T),
        wr=Wr,
        ident_bf=ident.astype(ml_dtypes.bfloat16),
        ident_f32=ident.astype(np.float32),
        utri_bf=utri.astype(ml_dtypes.bfloat16), utri_f=utri,
        ustrict_f32=ustrict,
        iota8=iota8, s160=s160, str8=str8,
        rowbase=(np.arange(8, dtype=np.float32) * T)[None, :],
        ones1f=np.ones((1, 128), np.float32),
        onescol_bf=np.ones((128, 1), ml_dtypes.bfloat16),
        onescol_f=np.ones((128, 1), np.float32),
        invd_bf=np.full((128, 1), 1.0 / 1024.0, ml_dtypes.bfloat16),
        epscol=np.full((128, 1), 1e-5, np.float32),
    )
    in_maps = []
    for b in range(8):
        m = dict(shared)
        m["x_fm"] = np.ascontiguousarray(x[b].T)
        m["x_tm"] = np.ascontiguousarray(x[b])
        m["w1e"] = np.ascontiguousarray(W1[b])
        m["w2e"] = np.ascontiguousarray(W2[b])
        m["bmask"] = (np.arange(8) < b).astype(np.float32)[:, None]
        m["ecol"] = (np.arange(8) == b).astype(np.float32)[:, None]
        in_maps.append(m)
    return in_maps


def kernel(**inputs):
    in_maps = prep_in_maps(inputs)
    nc = build_nc(**_BUILD_KW)
    res = run_bass_kernel_spmd(nc, in_maps, core_ids=list(range(8)))
    out = np.stack([r["y2"] for r in res.results], axis=0)
    return out


# version 30
# speedup vs baseline: 1.2945x; 1.0071x over previous
import numpy as np
import ml_dtypes
import concourse.bass as bass
import concourse.mybir as mybir
from concourse.bass_core_rust import IndirectOffsetOnAxis
from concourse.bass_utils import run_bass_kernel_spmd

F32 = mybir.dt.float32
BF16 = mybir.dt.bfloat16
U32 = mybir.dt.uint32
AF = mybir.ActivationFunctionType
ALU = mybir.AluOpType
AX = mybir.AxisListType

B, T, D = 8, 2048, 1024
H = 4
E, DFF, CAP = 8, 4096, 2560
SC, NSC = 512, 4
NCHS = 2
NT = 16
EPS = 1e-5
SC16 = 1.0 / 16.0
BIG = float(2 ** 24)

ENGINES = ("sync", "gpsimd", "tensor", "vector", "scalar")


class Sem:
    def __init__(self, nc, name):
        self.h = nc.alloc_semaphore(name)
        self.n = 0

    def bump(self, k):
        self.n += k
        return self.n


class Ctx:
    def __init__(self, nc):
        self.nc = nc
        self.streams = {e: [] for e in ENGINES}

    def emit(self, engine, fn):
        self.streams[engine].append(fn)

    def wait(self, engine, sem, val):
        if val <= 0:
            return
        self.emit(engine, lambda eng, s=sem.h, v=val: eng.wait_ge(s, v))

    def op(self, engine, build, sem=None, k=1):
        if sem is None:
            self.emit(engine, build)
            return None
        v = sem.bump(k)
        self.emit(engine, lambda eng, b=build, s=sem.h, kk=k: b(eng).then_inc(s, kk))
        return v


class Arena:

    def __init__(self, nc, dead_handles):
        self.nc = nc
        dsz = {F32: 4, BF16: 2, U32: 4}
        regs = []
        for h in dead_handles:
            m = nc.lookup_mloc(h)
            nb = 1
            for d in list(h.shape)[1:]:
                nb *= d
            nb *= dsz[h.dtype]
            regs.append([m.addr, m.addr + nb])
        regs.sort()
        merged = []
        for a, b in regs:
            if merged and a <= merged[-1][1]:
                merged[-1][1] = max(merged[-1][1], b)
            else:
                merged.append([a, b])
        self.free = merged

    def at(self, name, shape, dtype):
        import math
        nb = 1
        for d in shape[1:]:
            nb *= d
        nb *= {F32: 4, BF16: 2, U32: 4}[dtype]
        for r in self.free:
            a = (r[0] + 31) & ~31
            if a + nb <= r[1]:
                t = self.nc.alloc_sbuf_tensor_at(name, shape, dtype, offset=a)
                r[0] = a + nb
                return t
        raise RuntimeError(f"arena: no room for {name} ({nb}B); free={self.free}")


AG2_EARLY = False


def build_nc(debug=(), stage=4):
    nc = bass.Bass("TRN2", target_bir_lowering=False, debug=False)
    ctx = Ctx(nc)

    di = lambda n, s, d: nc.dram_tensor(n, s, d, kind="ExternalInput").ap()
    x_fm = di("x_fm", [D, T], F32)
    x_tm = di("x_tm", [T, D], F32)
    wqT = di("wqT", [D, D], F32)
    wkT = di("wkT", [D, D], F32)
    wvT = di("wvT", [D, D], F32)
    wfT = di("wfT", [D, D], F32)
    woT = di("woT", [D, D], F32)
    nw_c = di("nw_c", [128, 8], F32)
    gn_c = di("gn_c", [128, 8], F32)
    wr_in = di("wr", [D, E], F32)
    w1e_in = di("w1e", [D, DFF], F32)
    w2e_in = di("w2e", [DFF, D], F32)
    ident_bf_in = di("ident_bf", [128, 128], BF16)
    ident_f32_in = di("ident_f32", [128, 128], F32)
    utri_bf_in = di("utri_bf", [128, 128], BF16)
    utri_f_in = di("utri_f", [128, 128], F32)
    ustrict_f32_in = di("ustrict_f32", [128, 128], F32)
    iota8_in = di("iota8", [128, 8], F32)
    s160_in = di("s160", [128, 160], F32)
    str8_in = di("str8", [8, 8], F32)
    bmask_in = di("bmask", [8, 1], F32)
    ecol_in = di("ecol", [8, 1], F32)
    rowbase_in = di("rowbase", [1, 8], F32)
    ones1f_in = di("ones1f", [1, 128], F32)
    onescol_bf_in = di("onescol_bf", [128, 1], BF16)
    onescol_f_in = di("onescol_f", [128, 1], F32)
    invd_bf_in = di("invd_bf", [128, 1], BF16)
    epscol_in = di("epscol", [128, 1], F32)

    y2_out = nc.dram_tensor("y2", [T, D], F32, kind="ExternalOutput").ap()

    y1_dram = nc.dram_tensor("y1_dram", [T, D], F32).ap()
    ysort_dram = nc.dram_tensor("ysort_dram", [T, D], BF16).ap()
    cnt_in_d = nc.dram_tensor("cnt_in_d", [1, 8], F32).ap()
    cnt_ag_d = nc.dram_tensor("cnt_ag_d", [8, 8], F32, addr_space="Shared").ap()
    ag1_d = nc.dram_tensor("ag1_d", [B * T, D], BF16, addr_space="Shared").ap()
    eout_d = nc.dram_tensor("eout_d", [CAP, D], BF16).ap()
    ag2_d = nc.dram_tensor("ag2_d", [E * CAP, D], BF16,
                           addr_space="Shared").ap()

    dbg = {}

    def dbg_out(name, shape, dt=F32):
        if name in debug:
            dbg[name] = nc.dram_tensor("dbg_" + name, shape, dt,
                                       kind="ExternalOutput").ap()
            return dbg[name]
        return None

    sb = lambda n, s, d: nc.alloc_sbuf_tensor(n, s, d)
    wr_sb = sb("wr_sb", [128, 8, E], F32)
    nw_sb = sb("nw_sb", [128, 8], F32)
    gn_sb = sb("gn_sb", [128, 8], F32)

    id_bf = sb("id_bf", [128, 128], BF16)
    id_f = sb("id_f", [128, 128], F32)
    utri = sb("utri", [128, 128], BF16)
    utri_f = sb("utri_f_sb", [128, 128], F32)
    ustr = sb("ustr", [128, 128], F32)
    iota8 = sb("iota8_sb", [128, 8], F32)
    s160 = sb("s160_sb", [128, 160], F32)
    str8 = sb("str8_sb", [8, 8], F32)
    bmask = sb("bmask_sb", [8, 1], F32)
    ecol = sb("ecol_sb", [8, 1], F32)
    rowbase = sb("rowbase_sb", [1, 8], F32)
    ones1f = sb("ones1f_sb", [1, 128], F32)
    onescol_bf = sb("onescol_bf_sb", [128, 1], BF16)
    onescol_f = sb("onescol_f_sb", [128, 1], F32)
    invd_bf = sb("invd_bf_sb", [128, 1], BF16)
    epscol = sb("epscol_sb", [128, 1], F32)
    nc.const_aps.register(F32, EPS, epscol[:, 0:1])

    x_bf = sb("x_bf", [128, 8, SC], BF16)
    xsq2 = sb("xsq2", [128, 2, SC], BF16)
    h_sb = sb("h_sb", [128, 8, SC], BF16)
    q_sb = sb("q_sb", [128, 8, SC], BF16)
    k_sb = sb("k_sb", [128, 8, SC], BF16)
    v_sb = sb("v_sb", [128, 4, D], BF16)
    g_sb = sb("g_sb", [128, 4, D], F32)
    sA = sb("sA", [128, D], F32)
    rr1 = sb("rr1", [1, SC], F32)
    rr2 = sb("rr2", [1, SC], F32)

    ghi = sb("ghi", [128, 2, D], BF16)
    glo = sb("glo", [128, 2, D], BF16)
    gc_sb = sb("gc_sb", [128, 2, D], F32)
    gd_sb = sb("gd_sb", [128, 2, D], F32)
    gtotr = sb("gtotr", [1, D], F32)
    g0r = sb("g0r", [1, D], F32)
    egtot = sb("egtot", [1, D], F32)
    _xaddr = nc.lookup_mloc(x_bf).addr
    e1t = nc.alloc_sbuf_tensor_at("e1t", [128, D], F32, offset=_xaddr)
    ept = nc.alloc_sbuf_tensor_at("ept", [128, D], F32, offset=_xaddr + 4096)
    st_sb = sb("st_sb", [128, 2, D], BF16)
    z_sb = sb("z_sb", [128, D], F32)
    p_sb = sb("p_sb", [128, D], F32)
    mx4 = sb("mx4", [128, 4], F32)
    nmx4 = sb("nmx4", [128, 4], F32)
    sum4 = sb("sum4", [128, 4], F32)
    rden = sb("rden", [128, 4], F32)
    p1t = sb("p1t", [128, D], BF16)
    p2t = sb("p2t", [128, D], BF16)
    stf = sb("stf", [128, 16, 128], BF16)
    p1f = sb("p1f", [128, 8, 128], BF16)
    p2f = sb("p2f", [128, 8, 128], BF16)
    ktm = sb("ktm", [128, 16, 128], BF16)
    am_sb = sb("am_sb", [128, 8, 128], BF16)
    bt_sb = sb("bt_sb", [128, 8, 128], BF16)
    hk_sb = sb("hk_sb", [128, 8, 256], F32)
    hv_sb = sb("hv_sb", [128, 8, 256], F32)
    hkb = sb("hkb", [128, 8, 256], BF16)
    hvb = sb("hvb", [128, 8, 256], BF16)
    lamc = sb("lamc", [128, 8], F32)
    o_sb = sb("o_sb", [128, 8, SC], BF16)

    xt_sb = nc.alloc_sbuf_tensor_at("xt_sb", [128, D], F32, offset=_xaddr)
    pre_sb = nc.alloc_sbuf_tensor_at("pre_sb", [128, D], F32, offset=_xaddr + 4096)
    mean1 = sb("mean1", [128, 1], F32)
    var1 = sb("var1", [128, 1], F32)
    mraw = sb("mraw", [128, 1], F32)
    vraw = sb("vraw", [128, 1], F32)
    mraw2 = sb("mraw2", [128, 1], F32)
    vraw2 = sb("vraw2", [128, 1], F32)
    mean1b = sb("mean1b", [128, 1], F32)
    var1b = sb("var1b", [128, 1], F32)
    rstd1b = sb("rstd1b", [128, 1], F32)
    tmp1cb = sb("tmp1cb", [128, 1], F32)
    rstd1 = sb("rstd1", [128, 1], F32)
    y1t_sb = nc.alloc_sbuf_tensor_at("y1t_sb", [128, D], F32,
                                     offset=nc.lookup_mloc(st_sb).addr)
    y1tf_sb = nc.alloc_sbuf_tensor_at("y1tf_sb", [128, 8, 128], F32,
                                      offset=nc.lookup_mloc(stf).addr)
    y1_sb = nc.alloc_sbuf_tensor_at("y1_sb", [128, D], F32,
                                    offset=nc.lookup_mloc(ghi).addr)
    y1f_sb = nc.alloc_sbuf_tensor_at("y1f_sb", [128, 8, 128], F32,
                                     offset=nc.lookup_mloc(glo).addr)
    lg_sb = sb("lg_sb", [128, 8], F32)
    mx8 = sb("mx8", [128, 8], F32)
    idxu = sb("idxu", [128, 8], U32)
    nmx = sb("nmx", [128, 1], F32)
    pex = sb("pex", [128, 8], F32)
    den8 = sb("den8", [128, 1], F32)
    oh_sb = sb("oh_sb", [128, 8], F32)
    posx = sb("posx", [128, 8], F32)
    carry = sb("carry", [1, 8], F32)
    idx_all = sb("idx_all", [128, NT], F32)
    lp_all = sb("lp_all", [128, NT], F32)
    pmax_all = sb("pmax_all", [128, NT], F32)
    tmp8 = sb("tmp8", [128, 8], F32)
    tmp1c = sb("tmp1c", [128, 1], F32)

    PA = nc.alloc_psum_tensor("PA", [128, 512], F32)
    PB = nc.alloc_psum_tensor("PB", [128, 512], F32)
    PC = nc.alloc_psum_tensor("PC", [128, 512], F32)
    PD = nc.alloc_psum_tensor("PD", [128, 512], F32)
    PE_ = nc.alloc_psum_tensor("PE_", [128, 512], F32)
    PF = nc.alloc_psum_tensor("PF", [128, 512], F32)
    PT = nc.alloc_psum_tensor("PT", [128, 1024], BF16)
    PU = nc.alloc_psum_tensor("PU", [128, 512], F32)

    L = Sem(nc, "L")
    G = Sem(nc, "G")
    SQ = Sem(nc, "SQ")
    P = Sem(nc, "P")
    V = Sem(nc, "V")
    S = Sem(nc, "S")
    CC = Sem(nc, "CC")

    def sdma(src_ap, dst_ap):
        return ctx.op("sync", lambda e, o=dst_ap, i=src_ap: e.dma_start(out=o, in_=i),
                      L, 16)

    def gdma(src_ap, dst_ap):
        return ctx.op("gpsimd", lambda e, o=dst_ap, i=src_ap: e.dma_start(out=o, in_=i),
                      G, 16)

    def qdma(src_ap, dst_ap):
        return ctx.op("scalar", lambda e, o=dst_ap, i=src_ap: e.dma_start(out=o, in_=i),
                      SQ, 16)

    def mm(out_ap, lhsT_ap, rhs_ap, start, stop, sem=None):
        return ctx.op("tensor",
                      lambda e, o=out_ap, l=lhsT_ap, r=rhs_ap, st=start, sp=stop:
                      e.matmul(o, l, r, start=st, stop=sp), sem)

    def tr(out_ap, in_ap, ident_ap, sem=None):
        return ctx.op("tensor",
                      lambda e, o=out_ap, i=in_ap, d=ident_ap: e.transpose(o, i, d),
                      sem)

    def ve(build, sem=None):
        return ctx.op("vector", build, sem)

    def vdrain():
        ctx.emit("vector", lambda e: e.drain())

    def sdrain():
        ctx.emit("scalar", lambda e: e.drain())

    def se(build, sem=None):
        return ctx.op("scalar", build, sem)

    def wt(engine, sem, val):
        ctx.wait(engine, sem, val)

    wstate = {"par": 0, "last_use": [0, 0]}

    def wload(src):
        i = wstate["par"]
        wstate["par"] ^= 1
        wt("gpsimd", P, wstate["last_use"][i])
        gv = gdma(src.rearrange("(c p) o -> p c o", p=128), wbufs[i][:, :, :])
        return i, gv

    def wdone(i):
        wstate["last_use"][i] = P.n

    for src, dst in ((ident_bf_in, id_bf), (ident_f32_in, id_f), (utri_bf_in, utri),
                     (utri_f_in, utri_f), (ustrict_f32_in, ustr), (iota8_in, iota8),
                     (s160_in, s160), (str8_in, str8), (bmask_in, bmask),
                     (ecol_in, ecol), (rowbase_in, rowbase), (ones1f_in, ones1f),
                     (onescol_bf_in, onescol_bf), (onescol_f_in, onescol_f),
                     (invd_bf_in, invd_bf), (epscol_in, epscol),
                     (nw_c, nw_sb), (gn_c, gn_sb)):
        sdma(src[:, :], dst[:, :])
    sdma(wr_in.rearrange("(c p) e -> p c e", p=128), wr_sb[:, :, :])
    consts_done = L.n

    warm_d = nc.dram_tensor("warm_d", [1, 8], F32).ap()
    warm_o = nc.dram_tensor("warm_o", [8, 8], F32, addr_space="Shared").ap()
    bcregs = {}

    def _gp_init(e):
        for nm, val in (("t", T - 1), ("bt", B * T - 1), ("ec", E * CAP - 1),
                        ("eca", 8 * 2048 - 1), ("ecb", 8 * 512 - 1)):
            r = e.alloc_register("bc_" + nm)
            e.reg_mov(r, val)
            bcregs[nm] = r
    ctx.emit("gpsimd", _gp_init)
    wt("gpsimd", L, consts_done)
    wrm = gdma(ones1f[0:1, 0:8], warm_d[:, :])
    wt("gpsimd", G, wrm)
    ctx.op("gpsimd", lambda e: e.collective_compute(
        "AllGather", ALU.bypass, replica_groups=[list(range(8))],
        ins=[warm_d[:, :].opt()], outs=[warm_o[:, :].opt()]), CC)

    ve(lambda e: e.memset(hk_sb[:, :, :], 0.0))
    ve(lambda e: e.memset(hv_sb[:, :, :], 0.0))
    ve(lambda e: e.memset(hkb[:, :, :], 0.0))
    ve(lambda e: e.memset(hvb[:, :, :], 0.0))
    init_v = ve(lambda e: e.memset(carry[:, :], 0.0), V)

    h_done_v = 0
    prev_sc_pe = 0
    prev_sc_v = 0

    whalf = [sb("wh" + str(i), [128, 8, 512], BF16) for i in range(4)]
    wst = {"last": [0, 0, 0, 0]}

    def wload_h(src, half, sl):
        wt("gpsimd", P, wst["last"][sl])
        gv = gdma(src[:, half * 512:(half + 1) * 512]
                  .rearrange("(c p) o -> p c o", p=128), whalf[sl][:, :, :])
        return sl, gv

    def wdone_h(sl):
        wst["last"][sl] = P.n

    st_front = {}
    st_scan = {}
    scan_state = {"hv_add_v": init_v, "gl_p": 0, "y1q": 0, "split": {}}

    def emit_front(s):
        t0 = s * SC
        iq = [wload_h(wqT, h2, 2 + h2) for h2 in range(2)]
        prev = st_scan.get(s - 1, {})
        wt("gpsimd", V, prev.get("scan_done_v", 0))
        xld = gdma(x_fm[:, t0:t0 + SC].rearrange("(c p) t -> p c t", p=128),
                   x_bf[:, :, :])
        wt("vector", G, xld)
        wt("vector", L, consts_done)
        sq_mm_p = [0] * 8
        for c in range(8):
            if c >= 2:
                wt("vector", P, sq_mm_p[c - 2])
            vsq = ve(lambda e, c=c: e.tensor_mul(xsq2[:, c % 2, :], x_bf[:, c, :],
                                                 x_bf[:, c, :]), V)
            wt("tensor", V, vsq)
            sq_mm_p[c] = mm(PF[0:1, 0:512], invd_bf[:, :], xsq2[:, c % 2, :],
                            c == 0, c == 7, P)
        wt("scalar", P, P.n)
        se(lambda e: e.activation(out=rr1[:, :], in_=PF[0:1, 0:512], func=AF.Ln,
                                  bias=EPS))
        sdrain()
        sr = se(lambda e: e.activation(out=rr2[:, :], in_=rr1[:, :], func=AF.Exp,
                                       scale=-0.5), S)
        wt("tensor", S, sr)
        bc = mm(PE_[:, :], ones1f[:, :], rr2[:, :], True, True, P)
        wt("vector", P, bc)
        wt("vector", V, init_v)
        hv_ = 0
        for c in range(8):
            hv_ = ve(lambda e, c=c: e.scalar_tensor_tensor(
                h_sb[:, c, :], x_bf[:, c, :], nw_sb[:, c:c + 1], PE_[:, :],
                ALU.mult, ALU.mult), V)
        h_done = hv_

        wt("tensor", V, h_done)
        evac_v = []

        def fm_proj_h(slots, dst):
            for h2 in range(2):
                sl, gw = slots[h2]
                wt("tensor", G, gw)
                for dc in range(4):
                    pb = (PC, PD, PE_)[len(evac_v) % 3]
                    if len(evac_v) >= 3:
                        wt("tensor", V, evac_v[-3])
                    for kc in range(8):
                        mm(pb[:, :], whalf[sl][:, kc, dc * 128:(dc + 1) * 128],
                           h_sb[:, kc, :], kc == 0, kc == 7,
                           P if kc == 7 else None)
                    wt("vector", P, P.n)
                    evac_v.append(ve(lambda e, d=dst, dc=dc, h2=h2, pb=pb:
                                     e.tensor_copy(d[:, h2 * 4 + dc, :], pb[:, :]),
                                     V))
                wdone_h(sl)

        def tm_proj_h(slots, is_g):
            for h2 in range(2):
                sl, gw = slots[h2]
                wt("tensor", G, gw)
                for ts in range(4):
                    pb = (PC, PD, PE_)[len(evac_v) % 3]
                    if len(evac_v) >= 3:
                        wt("tensor", V, evac_v[-3])
                    for kc in range(8):
                        mm(pb[:, :], h_sb[:, kc, ts * 128:(ts + 1) * 128],
                           whalf[sl][:, kc, :], kc == 0, kc == 7,
                           P if kc == 7 else None)
                    wt("vector", P, P.n)
                    if is_g:
                        evac_v.append(ve(lambda e, ts=ts, h2=h2, pb=pb:
                                         e.tensor_copy(
                                             g_sb[:, ts, h2 * 512:(h2 + 1) * 512],
                                             pb[:, :]), V))
                    else:
                        evac_v.append(ve(lambda e, ts=ts, h2=h2, pb=pb:
                                         e.tensor_copy(
                                             v_sb[:, ts, h2 * 512:(h2 + 1) * 512],
                                             pb[:, :]), V))
                wdone_h(sl)

        fm_proj_h(iq, q_sb)
        q_done = evac_v[-1]
        wt("scalar", V, q_done)
        se(lambda e: e.activation(out=q_sb[:, :, :], in_=q_sb[:, :, :],
                                  func=AF.Silu))
        ik = [wload_h(wkT, h2, 2 + h2) for h2 in range(2)]
        fm_proj_h(ik, k_sb)
        k_done = evac_v[-1]
        wt("scalar", V, k_done)
        silu_qk = se(lambda e: e.activation(out=k_sb[:, :, :], in_=k_sb[:, :, :],
                                            func=AF.Silu), S)
        if_ = [wload_h(wfT, h2, 2 + h2) for h2 in range(2)]
        tm_proj_h(if_, True)
        g_done = evac_v[-1]
        iv = [wload_h(wvT, h2, 2 + h2) for h2 in range(2)]
        scr = ((sA, z_sb), (p_sb, gd_sb[:, 0, :]))
        lsig_v = [0, 0, 0, 0]
        for ts in range(4):
            a_s, b_s = scr[ts % 2]
            wt("scalar", V, g_done if ts < 2 else lsig_v[ts - 2])
            se(lambda e, ts=ts, a_s=a_s: e.activation(
                out=a_s[:, :], in_=g_sb[:, ts, :], func=AF.Abs))
            sdrain()
            se(lambda e, a_s=a_s: e.activation(out=a_s[:, :], in_=a_s[:, :],
                                               func=AF.Exp, scale=-1.0))
            sdrain()
            v3 = se(lambda e, a_s=a_s: e.activation(out=a_s[:, :], in_=a_s[:, :],
                                                    func=AF.Ln, bias=1.0), S)
            wt("vector", S, v3)
            ve(lambda e, ts=ts, b_s=b_s: e.tensor_scalar_min(
                b_s[:, :], g_sb[:, ts, :], 0.0))
            lsig_v[ts] = ve(lambda e, ts=ts, a_s=a_s, b_s=b_s: e.tensor_sub(
                g_sb[:, ts, :], b_s[:, :], a_s[:, :]), V)
        gates_done = lsig_v[3]
        st_front[s] = dict(gates_done=gates_done)
        emit_split(s, 0)
        tm_proj_h(iv, False)
        projs_done = evac_v[-1]
        wt("scalar", V, projs_done)
        silu_done = se(lambda e: e.activation(out=v_sb[:, :, :], in_=v_sb[:, :, :],
                                              func=AF.Silu), S)
        st_front[s] = dict(silu_done=silu_done, silu_qk=silu_qk,
                           gates_done=gates_done, sr=sr, h_done=h_done)

    def emit_split(s, ci):
        gates_done = st_front[s]["gates_done"]
        ts0, ts1 = 2 * ci, 2 * ci + 1
        wt("scalar", V, gates_done)
        wt("scalar", P, scan_state["gl_p"])
        se(lambda e, ts=ts0: e.copy(out=ghi[:, 0, :], in_=g_sb[:, ts, :]))
        ch_s = se(lambda e, ts=ts1: e.copy(out=ghi[:, 1, :],
                                           in_=g_sb[:, ts, :]), S)
        wt("vector", S, ch_s)
        ve(lambda e, ts=ts0: e.tensor_sub(p_sb[:, :], g_sb[:, ts, :],
                                          ghi[:, 0, :]))
        sub_v = ve(lambda e, ts=ts1: e.tensor_sub(z_sb[:, :], g_sb[:, ts, :],
                                                  ghi[:, 1, :]), V)
        wt("scalar", V, sub_v)
        se(lambda e: e.copy(out=glo[:, 0, :], in_=p_sb[:, :]))
        cl_s = se(lambda e: e.copy(out=glo[:, 1, :], in_=z_sb[:, :]), S)
        scan_state["split"][(s, ci)] = (sub_v, cl_s)

    def emit_scan(s):
        fr = st_front[s]
        silu_done, gates_done = fr["silu_done"], fr["gates_done"]
        silu_qk = fr["silu_qk"]
        prev_chunk_pe = 0
        scan_done_v = gates_done
        for ci in range(NCHS):
            ts0, ts1 = 2 * ci, 2 * ci + 1
            wt("vector", P, prev_chunk_pe)
            wt("vector", V, gates_done)
            if (s, ci) not in scan_state["split"]:
                emit_split(s, ci)
            sub_v, cl_s = scan_state["split"][(s, ci)]
            wt("tensor", V, sub_v)
            wt("tensor", S, cl_s)
            for n in range(2):
                pb = (PA, PB)[n]
                mm(pb[:, :], utri[:, :], ghi[:, 0, n * 512:(n + 1) * 512], True,
                   False)
                mm(pb[:, :], utri[:, :], glo[:, 0, n * 512:(n + 1) * 512], False,
                   True)
            for n in range(2):
                pb = (PF, PU)[n]
                mm(pb[0:1, 0:512], onescol_bf[:, :],
                   ghi[:, 0, n * 512:(n + 1) * 512], True, False)
                mm(pb[0:1, 0:512], onescol_bf[:, :],
                   glo[:, 0, n * 512:(n + 1) * 512], False, True,
                   P if n == 1 else None)
            g0_p = P.n
            wt("vector", P, g0_p)
            ve(lambda e: e.tensor_copy(gc_sb[:, 0, 0:512], PA[:, :]))
            ve(lambda e: e.tensor_copy(gc_sb[:, 0, 512:1024], PB[:, :]))
            ve(lambda e: e.tensor_copy(g0r[0:1, 0:512], PF[0:1, 0:512]))
            g0_v = ve(lambda e: e.tensor_copy(g0r[0:1, 512:1024], PU[0:1, 0:512]),
                      V)
            wt("tensor", V, g0_v)
            for n in range(2):
                pb = (PC, PD)[n]
                mm(pb[:, :], utri[:, :], ghi[:, 1, n * 512:(n + 1) * 512], True,
                   False)
                mm(pb[:, :], utri[:, :], glo[:, 1, n * 512:(n + 1) * 512], False,
                   False)
                mm(pb[:, :], ones1f[:, :], g0r[0:1, n * 512:(n + 1) * 512], False,
                   True)
            for n in range(2):
                pb = (PF, PU)[n]
                mm(pb[0:1, 0:512], onescol_bf[:, :],
                   ghi[:, 1, n * 512:(n + 1) * 512], False, False)
                mm(pb[0:1, 0:512], onescol_bf[:, :],
                   glo[:, 1, n * 512:(n + 1) * 512], False, True,
                   P if n == 1 else None)
            gt_p = P.n
            wt("vector", P, gt_p)
            ve(lambda e: e.tensor_copy(gc_sb[:, 1, 0:512], PC[:, :]))
            ve(lambda e: e.tensor_copy(gc_sb[:, 1, 512:1024], PD[:, :]))
            ve(lambda e: e.tensor_copy(gtotr[0:1, 0:512], PF[0:1, 0:512]))
            gt_v = ve(lambda e: e.tensor_copy(gtotr[0:1, 512:1024],
                                              PU[0:1, 0:512]), V)
            wt("scalar", V, gt_v)
            eg_s = se(lambda e: e.activation(out=egtot[:, :], in_=gtotr[:, :],
                                             func=AF.Exp, scale=SC16), S)
            wt("tensor", V, gt_v)
            for mc in range(8):
                mm(PU[:, 8 + mc:9 + mc], ghi[:, 0, mc * 128:(mc + 1) * 128],
                   onescol_bf[:, :], True, False)
                mm(PU[:, 8 + mc:9 + mc], glo[:, 0, mc * 128:(mc + 1) * 128],
                   onescol_bf[:, :], False, False)
                mm(PU[:, 8 + mc:9 + mc], ghi[:, 1, mc * 128:(mc + 1) * 128],
                   onescol_bf[:, :], False, False)
                mm(PU[:, 8 + mc:9 + mc], glo[:, 1, mc * 128:(mc + 1) * 128],
                   onescol_bf[:, :], False, True, P if mc == 7 else None)
            lamc_p = P.n
            scan_state["gl_p"] = lamc_p
            wt("scalar", P, lamc_p)
            lamc_s = se(lambda e: e.activation(out=lamc[:, :], in_=PU[:, 8:16],
                                               func=AF.Exp, scale=SC16), S)
            sdrain()
            mm(PE_[:, :], ones1f[:, :], gtotr[0:1, 0:512], True, True)
            mm(PF[:, :], ones1f[:, :], gtotr[0:1, 512:1024], True, True, P)
            bcg_p = P.n

            def tr_grp(grp):
                for sl in range(8):
                    gi = grp * 8 + sl
                    if gi < 16:
                        i, msl = gi // 8, gi % 8
                        tr(PT[:, sl * 128:(sl + 1) * 128],
                           st_sb[:, i, msl * 128:(msl + 1) * 128], id_bf[:, :],
                           P if sl == 7 else None)
                    else:
                        i, dsl = (gi - 16) // 8, (gi - 16) % 8
                        ts = 2 * ci + i
                        tr(PT[:, sl * 128:(sl + 1) * 128],
                           k_sb[:, dsl, ts * 128:ts * 128 + 128], id_bf[:, :],
                           P if sl == 7 else None)
                wt("vector", P, P.n)
                if grp < 2:
                    return ve(lambda e, grp=grp: e.tensor_copy(
                        stf[:, grp * 8:(grp + 1) * 8, :]
                        .rearrange("p a b -> p (a b)"), PT[:, :]), V)
                return ve(lambda e, grp=grp: e.tensor_copy(
                    ktm[:, (grp - 2) * 8:(grp - 1) * 8, :]
                    .rearrange("p a b -> p (a b)"), PT[:, :]), V)

            wt("tensor", S, silu_qk)
            ktv2 = tr_grp(2)
            wt("tensor", V, ktv2)
            ktv = tr_grp(3)
            wt("tensor", S, eg_s)
            mm(PC[:, :], ones1f[:, :], egtot[0:1, 0:512], True, True)
            mm(PD[:, :], ones1f[:, :], egtot[0:1, 512:1024], True, True, P)
            bce_p = P.n
            wt("vector", SQ, scan_state["y1q"])
            st_vs = [0, 0]
            for i, ts in enumerate((ts0, ts1)):
                wt("vector", P, bcg_p)
                ve(lambda e, i=i: e.tensor_tensor(gd_sb[:, i, 0:512],
                                                  gc_sb[:, i, 0:512],
                                                  PE_[:, :], ALU.subtract))
                gd_v = ve(lambda e, i=i: e.tensor_tensor(gd_sb[:, i, 512:1024],
                                                         gc_sb[:, i, 512:1024],
                                                         PF[:, :], ALU.subtract),
                          V)
                wt("scalar", V, gd_v)
                se(lambda e, ts=ts: e.activation(out=p_sb[:, :],
                                                 in_=g_sb[:, ts, :],
                                                 func=AF.Exp, scale=SC16))
                es_s = se(lambda e, i=i: e.activation(out=e1t[:, :],
                                                      in_=gd_sb[:, i, :],
                                                      func=AF.Exp, scale=-SC16),
                          S)
                wt("vector", S, es_s)
                ve(lambda e: e.tensor_scalar(z_sb[:, :], p_sb[:, :], -1.0, 1.0,
                                             ALU.mult, ALU.add))
                st_vs[i] = ve(lambda e, i=i: e.tensor_tensor(st_sb[:, i, :],
                                                             z_sb[:, :],
                                                             e1t[:, :],
                                                             ALU.mult), V)
            st_v = st_vs[1]
            se(lambda e: e.activation(out=gd_sb[:, 0, :], in_=gd_sb[:, 0, :],
                                      func=AF.Exp, scale=SC16))
            ept_s = se(lambda e: e.activation(out=gd_sb[:, 1, :],
                                              in_=gd_sb[:, 1, :],
                                              func=AF.Exp, scale=SC16), S)

            twins = (slice(ts0 * 128, ts0 * 128 + 128),
                     slice(ts1 * 128, ts1 * 128 + 128))
            epts = (gd_sb[:, 0, :], gd_sb[:, 1, :])
            a_slots = ((PU, PU), (PE_, PF))
            z_slots = ((PA, PB), (PC, PD))

            def a_mms(i):
                twin = twins[i]
                for jj in range(i + 1):
                    pslot = a_slots[i][jj]
                    for hh in range(H):
                        mm(pslot[:, hh * 128:hh * 128 + 128],
                           k_sb[:, 2 * hh, 2 * ci * 128 + jj * 128:
                                2 * ci * 128 + jj * 128 + 128],
                           q_sb[:, 2 * hh, twin], True, False)
                        mm(pslot[:, hh * 128:hh * 128 + 128],
                           k_sb[:, 2 * hh + 1, 2 * ci * 128 + jj * 128:
                                2 * ci * 128 + jj * 128 + 128],
                           q_sb[:, 2 * hh + 1, twin], False, True,
                           P if hh == H - 1 else None)
                return P.n

            def mask_evac(i, dst_sb, slots):
                e_s = 0
                for jj in range(i + 1):
                    pslot = slots[i][jj]
                    dst3 = dst_sb[:, jj * 4:jj * 4 + 4, :]
                    src3 = pslot[:, :].rearrange("p (h t) -> p h t", h=H)
                    if jj == i:
                        ve(lambda e, dst3=dst3, src3=src3: e.tensor_tensor(
                            dst3, src3,
                            utri_f[:, :].unsqueeze(1)
                            .broadcast_to([128, H, 128]), ALU.mult), V)
                    else:
                        e_s = se(lambda e, dst3=dst3, src3=src3:
                                 e.copy(out=dst3, in_=src3), S)
                return V.n, e_s

            def z_mms(i):
                twin = twins[i]
                for hh in range(H):
                    pz = z_slots[i][hh // 2]
                    col = (hh % 2) * 256
                    for jj in range(i + 1):
                        slot = hh if i == 0 else jj * 4 + hh
                        mm(pz[:, col:col + 256], am_sb[:, slot, :],
                           st_sb[:, jj, hh * 256:(hh + 1) * 256], jj == 0,
                           False)
                for hh in range(H):
                    pz = z_slots[i][hh // 2]
                    col = (hh % 2) * 256
                    mm(pz[:, col:col + 256], q_sb[:, 2 * hh, twin],
                       hkb[:, 2 * hh, :], False, False)
                    mm(pz[:, col:col + 256], q_sb[:, 2 * hh + 1, twin],
                       hkb[:, 2 * hh + 1, :], False, True,
                       P if hh == H - 1 else None)
                return P.n

            def sm_max(i, z_p):
                pz0, pz1 = z_slots[i]
                wt("vector", P, z_p)
                ve(lambda e, i=i, pz0=pz0: e.tensor_tensor(
                    z_sb[:, 0:512], epts[i][:, 0:512], pz0[:, :], ALU.mult))
                ve(lambda e, i=i, pz1=pz1: e.tensor_tensor(
                    z_sb[:, 512:1024], epts[i][:, 512:1024], pz1[:, :],
                    ALU.mult), V)
                ve(lambda e: e.reduce_max(
                    mx4[:, :], z_sb[:, :].rearrange("p (h m) -> p h m", h=H),
                    axis=AX.X))
                vdrain()
                return ve(lambda e: e.tensor_scalar_mul(nmx4[:, :], mx4[:, :],
                                                        -1.0), V)

            def sm_exp(nm_v):
                wt("scalar", V, nm_v)
                for hh2 in range(H):
                    se(lambda e, hh2=hh2: e.activation(
                        out=p_sb[:, hh2 * 256:(hh2 + 1) * 256],
                        in_=z_sb[:, hh2 * 256:(hh2 + 1) * 256], func=AF.Exp,
                        bias=nmx4[:, hh2:hh2 + 1],
                        accum_out=sum4[:, hh2:hh2 + 1]),
                       S if hh2 == H - 1 else None)
                return S.n

            def p2_chain(i, px_s):
                wt("vector", S, px_s)
                ve(lambda e: e.reciprocal(rden[:, :], sum4[:, :]))
                vdrain()
                ve(lambda e, i=i: e.tensor_tensor(z_sb[:, :], p_sb[:, :],
                                                  epts[i][:, :], ALU.mult))
                return ve(lambda e: e.tensor_tensor(
                    p2t[:, :].rearrange("p (h m) -> p h m", h=H),
                    z_sb[:, :].rearrange("p (h m) -> p h m", h=H),
                    rden[:, :].unsqueeze(2).broadcast_to([128, H, 256]),
                    ALU.mult), V)

            def p2_tr(pt_v):
                wt("tensor", V, pt_v)
                for sl in range(8):
                    tr(PT[:, sl * 128:(sl + 1) * 128],
                       p2t[:, sl * 128:(sl + 1) * 128], id_bf[:, :],
                       P if sl == 7 else None)
                wt("vector", P, P.n)
                return ve(lambda e: e.tensor_copy(
                    p2f[:, :, :].rearrange("p a b -> p (a b)"), PT[:, :]), V)

            def b_mms(i, tv2):
                wt("tensor", V, tv2)
                for jj in range(i + 1):
                    pslot = a_slots[i][jj]
                    for hh in range(H):
                        mm(pslot[:, hh * 128:hh * 128 + 128],
                           stf[:, jj * 8 + 2 * hh, :], p2f[:, 2 * hh, :],
                           True, False)
                        mm(pslot[:, hh * 128:hh * 128 + 128],
                           stf[:, jj * 8 + 2 * hh + 1, :],
                           p2f[:, 2 * hh + 1, :], False, True,
                           P if hh == H - 1 else None)
                return P.n

            def o_mms(i, bt_v, bt_s):
                wt("tensor", V, bt_v)
                if bt_s:
                    wt("tensor", S, bt_s)
                if i == 0:
                    wt("tensor", S, hvb_s)
                    wt("tensor", S, silu_done)
                for hh in range(H):
                    po = z_slots[i][hh // 2]
                    for vc in range(2):
                        col = (hh % 2) * 256 + vc * 128
                        mm(po[:, col:col + 128],
                           hvb[:, 2 * hh, vc * 128:vc * 128 + 128],
                           p2f[:, 2 * hh, :], True, False)
                        mm(po[:, col:col + 128],
                           hvb[:, 2 * hh + 1, vc * 128:vc * 128 + 128],
                           p2f[:, 2 * hh + 1, :], False, False)
                        for jj in range(i + 1):
                            slot = hh if i == 0 else jj * 4 + hh
                            mm(po[:, col:col + 128],
                               v_sb[:, 2 * ci + jj,
                                    hh * 256 + vc * 128:hh * 256 + vc * 128
                                    + 128],
                               bt_sb[:, slot, :], False, jj == i,
                               P if (hh == H - 1 and vc == 1 and jj == i)
                               else None)
                return P.n

            def o_evac(i, o_p):
                ts = (ts0, ts1)[i]
                po0, po1 = z_slots[i]
                wt("vector", P, o_p)
                wt("scalar", P, o_p)
                ov = ve(lambda e, ts=ts, po0=po0: e.tensor_copy(
                    o_sb[:, 0:4, ts * 128:ts * 128 + 128],
                    po0[:, :].rearrange("p (a t) -> p a t", a=4)), V)
                os_ = se(lambda e, ts=ts, po1=po1: e.copy(
                    out=o_sb[:, 4:8, ts * 128:ts * 128 + 128],
                    in_=po1[:, :].rearrange("p (a t) -> p a t", a=4)), S)
                return ov, os_

            wt("tensor", S, lamc_s)
            a0_p = a_mms(0)
            wt("tensor", V, st_vs[0])
            wt("tensor", V, ktv)
            stv0 = tr_grp(0)
            wt("vector", P, a0_p)
            am0_v, _ = mask_evac(0, am_sb, a_slots)
            wt("vector", P, bce_p)
            for hh in range(H):
                pb = (PC, PD)[hh // 2]
                col = (hh % 2) * 256
                ve(lambda e, hh=hh, pb=pb, col=col: e.tensor_tensor(
                    hk_sb[:, 2 * hh:2 * hh + 2, :],
                    hk_sb[:, 2 * hh:2 * hh + 2, :],
                    pb[:, col:col + 256].unsqueeze(1).broadcast_to([128, 2, 256]),
                    ALU.mult), V if hh == H - 1 else None)
            hkscale_v = V.n
            wt("scalar", V, hkscale_v)
            hkb_s = se(lambda e: e.copy(out=hkb[:, :, :], in_=hk_sb[:, :, :]), S)
            wt("scalar", V, scan_state["hv_add_v"])
            for sl in range(8):
                se(lambda e, sl=sl: e.activation(
                    out=hv_sb[:, sl, :], in_=hv_sb[:, sl, :], func=AF.Identity,
                    scale=lamc[:, sl:sl + 1]))
            hvb_s = se(lambda e: e.copy(out=hvb[:, :, :], in_=hv_sb[:, :, :]), S)
            wt("tensor", V, st_vs[1])
            wt("tensor", V, stv0)
            tr_grp(1)
            wt("tensor", V, am0_v)
            wt("tensor", S, hkb_s)
            z0_p = z_mms(0)
            a1_p = a_mms(1)
            wt("vector", S, ept_s)
            nm0 = sm_max(0, z0_p)
            wt("vector", P, a1_p)
            wt("scalar", P, a1_p)
            am1_v, am1_s = mask_evac(1, am_sb, a_slots)
            px0 = sm_exp(nm0)
            wt("tensor", V, am1_v)
            wt("tensor", S, am1_s)
            z1_p = z_mms(1)
            pt0 = p2_chain(0, px0)
            tv20 = p2_tr(pt0)
            nm1 = sm_max(1, z1_p)
            px1 = sm_exp(nm1)
            b0_p = b_mms(0, tv20)
            wt("vector", P, b0_p)
            bt0_v, _ = mask_evac(0, bt_sb, a_slots)
            o0_p = o_mms(0, bt0_v, 0)
            o_v, o_s = o_evac(0, o0_p)
            pt1 = p2_chain(1, px1)
            tv21 = p2_tr(pt1)
            b1_p = b_mms(1, tv21)
            wt("vector", P, b1_p)
            wt("scalar", P, b1_p)
            bt1_v, bt1_s = mask_evac(1, bt_sb, a_slots)
            o1_p = o_mms(1, bt1_v, bt1_s)
            o_v, o_s = o_evac(1, o1_p)

            wt("tensor", V, o_v)
            wt("tensor", S, o_s)
            last_chunk = (s == NSC - 1 and ci == NCHS - 1)
            if not last_chunk:
                for hh in range(H):
                    phk = (PA, PB, PC, PD)[hh]
                    for dc in range(2):
                        for jj in range(2):
                            mm(phk[:, dc * 256:dc * 256 + 256],
                               ktm[:, jj * 8 + 2 * hh + dc, :],
                               st_sb[:, jj, hh * 256:(hh + 1) * 256], jj == 0,
                               jj == 1,
                               P if (hh == H - 1 and dc == 1 and jj == 1)
                               else None)
                hk_p = P.n
                wt("vector", P, hk_p)
                for hh in range(H):
                    phk = (PA, PB, PC, PD)[hh]
                    ve(lambda e, hh=hh, phk=phk: e.tensor_tensor(
                        hk_sb[:, 2 * hh:2 * hh + 2, :]
                        .rearrange("p a b -> p (a b)"),
                        hk_sb[:, 2 * hh:2 * hh + 2, :]
                        .rearrange("p a b -> p (a b)"),
                        phk[:, :], ALU.add), V if hh == H - 1 else None)
                hk_v = V.n
                wt("tensor", V, hk_v)
                for hh in range(H):
                    phv = (PA, PB, PC, PD)[hh]
                    for mc in range(2):
                        for jj in range(2):
                            mm(phv[:, mc * 256:mc * 256 + 256],
                               st_sb[:, jj,
                                     hh * 256 + mc * 128:hh * 256 + mc * 128
                                     + 128],
                               v_sb[:, 2 * ci + jj, hh * 256:(hh + 1) * 256],
                               jj == 0, jj == 1,
                               P if (hh == H - 1 and mc == 1 and jj == 1)
                               else None)
                hv_p = P.n
                wt("vector", P, hv_p)
                wt("vector", S, hvb_s)
                for hh in range(H):
                    phv = (PA, PB, PC, PD)[hh]
                    ve(lambda e, hh=hh, phv=phv: e.tensor_tensor(
                        hv_sb[:, 2 * hh:2 * hh + 2, :]
                        .rearrange("p a b -> p (a b)"),
                        hv_sb[:, 2 * hh:2 * hh + 2, :]
                        .rearrange("p a b -> p (a b)"),
                        phv[:, :], ALU.add), V if hh == H - 1 else None)
                hv_v = V.n
                scan_state["hv_add_v"] = hv_v
                prev_chunk_pe = P.n
                scan_done_v = hv_v
            else:
                prev_chunk_pe = P.n
                scan_done_v = o_v
        wt("scalar", V, scan_done_v)
        sil_c = [0] * 8
        for c in range(8):
            sil_c[c] = se(lambda e, c=c: e.activation(
                out=o_sb[:, c, :], in_=o_sb[:, c, :], func=AF.Silu), S)
        osilu_s = sil_c[7]
        sq_mm_p = [0] * 8
        for c in range(8):
            wt("vector", S, sil_c[c])
            if c >= 2:
                wt("vector", P, sq_mm_p[c - 2])
            vsq2 = ve(lambda e, c=c: e.tensor_mul(xsq2[:, c % 2, :], o_sb[:, c, :],
                                                  o_sb[:, c, :]), V)
            wt("tensor", V, vsq2)
            sq_mm_p[c] = mm(PF[0:1, 0:512], invd_bf[:, :], xsq2[:, c % 2, :],
                            c == 0, c == 7, P)
        wt("scalar", P, P.n)
        se(lambda e: e.activation(out=rr1[:, :], in_=PF[0:1, 0:512], func=AF.Ln,
                                  bias=EPS))
        sdrain()
        sr2 = se(lambda e: e.activation(out=rr2[:, :], in_=rr1[:, :], func=AF.Exp,
                                        scale=-0.5), S)
        wt("tensor", S, sr2)
        bc2_ = mm(PE_[:, :], ones1f[:, :], rr2[:, :], True, True, P)
        wt("vector", P, bc2_)
        onv = 0
        for c in range(8):
            onv = ve(lambda e, c=c: e.scalar_tensor_tensor(
                o_sb[:, c, :], o_sb[:, c, :], gn_sb[:, c:c + 1], PE_[:, :],
                ALU.mult, ALU.mult), V)
        st_scan[s] = dict(scan_done_v=scan_done_v, on_v=onv)

    def emit_tail(s, wo_g):
        fr, sc_ = st_front[s], st_scan[s]
        wt("tensor", V, sc_["on_v"])
        wt("tensor", G, wo_g)
        if s + 1 in st_front:
            wt("tensor", S, st_front[s + 1]["sr"])
            wt("sync", V, st_front[s + 1]["h_done"])
        wt("sync", V, sc_["scan_done_v"])
        pre_done_v = 0
        pos_s_prev = 0
        for ts in range(4):
            tile = s * 4 + ts
            for nt in range(2):
                pb = (PA, PB)[nt]
                for kc in range(8):
                    mm(pb[:, :], o_sb[:, kc, ts * 128:(ts + 1) * 128],
                       whalf[nt][:, kc, :], kc == 0, kc == 7,
                       P if (kc == 7 and nt == 1) else None)
            at_p = P.n
            if ts > 0:
                wt("sync", V, pre_done_v)
            xl = sdma(x_tm[tile * 128:(tile + 1) * 128, :], xt_sb[:, :])
            wt("vector", P, at_p)
            wt("vector", L, xl)
            ve(lambda e: e.tensor_tensor(pre_sb[:, 0:512], xt_sb[:, 0:512],
                                         PA[:, :], ALU.add))
            pre_done_v = ve(lambda e: e.tensor_tensor(pre_sb[:, 512:1024],
                                                      xt_sb[:, 512:1024],
                                                      PB[:, :], ALU.add), V)
            wt("scalar", V, pre_done_v)
            se(lambda e: e.activation(out=sA[:, :], in_=pre_sb[:, :],
                                      func=AF.Identity, accum_out=mraw[:, :]))
            se(lambda e: e.activation(out=sA[:, :], in_=pre_sb[:, :],
                                      func=AF.Square, accum_out=vraw[:, :]))
            sdrain()
            se(lambda e: e.activation(out=tmp1c[:, :], in_=mraw[:, :],
                                      func=AF.Square, scale=1.0 / D))
            mn_s = se(lambda e: e.activation(out=mean1[:, :], in_=mraw[:, :],
                                             func=AF.Identity, scale=1.0 / D), S)
            wt("vector", S, mn_s)
            v2 = ve(lambda e: e.tensor_scalar(var1[:, :], vraw[:, :], 1.0 / D,
                                              tmp1c[:, :], ALU.mult,
                                              ALU.subtract), V)
            wt("scalar", V, v2)
            se(lambda e: e.activation(out=rstd1[:, :], in_=var1[:, :], func=AF.Ln,
                                      bias=EPS))
            sdrain()
            rs_s = se(lambda e: e.activation(out=rstd1[:, :], in_=rstd1[:, :],
                                             func=AF.Exp, scale=-0.5), S)
            wt("vector", S, rs_s)
            if ts > 0 or s > 0:
                wt("vector", SQ, SQ.n)
            y1v = ve(lambda e: e.tensor_scalar(
                y1t_sb[:, :], pre_sb[:, :], mean1[:, :], rstd1[:, :],
                ALU.subtract, ALU.mult), V)
            wt("scalar", V, y1v)
            qdma(y1t_sb[:, :], y1_dram[tile * 128:(tile + 1) * 128, :])
            wt("tensor", V, y1v)
            for half in range(2):
                for sl in range(4):
                    tr(PU[:, sl * 128:(sl + 1) * 128],
                       y1t_sb[:, half * 512 + sl * 128:half * 512 + (sl + 1)
                             * 128], id_f[:, :], P if sl == 3 else None)
                if half == 0:
                    wt("scalar", P, P.n)
                    tvh_s = se(lambda e: e.copy(
                        out=y1tf_sb[:, 0:4, :].rearrange("p a b -> p (a b)"),
                        in_=PU[:, :]), S)
                    wt("tensor", S, tvh_s)
                else:
                    wt("vector", P, P.n)
                    tvh = ve(lambda e: e.tensor_copy(
                        y1tf_sb[:, 4:8, :].rearrange("p a b -> p (a b)"),
                        PU[:, :]), V)
                    wt("tensor", V, tvh)
            if pos_s_prev:
                wt("tensor", S, pos_s_prev)
            for kc in range(8):
                mm(PF[:, 0:8], y1tf_sb[:, kc, :], wr_sb[:, kc, :],
                   kc == 0, kc == 7, P if kc == 7 else None)
            lg_p = P.n
            wt("scalar", P, lg_p)
            lg_s = se(lambda e: e.copy(out=lg_sb[:, :], in_=PF[:, 0:8]), S)
            wt("vector", S, lg_s)
            ve(lambda e: e.max(mx8[:, :], lg_sb[:, :]))
            vdrain()
            ve(lambda e: e.max_index(idxu[:, :], mx8[:, :], lg_sb[:, :]))
            vdrain()
            ve(lambda e, tile=tile: e.tensor_copy(idx_all[:, tile:tile + 1],
                                                  idxu[:, 0:1]))
            nm_v = ve(lambda e: e.tensor_scalar_mul(nmx[:, :], mx8[:, 0:1], -1.0),
                      V)
            wt("scalar", V, nm_v)
            px_s = se(lambda e: e.activation(out=pex[:, :], in_=lg_sb[:, :],
                                             func=AF.Exp, bias=nmx[:, :],
                                             accum_out=den8[:, :]), S)
            wt("vector", S, px_s)
            ve(lambda e, tile=tile: e.reciprocal(pmax_all[:, tile:tile + 1],
                                                 den8[:, :]))
            vdrain()
            oh_v = ve(lambda e, tile=tile: e.tensor_scalar(
                oh_sb[:, :], iota8[:, :], idx_all[:, tile:tile + 1], None,
                ALU.is_equal), V)
            wt("tensor", V, oh_v)
            mm(PF[:, 8:16], ustr[:, :], oh_sb[:, :], True, False)
            mm(PF[:, 8:16], ones1f[:, :], carry[0:1, :], False, True)
            cs_p = mm(PF[0:1, 16:24], onescol_f[:, :], oh_sb[:, :], True, True,
                      P)
            wt("scalar", P, cs_p)
            pos_s_prev = se(lambda e: e.copy(out=posx[:, :], in_=PF[:, 8:16]), S)
            wt("vector", S, pos_s_prev)
            wt("vector", P, cs_p)
            ve(lambda e: e.tensor_mul(tmp8[:, :], oh_sb[:, :], posx[:, :]))
            vdrain()
            ve(lambda e, tile=tile: e.reduce_sum(lp_all[:, tile:tile + 1],
                                                 tmp8[:, :], axis=AX.X))
            vdrain()
            cu_v = ve(lambda e: e.tensor_tensor(carry[0:1, :], carry[0:1, :],
                                                PF[0:1, 16:24], ALU.add), V)
            wt("tensor", V, cu_v)
        scan_state["y1q"] = SQ.n
        wdone_h(0)
        wdone_h(1)
        if s + 1 < NSC:
            return max(wload_h(woT, h2, h2)[1] for h2 in range(2))
        return 0

    wo_g = max(wload_h(woT, h2, h2)[1] for h2 in range(2))
    emit_front(0)
    for s in range(NSC):
        emit_scan(s)
        if s + 1 < NSC:
            emit_front(s + 1)
        wo_g = emit_tail(s, wo_g)
    prev_sc_v = V.n
    prev_sc_pe = P.n

    def dump(ap, name, shape, dt=F32):
        d = dbg_out(name, shape, dt)
        if d is not None:
            wt("gpsimd", V, V.n)
            wt("gpsimd", S, S.n)
            wt("gpsimd", P, P.n)
            gdma(ap, d)

    dump(h_sb[:, :, :], "on", [128, 8, SC], BF16)
    dump(q_sb[:, :, :], "q", [128, 8, SC], BF16)
    dump(k_sb[:, :, :], "k", [128, 8, SC], BF16)
    dump(v_sb[:, :, :], "v", [128, 4, D], BF16)
    dump(g_sb[:, :, :], "g", [128, 4, D])
    dump(o_sb[:, :, :], "o", [128, 8, SC], BF16)
    dump(carry[:, :], "carry", [1, 8])
    dump(idx_all[:, :], "idx", [128, NT])
    dump(lp_all[:, :], "lp", [128, NT])
    dump(pmax_all[:, :], "pmax", [128, NT])
    dump(hk_sb[:, :, :], "hk", [128, 8, 256])
    dump(hv_sb[:, :, :], "hv", [128, 8, 256])
    dump(z_sb[:, :], "z", [128, D])
    dump(p_sb[:, :], "p", [128, D])
    dump(p1t[:, :], "p1t", [128, D], BF16)
    dump(p2t[:, :], "p2t", [128, D], BF16)
    dump(am_sb[:, :, :], "am", [128, 8, 128], BF16)
    dump(bt_sb[:, :, :], "bt", [128, 8, 128], BF16)
    dump(stf[:, :, :], "stf", [128, 16, 128], BF16)
    dump(ktm[:, :, :], "ktm", [128, 16, 128], BF16)
    dump(st_sb[:, :, :], "st", [128, 2, D], BF16)
    dump(gc_sb[:, :, :], "gc", [128, 2, D])
    dump(pre_sb[:, :], "pre", [128, D])
    dump(y1t_sb[:, :], "y1t", [128, D])
    dump(xt_sb[:, :], "xt", [128, D])
    dump(lg_sb[:, :], "lg", [128, 8])
    dump(mx8[:, :], "mx8", [128, 8])
    dump(y1tf_sb[:, :, :], "y1f", [128, 8, 128])
    dump(mean1[:, :], "mean1", [128, 1])
    dump(var1[:, :], "var1", [128, 1])

    if stage >= 4:
        arena = Arena(nc, [h_sb, q_sb, k_sb, v_sb, g_sb, gc_sb, gd_sb, st_sb,
                           z_sb, p_sb, p1t, p2t, stf, p1f, p2f, ktm, am_sb,
                           bt_sb, hk_sb, hv_sb, hkb, hvb, o_sb, xsq2,
                           whalf[0], whalf[1], whalf[2], whalf[3],
                           sA, rr1, rr2])
        w1p = [arena.at(f"w1_{i}", [128, 2, DFF], BF16) for i in range(4)]
        w2p = [arena.at(f"w2_{i}", [128, 8, D], BF16) for i in range(4)]
        h1p = [arena.at(f"h1_{i}", [128, 16, 256], BF16) for i in range(2)]
        dgat = arena.at("dgat", [128, 2, D], BF16)
        dfm = arena.at("dfm", [128, 8, 256], BF16)
        eo_sb = arena.at("eo_sb", [128, 2, D], BF16)
        eo32 = arena.at("eo32", [128, D], F32)
        comb = arena.at("comb", [128, D], BF16)
        cnt_sb = arena.at("cnt_sb", [8, 8], F32)
        cntT = arena.at("cntT", [8, 8], F32)
        offsT = arena.at("offsT", [8, 8], F32)
        cstT = arena.at("cstT", [8, 8], F32)
        ccol = arena.at("ccol", [8, 1], F32)
        cst_row = arena.at("cst_row", [1, 8], F32)
        exoffs = arena.at("exoffs", [1, 8], F32)
        excnt = arena.at("excnt", [1, 8], F32)
        hi_row = arena.at("hi_row", [1, 8], F32)
        base_row = arena.at("base_row", [1, 8], F32)
        lo_bc = arena.at("lo_bc", [128, 8], F32)
        hi_bc = arena.at("hi_bc", [128, 8], F32)
        base_bc = arena.at("base_bc", [128, 8], F32)
        obc = arena.at("obc", [128, 8], F32)
        m160 = arena.at("m160", [128, 160], F32)
        v160 = arena.at("v160", [128, 160], F32)
        srcg = arena.at("srcg", [128, 20], F32)
        srcgu = arena.at("srcgu", [128, 20], U32)
        sru = arena.at("sru", [128, NT], U32)
        dval = arena.at("dval", [128, 1], F32)
        offs_row = arena.at("offs_row", [1, 8], F32)
        kcol = arena.at("kcol", [128, 1], F32)
        didx = arena.at("didx", [128, 1], U32)
        y1bf = arena.at("y1bf", [128, D], BF16)
        cstb = arena.at("cstb", [128, 8], F32)
        sr_all = arena.at("sr_all", [128, NT], F32)
        f2_all = arena.at("f2_all", [128, NT], F32)
        srb_all = arena.at("srb_all", [128, NT], F32)
        srub = arena.at("srub", [128, NT], U32)
        kca = arena.at("kca", [128, 1], F32)
        kcb = arena.at("kcb", [128, 1], F32)

        wt("scalar", V, V.n)
        cq = qdma(carry[0:1, :], cnt_in_d[0:1, :])
        wt("gpsimd", SQ, cq)
        cc1 = ctx.op("gpsimd", lambda e: e.collective_compute(
            "AllGather", ALU.bypass, replica_groups=[list(range(8))],
            ins=[cnt_in_d[:, :].opt()], outs=[cnt_ag_d[:, :].opt()]), CC)

        wt("tensor", V, V.n)
        tr(PU[0:8, 0:1], carry[0:1, :], id_f[0:1, 0:1], P)
        wt("vector", P, P.n)
        ve(lambda e: e.tensor_copy(ccol[:, :], PU[0:8, 0:1]), V)
        vdrain()
        wt("tensor", V, V.n)
        mm(PU[0:1, 8:16], ccol[:, :], str8[:, :], True, True, P)
        wt("vector", P, P.n)
        ve(lambda e: e.tensor_copy(cst_row[:, :], PU[0:1, 8:16]), V)
        vdrain()
        wt("tensor", V, V.n)
        mm(PU[:, 16:24], ones1f[:, :], cst_row[0:1, :], True, True, P)
        wt("vector", P, P.n)
        ve(lambda e: e.tensor_copy(cstb[:, :], PU[:, 16:24]), V)
        vdrain()

        dall = arena.at("dall", [128, NT], F32)
        didx_all = arena.at("didx_all", [128, NT], U32)
        y1s2 = nc.alloc_sbuf_tensor_at("y1s2", [128, D], F32,
                                       offset=nc.lookup_mloc(w1p[0]).addr)
        y1bf2 = nc.alloc_sbuf_tensor_at("y1bf2", [128, D], BF16,
                                        offset=nc.lookup_mloc(w1p[0]).addr + 4096)
        y1s3 = nc.alloc_sbuf_tensor_at("y1s3", [128, D], F32,
                                       offset=nc.lookup_mloc(w1p[0]).addr + 6144)
        y1bf3 = nc.alloc_sbuf_tensor_at(
            "y1bf3", [128, D], BF16, offset=nc.lookup_mloc(w1p[0]).addr + 10240)
        y1s4 = nc.alloc_sbuf_tensor_at("y1s4", [128, D], F32,
                                       offset=nc.lookup_mloc(w1p[1]).addr)
        y1bf4 = nc.alloc_sbuf_tensor_at("y1bf4", [128, D], BF16,
                                        offset=nc.lookup_mloc(w1p[1]).addr + 4096)
        ybufs = (y1_sb, y1s2, y1s3, y1s4)
        bbufs = (y1bf, y1bf2, y1bf3, y1bf4)
        wt("sync", V, V.n)
        wt("sync", P, P.n)
        yls = []
        for tile in range(4):
            yls.append((sdma(y1_dram[tile * 128:(tile + 1) * 128, :],
                             ybufs[tile % 4][:, :]), "sync", L))
        for tile in range(NT):
            ve(lambda e, tile=tile: e.tensor_scalar(
                oh_sb[:, :], iota8[:, :], idx_all[:, tile:tile + 1], None,
                ALU.is_equal))
            vdrain()
            ve(lambda e: e.tensor_mul(tmp8[:, :], oh_sb[:, :], cstb[:, :]))
            vdrain()
            ve(lambda e, tile=tile: e.reduce_sum(dall[:, tile:tile + 1],
                                                 tmp8[:, :], axis=AX.X))
            vdrain()
            ve(lambda e, tile=tile: e.tensor_tensor(
                dall[:, tile:tile + 1], dall[:, tile:tile + 1],
                lp_all[:, tile:tile + 1], ALU.add))
            vdrain()
        ve(lambda e: e.tensor_copy(didx_all[:, :], dall[:, :]), V)
        vdrain()
        d_v = V.n
        scat_g = []
        for tile in range(NT):
            lsem, _, lq = yls[tile]
            wt("vector", lq, lsem)
            if len(scat_g) >= 4:
                wt("vector", G, scat_g[-4])
            yb = ve(lambda e, tile=tile: e.tensor_copy(bbufs[tile % 4][:, :],
                                                       ybufs[tile % 4][:, :]), V)
            wt("gpsimd", V, max(yb, d_v))
            sg = ctx.op("gpsimd", lambda e, tile=tile: e.indirect_dma_start(
                out=ysort_dram[:, :],
                out_offset=IndirectOffsetOnAxis(didx_all[:, tile:tile + 1], 0),
                in_=bbufs[tile % 4][:, :], in_offset=None,
                bounds_check=bcregs["t"], oob_is_err=False), G, 16)
            scat_g.append(sg)
            if tile + 4 < NT:
                eng = ("sync", "scalar")[tile % 2]
                wt(eng, V, yb)
                yls.append(((sdma, qdma)[tile % 2](
                    y1_dram[(tile + 4) * 128:(tile + 5) * 128, :],
                    ybufs[tile % 4][:, :]), eng, (L, SQ)[tile % 2]))
        wt("gpsimd", G, scat_g[-1])
        cc2 = ctx.op("gpsimd", lambda e: e.collective_compute(
            "AllGather", ALU.bypass, replica_groups=[list(range(8))],
            ins=[ysort_dram[:, :].opt()], outs=[ag1_d[:, :].opt()]), CC)


        wt("sync", CC, cc1)
        cl = sdma(cnt_ag_d[:, :], cnt_sb[:, :])

        for i in range(4):
            gw1 = gdma(w1e_in[i * 256:(i + 1) * 256, :]
                       .rearrange("(c p) f -> p c f", p=128), w1p[i][:, :, :])
        for i in range(4):
            gw2b = gdma(w2e_in[i * 1024:(i + 1) * 1024, :]
                        .rearrange("(c p) d -> p c d", p=128), w2p[i][:, :, :])

        wt("vector", L, cl)
        wt("tensor", L, cl)
        wt("tensor", V, V.n)
        mm(PU[0:1, 24:32], bmask[:, :], cnt_sb[:, :], True, True, P)
        wt("vector", P, P.n)
        ve(lambda e: e.tensor_copy(offs_row[:, :], PU[0:1, 24:32]), V)
        vdrain()
        wt("tensor", V, V.n)
        mm(PU[:, 32:40], ones1f[:, :], offs_row[0:1, :], True, True, P)
        wt("vector", P, P.n)
        ve(lambda e: e.tensor_copy(obc[:, :], PU[:, 32:40]), V)
        vdrain()
        for tile in range(NT):
            ve(lambda e, tile=tile: e.tensor_scalar(
                oh_sb[:, :], iota8[:, :], idx_all[:, tile:tile + 1], None,
                ALU.is_equal))
            vdrain()
            ve(lambda e: e.tensor_mul(tmp8[:, :], oh_sb[:, :], obc[:, :]))
            vdrain()
            ve(lambda e: e.reduce_sum(dval[:, :], tmp8[:, :], axis=AX.X))
            vdrain()
            ve(lambda e, tile=tile: e.tensor_tensor(
                dval[:, :], dval[:, :], lp_all[:, tile:tile + 1], ALU.add))
            vdrain()
            ve(lambda e: e.tensor_single_scalar(kcol[:, :], dval[:, :], float(CAP),
                                                ALU.is_lt))
            vdrain()
            ve(lambda e, tile=tile: e.tensor_scalar(
                f2_all[:, tile:tile + 1], kcol[:, :], -1.0, 2.0, ALU.mult, ALU.add))
            ve(lambda e, tile=tile: e.scalar_tensor_tensor(
                sr_all[:, tile:tile + 1], idx_all[:, tile:tile + 1], float(CAP),
                dval[:, :], ALU.mult, ALU.add))
            vdrain()
            ve(lambda e: e.tensor_scalar(kca[:, :], kcol[:, :], -BIG, BIG,
                                         ALU.mult, ALU.add))
            vdrain()
            ve(lambda e, tile=tile: e.tensor_tensor(
                sr_all[:, tile:tile + 1], sr_all[:, tile:tile + 1], kca[:, :],
                ALU.add), V)
            vdrain()
        ve(lambda e: e.tensor_copy(sru[:, :], sr_all[:, :]), V)
        vdrain()

        wt("tensor", V, V.n)
        tr(PU[0:8, 40:48], cnt_sb[:, :], id_f[0:8, 0:8], P)
        wt("vector", P, P.n)
        ve(lambda e: e.tensor_copy(cntT[:, :], PU[0:8, 40:48]), V)
        vdrain()
        wt("tensor", V, V.n)
        mm(PU[0:8, 48:56], cnt_sb[:, :], str8[:, :], True, True)
        mm(PU[0:8, 56:64], str8[:, :], cntT[:, :], True, True, P)
        wt("vector", P, P.n)
        ve(lambda e: e.tensor_copy(offsT[:, :], PU[0:8, 48:56]))
        ve(lambda e: e.tensor_copy(cstT[:, :], PU[0:8, 56:64]), V)
        vdrain()
        wt("tensor", V, V.n)
        mm(PU[0:1, 64:72], ecol[:, :], offsT[:, :], True, True)
        mm(PU[0:1, 72:80], ecol[:, :], cstT[:, :], True, True)
        mm(PU[0:1, 80:88], ecol[:, :], cntT[:, :], True, True, P)
        wt("vector", P, P.n)
        ve(lambda e: e.tensor_copy(exoffs[:, :], PU[0:1, 64:72]))
        ve(lambda e: e.tensor_copy(cst_row[:, :], PU[0:1, 72:80]))
        ve(lambda e: e.tensor_copy(excnt[:, :], PU[0:1, 80:88]), V)
        vdrain()
        ve(lambda e: e.tensor_tensor(hi_row[:, :], exoffs[:, :], excnt[:, :],
                                     ALU.add))
        ve(lambda e: e.tensor_tensor(base_row[:, :], rowbase[:, :], cst_row[:, :],
                                     ALU.add))
        vdrain()
        ve(lambda e: e.tensor_tensor(base_row[:, :], base_row[:, :], exoffs[:, :],
                                     ALU.subtract))
        vdrain()
        br = ve(lambda e: e.tensor_scalar_add(base_row[:, :], base_row[:, :],
                                              -BIG), V)
        wt("tensor", V, br)
        mm(PU[:, 88:96], ones1f[:, :], exoffs[0:1, :], True, True)
        mm(PU[:, 96:104], ones1f[:, :], hi_row[0:1, :], True, True)
        mm(PU[:, 104:112], ones1f[:, :], base_row[0:1, :], True, True, P)
        wt("vector", P, P.n)
        ve(lambda e: e.tensor_copy(lo_bc[:, :], PU[:, 88:96]))
        ve(lambda e: e.tensor_copy(hi_bc[:, :], PU[:, 96:104]))
        ve(lambda e: e.tensor_copy(base_bc[:, :], PU[:, 104:112]), V)
        vdrain()
        s3 = s160[:, :].rearrange("p (g e) -> p g e", e=8)
        ve(lambda e: e.tensor_tensor(
            m160[:, :].rearrange("p (g e) -> p g e", e=8), s3,
            lo_bc[:, :].unsqueeze(1).broadcast_to([128, 20, 8]), ALU.is_ge))
        ve(lambda e: e.tensor_tensor(
            v160[:, :].rearrange("p (g e) -> p g e", e=8), s3,
            hi_bc[:, :].unsqueeze(1).broadcast_to([128, 20, 8]), ALU.is_lt))
        vdrain()
        ve(lambda e: e.tensor_mul(m160[:, :], m160[:, :], v160[:, :]))
        ve(lambda e: e.tensor_tensor(
            v160[:, :].rearrange("p (g e) -> p g e", e=8), s3,
            base_bc[:, :].unsqueeze(1).broadcast_to([128, 20, 8]), ALU.add))
        vdrain()
        ve(lambda e: e.tensor_mul(v160[:, :], v160[:, :], m160[:, :]))
        vdrain()
        ve(lambda e: e.reduce_sum(srcg[:, :],
                                  v160[:, :].rearrange("p (g e) -> p g e", e=8),
                                  axis=AX.X))
        vdrain()
        ve(lambda e: e.tensor_scalar_add(srcg[:, :], srcg[:, :], BIG))
        vdrain()
        sgu = ve(lambda e: e.tensor_copy(srcgu[:, :], srcg[:, :]), V)
        vdrain()

        wt("gpsimd", CC, cc2)
        wt("gpsimd", V, sgu)
        wt("tensor", G, gw1)
        wt("tensor", G, gw2b)
        eo_q = [0, 0]
        dg_g = [0] * 22
        tr_p = [0] * 22
        ev_v = [0, 0]

        def dgat_fetch(tt):
            buf = tt % 2
            if tt >= 2:
                wt("vector", P, tr_p[tt - 2])
            ms = ve(lambda e, buf=buf: e.memset(dgat[:, buf, :], 0.0), V)
            wt("gpsimd", V, ms)
            dg_g[tt] = ctx.op("gpsimd", lambda e, buf=buf, tt=tt:
                              e.indirect_dma_start(
                                  out=dgat[:, buf, :], out_offset=None,
                                  in_=ag1_d[:, :],
                                  in_offset=IndirectOffsetOnAxis(
                                      srcgu[:, tt:tt + 1], 0),
                                  bounds_check=bcregs["bt"],
                                  oob_is_err=False), G, 16)

        dgat_fetch(0)
        dgat_fetch(1)
        for grp in range(10):
            for half in range(2):
                tt = grp * 2 + half
                buf = tt % 2
                wt("tensor", G, dg_g[tt])
                for sl in range(8):
                    tr(PT[:, sl * 128:(sl + 1) * 128],
                       dgat[:, buf, sl * 128:(sl + 1) * 128], id_bf[:, :],
                       P if sl == 7 else None)
                tr_p[tt] = P.n
                wt("vector", P, tr_p[tt])
                tv = ve(lambda e, half=half: e.tensor_copy(
                    dfm[:, :, :].rearrange("p a b -> p a b")[:, :, half * 128:
                                                            half * 128 + 128],
                    PT[:, :].rearrange("p (a b) -> p a b", a=8)), V)
                wt("tensor", V, tv)
                if tt + 2 < 20:
                    dgat_fetch(tt + 2)
            hv_evac = []
            for fp in range(16):
                pb = (PE_, PF)[fp % 2]
                if fp >= 2:
                    wt("tensor", S, hv_evac[fp - 2])
                for half2 in range(2):
                    fc = 2 * fp + half2
                    for dc in range(8):
                        mm(pb[:, half2 * 256:half2 * 256 + 256],
                           w1p[dc // 2][:, dc % 2, fc * 128:(fc + 1) * 128],
                           dfm[:, dc, :], dc == 0, dc == 7,
                           P if (dc == 7 and half2 == 1) else None)
                wt("scalar", P, P.n)
                hv_evac.append(se(lambda e, fp=fp, pb=pb: e.activation(
                    out=h1p[fp // 8][:, (2 * fp) % 16:(2 * fp) % 16 + 2, :]
                    .rearrange("p a b -> p (a b)"),
                    in_=pb[:, :], func=AF.Relu), S))
            h1_s = hv_evac[-1]
            wt("tensor", S, h1_s)
            for half in range(2):
                wt("tensor", V, ev_v[half])
                bks = ((PC, PD), (PA, PB))[half]
                for nt in range(2):
                    pb = bks[nt]
                    for fc in range(32):
                        mm(pb[:, :],
                           h1p[fc // 16][:, fc % 16, half * 128:half * 128 + 128],
                           w2p[fc // 8][:, fc % 8, nt * 512:(nt + 1) * 512],
                           fc == 0, fc == 31,
                           P if (fc == 31 and nt == 1) else None)
                eo_p = P.n
                wt("vector", P, eo_p)
                wt("vector", L, eo_q[half])
                ve(lambda e, half=half, bks=bks: e.tensor_copy(
                    eo_sb[:, half, 0:512], bks[0][:, :]))
                ev_v[half] = ve(lambda e, half=half, bks=bks: e.tensor_copy(
                    eo_sb[:, half, 512:1024], bks[1][:, :]), V)
                wt("sync", V, ev_v[half])
                row0 = grp * 256 + half * 128
                eo_q[half] = sdma(eo_sb[:, half, :], eout_d[row0:row0 + 128, :])
        wt("gpsimd", L, max(eo_q))
        cc3 = ctx.op("gpsimd", lambda e: e.collective_compute(
            "AllGather", ALU.bypass, replica_groups=[list(range(8))],
            ins=[eout_d[:, :].opt()], outs=[ag2_d[:, :].opt()]), CC)

        comb2 = nc.alloc_sbuf_tensor_at("comb2", [128, D], BF16,
                                        offset=nc.lookup_mloc(w1p[1]).addr)
        y2st2 = nc.alloc_sbuf_tensor_at("y2st2", [128, D], F32,
                                        offset=nc.lookup_mloc(dfm).addr)
        y1c2 = nc.alloc_sbuf_tensor_at("y1c2", [128, D], F32,
                                       offset=nc.lookup_mloc(w1p[1]).addr + 4096)
        preb = nc.alloc_sbuf_tensor_at("preb", [128, D], F32, offset=_xaddr)
        lnsc = nc.alloc_sbuf_tensor_at("lnsc", [128, D], F32,
                                       offset=nc.lookup_mloc(h1p[0]).addr)
        cbufs = (comb, comb2)
        ycbufs = (y1_sb, y1c2)
        prebufs = (pre_sb, preb)
        statn = ((mraw, vraw, mean1, var1, rstd1, tmp1c),
                 (mraw2, vraw2, mean1b, var1b, rstd1b, tmp1cb))
        wt("gpsimd", CC, cc3)
        out_q = {}
        cgs = {}
        ylc = {}
        pd_t = {}
        mn_t = {}
        v2_t = {}
        rs_t = {}

        def c_prefetch(tile):
            p = tile % 2
            msv = ve(lambda e, p=p: e.memset(cbufs[p][:, :], 0.0), V)
            wt("gpsimd", V, msv)
            if tile >= 2:
                wt("gpsimd", G, cgs[tile - 2])
            cgs[tile] = ctx.op(
                "gpsimd", lambda e, p=p, tile=tile: e.indirect_dma_start(
                    out=cbufs[p][:, :], out_offset=None, in_=ag2_d[:, :],
                    in_offset=IndirectOffsetOnAxis(sru[:, tile:tile + 1], 0),
                    bounds_check=bcregs["ec"], oob_is_err=False), G, 16)
            wt(("sync", "scalar")[p], V, pd_t.get(tile - 2, 0))
            ylc[tile] = (sdma, qdma)[p](y1_dram[tile * 128:(tile + 1) * 128, :],
                                        ycbufs[p][:, :])

        def c_stage1(tile):
            p = tile % 2
            wt("vector", G, cgs[tile])
            wt("vector", (L, SQ)[p], ylc[tile])
            ve(lambda e, p=p, tile=tile: e.tensor_scalar(
                prebufs[p][:, :], ycbufs[p][:, :], f2_all[:, tile:tile + 1],
                None, ALU.mult))
            vdrain()
            pd_t[tile] = ve(lambda e, p=p, tile=tile: e.scalar_tensor_tensor(
                prebufs[p][:, :], cbufs[p][:, :], pmax_all[:, tile:tile + 1],
                prebufs[p][:, :], ALU.mult, ALU.add), V)
            if tile + 2 < NT:
                c_prefetch(tile + 2)

        def c_stage2(tile):
            p = tile % 2
            mr, vr, mn, vv, rs, tc = statn[p]
            wt("scalar", V, pd_t[tile])
            se(lambda e, p=p, mr=mr: e.activation(
                out=lnsc[:, :], in_=prebufs[p][:, :], func=AF.Identity,
                accum_out=mr[:, :]))
            se(lambda e, p=p, vr=vr: e.activation(
                out=lnsc[:, :], in_=prebufs[p][:, :], func=AF.Square,
                accum_out=vr[:, :]))
            sdrain()
            se(lambda e, mr=mr, tc=tc: e.activation(
                out=tc[:, :], in_=mr[:, :], func=AF.Square, scale=1.0 / D))
            mn_t[tile] = se(lambda e, mr=mr, mn=mn: e.activation(
                out=mn[:, :], in_=mr[:, :], func=AF.Identity, scale=1.0 / D), S)

        def c_finish(tile):
            p = tile % 2
            mr, vr, mn, vv, rs, tc = statn[p]
            wt("vector", S, mn_t[tile])
            v2_t[tile] = ve(lambda e, vr=vr, vv=vv, tc=tc: e.tensor_scalar(
                vv[:, :], vr[:, :], 1.0 / D, tc[:, :], ALU.mult,
                ALU.subtract), V)
            wt("scalar", V, v2_t[tile])
            se(lambda e, vv=vv, rs=rs: e.activation(out=rs[:, :], in_=vv[:, :],
                                                    func=AF.Ln, bias=EPS))
            sdrain()
            rs_t[tile] = se(lambda e, rs=rs: e.activation(
                out=rs[:, :], in_=rs[:, :], func=AF.Exp, scale=-0.5), S)
            wt("vector", S, rs_t[tile])
            if tile >= 2:
                wt("vector", SQ, out_q[tile - 2])
            y2buf = (eo32[:, :], y2st2[:, :])[p]
            y2v = ve(lambda e, y2buf=y2buf, p=p, mn=mn, rs=rs: e.tensor_scalar(
                y2buf, prebufs[p][:, :], mn[:, :], rs[:, :],
                ALU.subtract, ALU.mult), V)
            wt("scalar", V, y2v)
            out_q[tile] = qdma(y2buf, y2_out[tile * 128:(tile + 1) * 128, :])

        for tile in range(min(2, NT)):
            c_prefetch(tile)
        for tile in range(NT):
            c_stage1(tile)
            c_stage2(tile)
            if tile > 0:
                c_finish(tile - 1)
        c_finish(NT - 1)
        wt("gpsimd", SQ, out_q[NT - 1])
    else:
        wt("gpsimd", SQ, SQ.n)
        wt("gpsimd", V, V.n)
        for tile in range(NT):
            gdma(y1_dram[tile * 128:(tile + 1) * 128, :],
                 y2_out[tile * 128:(tile + 1) * 128, :])
        wt("gpsimd", G, G.n)

    with nc.Block() as block:
        @block.sync
        def _(eng):
            for fn in ctx.streams["sync"]:
                fn(eng)

        @block.gpsimd
        def _(eng):
            for fn in ctx.streams["gpsimd"]:
                fn(eng)

        @block.tensor
        def _(eng):
            for fn in ctx.streams["tensor"]:
                fn(eng)

        @block.vector
        def _(eng):
            for fn in ctx.streams["vector"]:
                fn(eng)

        @block.scalar
        def _(eng):
            for fn in ctx.streams["scalar"]:
                fn(eng)

    return nc


def _consts():
    ident = np.eye(128)
    jj, tt = np.mgrid[0:128, 0:128]
    utri = (jj <= tt).astype(np.float32)
    ustrict = (jj < tt).astype(np.float32)
    iota8 = np.tile(np.arange(8, dtype=np.float32), (128, 1))
    s160 = np.zeros((128, 160), np.float32)
    for g in range(20):
        for e in range(8):
            s160[:, g * 8 + e] = g * 128 + np.arange(128)
    ee, e2 = np.mgrid[0:8, 0:8]
    str8 = (ee < e2).astype(np.float32)
    return ident, utri, ustrict, iota8, s160, str8


_BUILD_KW = {}


def prep_in_maps(inputs):
    x = np.asarray(inputs["L0_attn_hidden_states"], np.float32)
    nw = np.asarray(inputs["L0_attn_norm_weight"], np.float32)
    gn = np.asarray(inputs["L0_attn_gnorm_weight"], np.float32)
    wq = np.asarray(inputs["L0_attn_w_q"], np.float32)
    wk = np.asarray(inputs["L0_attn_w_k"], np.float32)
    wv = np.asarray(inputs["L0_attn_w_v"], np.float32)
    wf = np.asarray(inputs["L0_attn_w_f"], np.float32)
    wo = np.asarray(inputs["L0_attn_w_o"], np.float32)
    Wr = np.asarray(inputs["L0_ffn_Wr"], np.float32)
    W1 = np.asarray(inputs["L0_ffn_W1"], np.float32)
    W2 = np.asarray(inputs["L0_ffn_W2"], np.float32)

    ident, utri, ustrict, iota8, s160, str8 = _consts()
    shared = dict(
        wqT=np.ascontiguousarray(wq.T), wkT=np.ascontiguousarray(wk.T),
        wvT=np.ascontiguousarray(wv.T), wfT=np.ascontiguousarray(wf.T),
        woT=np.ascontiguousarray(wo.T),
        nw_c=np.ascontiguousarray(nw.reshape(8, 128).T),
        gn_c=np.ascontiguousarray(gn.reshape(8, 128).T),
        wr=Wr,
        ident_bf=ident.astype(ml_dtypes.bfloat16),
        ident_f32=ident.astype(np.float32),
        utri_bf=utri.astype(ml_dtypes.bfloat16), utri_f=utri,
        ustrict_f32=ustrict,
        iota8=iota8, s160=s160, str8=str8,
        rowbase=(np.arange(8, dtype=np.float32) * T)[None, :],
        ones1f=np.ones((1, 128), np.float32),
        onescol_bf=np.ones((128, 1), ml_dtypes.bfloat16),
        onescol_f=np.ones((128, 1), np.float32),
        invd_bf=np.full((128, 1), 1.0 / 1024.0, ml_dtypes.bfloat16),
        epscol=np.full((128, 1), 1e-5, np.float32),
    )
    in_maps = []
    for b in range(8):
        m = dict(shared)
        m["x_fm"] = np.ascontiguousarray(x[b].T)
        m["x_tm"] = np.ascontiguousarray(x[b])
        m["w1e"] = np.ascontiguousarray(W1[b])
        m["w2e"] = np.ascontiguousarray(W2[b])
        m["bmask"] = (np.arange(8) < b).astype(np.float32)[:, None]
        m["ecol"] = (np.arange(8) == b).astype(np.float32)[:, None]
        in_maps.append(m)
    return in_maps


def kernel(**inputs):
    in_maps = prep_in_maps(inputs)
    nc = build_nc(**_BUILD_KW)
    res = run_bass_kernel_spmd(nc, in_maps, core_ids=list(range(8)))
    out = np.stack([r["y2"] for r in res.results], axis=0)
    return out
